# version 21
# baseline (speedup 1.0000x reference)
import numpy as np
import concourse.bass as bass
import concourse.mybir as mybir
from concourse.bass_utils import run_bass_kernel_spmd

F32 = mybir.dt.float32
BF16 = mybir.dt.bfloat16
I32 = mybir.dt.int32
AL = mybir.AluOpType
AF = mybir.ActivationFunctionType
AX = mybir.AxisListType

D = 2048
NTOK = 2048
NT = 16
NQ = 1024
QT = 8
NH = 16
HD = 128
HF = 8192
HC = 256
NCH = HF // HC
PI = float(np.pi)
TWO_PI = 2.0 * np.pi

ENGINES = ("sync", "tensor", "vector", "scalar", "gpsimd")


class Em:

    def __init__(self, nc):
        self.nc = nc
        self.q = {e: [] for e in ENGINES}
        self.semh = {}
        self.cnt = {}
        self.waited = {e: {} for e in ENGINES}
        self.w_ev = {}
        self.r_ev = {}

    def sem(self, name):
        self.semh[name] = self.nc.alloc_semaphore(name=name)
        self.cnt[name] = 0

    def _emit_op(self, eng, meth, a, k, inc, n):
        if inc is not None:
            self.cnt[inc] += n
        iv = (self.semh[inc], n) if inc is not None else None

        def fn(h, _a=a, _k=k, _m=meth, _iv=iv):
            r = getattr(h, _m)(*_a, **_k)
            if _iv is not None:
                r.then_inc(_iv[0], _iv[1])
        self.q[eng].append(fn)
        return (inc, self.cnt[inc]) if inc is not None else None

    def raw(self, eng, fn):
        self.q[eng].append(fn)

    def wait(self, eng, sem, val):
        if val is None or val <= 0:
            return
        if self.waited[eng].get(sem, -1) >= val:
            return
        self.waited[eng][sem] = val
        h = self.semh[sem]
        self.q[eng].append(lambda e, _h=h, _v=val: e.wait_ge(_h, _v))

    def pre(self, eng, rd=(), wr=()):
        for res in rd:
            ev = self.w_ev.get(res)
            if ev:
                self.wait(eng, ev[0], ev[1])
        for res in wr:
            ev = self.w_ev.get(res)
            if ev:
                self.wait(eng, ev[0], ev[1])
            for s, v in self.r_ev.get(res, {}).items():
                self.wait(eng, s, v)

    def post(self, ev, rd=(), wr=()):
        assert ev is not None, "tracked op needs an inc semaphore"
        for res in wr:
            self.w_ev[res] = ev
            self.r_ev[res] = {}
        for res in rd:
            d = self.r_ev.setdefault(res, {})
            if d.get(ev[0], -1) < ev[1]:
                d[ev[0]] = ev[1]

    def op(self, eng, meth, *a, rd=(), wr=(), inc=None, n=1, **k):
        self.pre(eng, rd, wr)
        ev = self._emit_op(eng, meth, a, k, inc, n)
        if ev is not None:
            self.post(ev, rd, wr)
        return ev

    def barrier(self, esem):
        for eng, s in esem.items():
            self.cnt[s] += 1
            h = self.semh[s]
            def closer(e, _h=h):
                e.drain()
                e.nop().then_inc(_h, 1)
            self.q[eng].append(closer)
        vals = dict(self.cnt)
        for eng in ENGINES:
            for s, v in vals.items():
                self.wait(eng, s, v)


def ap2(t, off, prow, npart, nfree, fstep=1):
    return bass.AP(t, off, [[prow, npart], [fstep, nfree]])


def build():
    nc = bass.Bass("TRN2", target_bir_lowering=False, debug=False)

    dX = nc.dram_tensor("xT", [D, NTOK], F32, kind="ExternalInput")
    dXQ = nc.dram_tensor("xqT", [D, NQ], F32, kind="ExternalInput")
    dXR = nc.dram_tensor("xr", [NQ, D], F32, kind="ExternalInput")
    dWQ = nc.dram_tensor("wqT", [D, D], F32, kind="ExternalInput")
    dWK = nc.dram_tensor("wkT", [D, D], F32, kind="ExternalInput")
    dWV = nc.dram_tensor("wvT", [D, D], F32, kind="ExternalInput")
    dWO = nc.dram_tensor("woT", [D, D], F32, kind="ExternalInput")
    dWG = nc.dram_tensor("wgT", [D, HF], F32, kind="ExternalInput")
    dWU = nc.dram_tensor("wuT", [D, HF], F32, kind="ExternalInput")
    dWD = nc.dram_tensor("wdT", [HF, D], F32, kind="ExternalInput")
    dDYN = nc.dram_tensor("dynv_d", [1, 8], I32, kind="ExternalInput")
    dSEL = nc.dram_tensor("selv", [1, 2], F32, kind="ExternalInput")
    dPRM = nc.dram_tensor("perm", [NT, QT], F32, kind="ExternalInput")
    dY = nc.dram_tensor("y", [NQ, D], F32, kind="ExternalOutput")
    dOS = nc.dram_tensor("o_scr", [NH * HD, NQ], F32)
    dX2 = nc.dram_tensor("x2_scr", [NQ, D], F32)

    em = Em(nc)
    globals()["_EM"] = em
    for s in ("sT", "sV", "sA", "sG", "dDy", "dSe", "dPm", "dXA", "dXB",
              "dWk", "dWv", "dWq", "dQs", "dQg", "dQgB", "dNr", "dNq", "dOS", "dWo",
              "dO0", "dO1", "dXr", "dX2s", "dX2f", "dX2fB", "dX2r", "dY", "dRcK", "dRcQ",
              "dPr0", "dPr1", "dPs0", "dPs1", "dWg0", "dWg1", "dWu0", "dWu1",
              "dWd0", "dWd1", "dDu1", "dDu2", "dKtA", "dKtB", "dWoB", "dWoC", "dWoD"):
        em.sem(s)
    C = em.cnt

    ESEM = {"tensor": "sT", "vector": "sV", "scalar": "sA", "gpsimd": "sG"}
    import contextlib
    stack = contextlib.ExitStack()

    def sb(name, shape, dtype=F32, scope=None):
        if scope is None:
            return nc.alloc_sbuf_tensor(name, shape, dtype)
        return scope.enter_context(nc.sbuf_tensor(name, shape, dtype))

    ident = sb("ident", [128, 128])
    identB = sb("identB", [128, 128], BF16)
    ones = sb("ones", [128, 128])
    dynv = sb("dynv", [1, 8], I32)
    selt = sb("selt", [128, 2])
    zrow = sb("zrow", [1, 16])
    constv = sb("constv", [128, 4])

    pb = [nc.alloc_psum_tensor("pb%d" % i, [128, 512], F32) for i in range(8)]
    PB = ["pb%d" % i for i in range(8)]

    rg_v = {k: nc.vector.alloc_register("rv_" + k) for k in ("qe0", "qe1")}

    sc1 = contextlib.ExitStack()
    permm = sb("permm", [NT, QT], scope=sc1)
    selrow = sb("selrow", [1, 2], scope=sc1)
    negm16 = sb("negm16", [NT, 128], scope=sc1)
    negmQ = sb("negmQ", [QT, 128], scope=sc1)
    rstdc = sb("rstdc", [128, NT], scope=sc1)
    csrC = sb("csrC", [128, NTOK], scope=sc1)
    csrS = sb("csrS", [128, NTOK], scope=sc1)
    csrQC = sb("csrQC", [128, NQ], scope=sc1)
    csrQS = sb("csrQS", [128, NQ], scope=sc1)
    crow = sb("crow", [1, 768], scope=sc1)
    xqcA = sb("xqcA", [128, 512], scope=sc1)
    xqcB = sb("xqcB", [128, 512], scope=sc1)
    xq_bufs = [xqcA, xqcB]
    wk_sl = sb("wk_sl", [128, NT * 256], scope=sc1)
    wv_sl = sb("wv_sl", [128, NT * 256], scope=sc1)
    wq_sl = sb("wq_sl", [128, NT * 256], scope=sc1)
    kropU = sb("kropU", [128, NTOK], scope=sc1)
    kropL = sb("kropL", [128, NTOK], scope=sc1)
    qropU = sb("qropU", [128, NQ], scope=sc1)
    qropL = sb("qropL", [128, NQ], scope=sc1)
    vrowsA = sb("vrowsA", [128, NT * 128], scope=sc1)
    vrowsB = sb("vrowsB", [128, NT * 128], scope=sc1)
    kT = sb("kTb", [128, NTOK], scope=sc1)
    qTt = sb("qTb", [128, NQ], scope=sc1)
    qquadA = sb("qquadA", [128, 4 * 512], scope=sc1)
    qquadB = sb("qquadB", [128, 4 * 512], scope=sc1)
    mtil = sb("mtil", [128, 8 * 512], scope=sc1)
    s2b = sb("s2b", [128, 1024], scope=sc1)
    s4b = sb("s4b", [128, 512], scope=sc1)
    tac = sb("tac", [128, 2 * 512], scope=sc1)
    obuf = sb("obuf", [128, 1024], scope=sc1)
    drow = sb("drow", [1, 1024], scope=sc1)
    negmrow = sb("negmrow", [1, 2048], scope=sc1)
    negmqrow = sb("negmqrow", [1, 1024], scope=sc1)
    bcst = sb("bcst", [128, 512], scope=sc1)
    sc1b = contextlib.ExitStack()
    xsq = sb("xsq", [128, 2048], scope=sc1b)
    rowst16 = sb("rowst16", [16, 128], scope=sc1b)
    statA = sb("statA", [128, NT], scope=sc1b)
    statB = sb("statB", [128, NT], scope=sc1b)
    statC = sb("statC", [128, NT], scope=sc1b)

    em.op("sync", "dma_start", out=dynv[:], in_=dDYN.ap()[0:1, :],
          wr=["dynv"], inc="dDy", n=16)
    em.op("sync", "dma_start", out=selrow[:], in_=dSEL.ap()[0:1, :],
          wr=["selrow"], inc="dSe", n=16)
    em.op("sync", "dma_start", out=permm[:], in_=dPRM.ap()[:, :],
          wr=["permm"], inc="dPm", n=16)

    em.op("gpsimd", "memset", ident[:], 0.0, wr=["identz"], inc="sG")
    em.op("gpsimd", "memset", ones[:], 1.0, wr=["ones"], inc="sG")
    em.op("gpsimd", "memset", zrow[:], 0.0, wr=["zrow"], inc="sG")
    _consts = [float(-np.log(10000.0) / 64.0), -1.0, PI / 2.0, 63.5]
    for ci, cv in enumerate(_consts):
        em.op("gpsimd", "memset", constv[:, ci:ci + 1], cv,
              wr=["constv"], inc="sG")
        nc.const_aps.register(F32, cv, constv[:, ci:ci + 1])
    em.op("gpsimd", "affine_select", out=ident[:], in_=ident[:],
          compare_op=AL.not_equal, fill=1.0, base=0,
          pattern=[[-1, 128]], channel_multiplier=1,
          rd=["identz"], wr=["ident"], inc="sG")
    em.op("scalar", "copy", identB[:], ident[:],
          rd=["ident"], wr=["identB"], inc="sA")

    em.pre("vector", rd=["dynv"])
    em.raw("vector", lambda e: e.load(rg_v["qe0"], dynv[0:1, 0:1]))
    em.raw("vector", lambda e: e.load(rg_v["qe1"], dynv[0:1, 1:2]))

    em.op("tensor", "matmul", pb[7][:, 0:2], ones[0:1, 0:128], selrow[0:1, :],
          start=True, stop=True, rd=["ones", "selrow"], wr=[PB[7]], inc="sT")
    em.op("vector", "tensor_copy", selt[:], pb[7][:, 0:2],
          rd=[PB[7]], wr=["selt"], inc="sV")
    for fam in range(2):
        for j in range(4):
            sl = mtil[:, fam * 2048 + j * 512:fam * 2048 + j * 512 + 512]
            em.op("gpsimd", "memset", sl, 1.0, wr=["mtil"], inc="sG")
            em.op("gpsimd", "affine_select", out=sl, in_=sl,
                  compare_op=AL.is_ge, fill=0.0, base=-128 * j,
                  pattern=[[1, 512]], channel_multiplier=-1,
                  wr=["mtil"], inc="sG")
            em.op("vector", "tensor_scalar", sl, sl,
                  selt[:, fam:fam + 1], selt[:, 1 - fam:2 - fam],
                  AL.mult, AL.add, rd=["selt"], wr=["mtil"], inc="sV")

    def emit_wslice_load(g):
        for (dst, dn, dram, dsm) in ((wv_sl, "wv_sl", dWV, "dWv"),
                                     (wk_sl, "wk_sl", dWK, "dWk"),
                                     (wq_sl, "wq_sl", dWQ, "dWq")):
            src = bass.AP(dram, g * 256,
                          [[NTOK, 128], [128 * NTOK, 16], [1, 256]])
            dap = bass.AP(dst, 0, [[NT * 256, 128], [256, 16], [1, 256]])
            em.op("sync", "dma_start", out=dap, in_=src, wr=[dn], inc=dsm,
                  n=16)

    em.op("tensor", "matmul", pb[0][:, 0:16], ones[0:1, 0:128], zrow[0:1, 0:16],
          start=True, stop=False, rd=["ones", "zrow"], wr=[PB[0]], inc="sT")
    em.op("tensor", "matmul", pb[1][:, 0:16], ones[0:1, 0:128], zrow[0:1, 0:16],
          start=True, stop=False, rd=["ones", "zrow"], wr=[PB[1]], inc="sT")
    pidx = 0
    for dchunk in range(16):
        for piece in range(4):
            par = pidx % 2
            buf = xq_bufs[par][:, :]
            bn = "xqc%d" % par
            sqb = xsq[:, par * 512:par * 512 + 512]
            sqn = "xsq%d" % par
            em.op("sync", "dma_start", out=buf,
                  in_=ap2(dX, dchunk * 128 * NTOK + piece * 512, NTOK, 128, 512),
                  wr=[bn], inc=("dXA" if par == 0 else "dXB"), n=16)
            em.op("scalar", "activation", sqb, buf, AF.Square,
                  rd=[bn], wr=[sqn], inc="sA")
            tg = piece * 4
            em.pre("tensor", rd=[bn, sqn], wr=[PB[0], PB[1]])
            last = (dchunk == 15 and piece == 3)
            for tt in range(4):
                em._emit_op("tensor", "matmul",
                            (pb[0][:, tg + tt:tg + tt + 1],
                             buf[:, tt * 128:tt * 128 + 128], ones[:, 0:1]),
                            dict(start=False, stop=(last and tt == 3)), None, 1)
                ev = em._emit_op("tensor", "matmul",
                                 (pb[1][:, tg + tt:tg + tt + 1],
                                  sqb[:, tt * 128:tt * 128 + 128], ones[:, 0:1]),
                                 dict(start=False, stop=(last and tt == 3)),
                                 "sT", 1)
            em.post(ev, rd=[bn, sqn], wr=[PB[0], PB[1]])
            pidx += 1

    emit_wslice_load(0)

    em.op("vector", "tensor_scalar", statA[:, :], pb[0][:, 0:16],
          -1.0 / D, None, AL.mult, rd=[PB[0]], wr=["statA"], inc="sV")
    em.op("vector", "tensor_scalar", statB[:, :], pb[1][:, 0:16],
          1.0 / D, None, AL.mult, rd=[PB[1]], wr=["statB"], inc="sV")
    em.op("vector", "tensor_tensor", statC[:, :], statA[:, :], statA[:, :],
          AL.mult, rd=["statA"], wr=["statC"], inc="sV")
    em.op("vector", "tensor_tensor", statB[:, :], statB[:, :], statC[:, :],
          AL.subtract, rd=["statC"], wr=["statB"], inc="sV")
    em.op("vector", "tensor_scalar", statB[:, :], statB[:, :],
          1e-5, None, AL.add, wr=["statB"], inc="sV")
    em.op("scalar", "activation", statC[:, :], statB[:, :], AF.Sqrt,
          rd=["statB"], wr=["statC"], inc="sA")
    em.op("vector", "reciprocal", rstdc[:, :], statC[:, :],
          rd=["statC"], wr=["rstdc"], inc="sV")
    for _ in range(2):
        em.op("vector", "tensor_tensor", statC[:, :], rstdc[:, :], rstdc[:, :],
              AL.mult, rd=["rstdc"], wr=["statC"], inc="sV")
        em.op("vector", "tensor_tensor", statC[:, :], statC[:, :], statB[:, :],
              AL.mult, rd=["statB"], wr=["statC"], inc="sV")
        em.op("vector", "tensor_scalar", statC[:, :], statC[:, :],
              -0.5, 1.5, AL.mult, AL.add, wr=["statC"], inc="sV")
        em.op("vector", "tensor_tensor", rstdc[:, :], rstdc[:, :], statC[:, :],
              AL.mult, rd=["statC"], wr=["rstdc"], inc="sV")

    em.op("tensor", "transpose", pb[2][0:16, 0:128], statA[:, :], ident[:],
          rd=["statA", "ident"], wr=[PB[2]], inc="sT")
    em.op("vector", "tensor_copy", negm16[:, :], pb[2][0:16, 0:128],
          rd=[PB[2]], wr=["negm16"], inc="sV")
    em.op("tensor", "matmul", pb[2][0:QT, 0:128], permm[:, :], negm16[:, :],
          start=True, stop=True, rd=["permm", "negm16"], wr=[PB[2]], inc="sT")
    em.op("vector", "tensor_copy", negmQ[:, :], pb[2][0:QT, 0:128],
          rd=[PB[2]], wr=["negmQ"], inc="sV")
    em.op("gpsimd", "dma_start",
          out=bass.AP(negmrow, 0, [[2048, 1], [1, 2048]]),
          in_=bass.AP(negm16, 0, [[128, 16], [1, 128]]),
          rd=["negm16"], wr=["negmrow"], inc="dNr", n=16)
    em.op("gpsimd", "dma_start",
          out=bass.AP(negmqrow, 0, [[1024, 1], [1, 1024]]),
          in_=bass.AP(negmQ, 0, [[128, 8], [1, 128]]),
          rd=["negmQ"], wr=["negmqrow"], inc="dNq", n=16)

    em.op("gpsimd", "iota", statC[0:64, 0:1], pattern=[[0, 1]], base=0,
          channel_multiplier=1, allow_small_or_imprecise_dtypes=True,
          wr=["statC"], inc="sG")
    em.op("gpsimd", "iota", statC[64:128, 0:1], pattern=[[0, 1]], base=0,
          channel_multiplier=1, allow_small_or_imprecise_dtypes=True,
          wr=["statC"], inc="sG")
    em.op("scalar", "activation", statC[:, 1:2], statC[:, 0:1], AF.Exp,
          scale=float(-np.log(10000.0) / 64.0),
          rd=["statC", "constv"], wr=["theta"], inc="sA")
    em.op("gpsimd", "iota", xsq[:, 0:2048], pattern=[[1, 2048]], base=0,
          channel_multiplier=0, allow_small_or_imprecise_dtypes=True,
          wr=["xsq0", "xsq1"], inc="sG")
    ang = csrS
    em.op("vector", "tensor_scalar", ang[0:64, 0:2048], xsq[0:64, 0:2048],
          statC[0:64, 1:2], None, AL.mult,
          rd=["xsq0", "xsq1", "theta", "statC"], wr=["csrS"], inc="sV")
    bb = xsq[0:64, 0:2048]
    for kk in range(8, -1, -1):
        m = TWO_PI * (2.0 ** kk)
        m_hi = float(np.float32(m))
        m_lo = float(m - np.float64(np.float32(m)))
        em.op("vector", "tensor_single_scalar", bb, ang[0:64, 0:2048], m_hi,
              AL.is_ge, rd=["csrS"], wr=["xsq0", "xsq1"], inc="sV")
        em.op("vector", "scalar_tensor_tensor", ang[0:64, 0:2048], bb, -m_hi,
              ang[0:64, 0:2048], AL.mult, AL.add,
              rd=["xsq0", "xsq1"], wr=["csrS"], inc="sV")
        em.op("vector", "scalar_tensor_tensor", ang[0:64, 0:2048], bb, -m_lo,
              ang[0:64, 0:2048], AL.mult, AL.add,
              rd=["xsq0", "xsq1"], wr=["csrS"], inc="sV")
    tp_hi = float(np.float32(TWO_PI))
    tp_lo = float(TWO_PI - np.float64(np.float32(TWO_PI)))
    em.op("vector", "tensor_single_scalar", bb, ang[0:64, 0:2048], PI, AL.is_ge,
          rd=["csrS"], wr=["xsq0", "xsq1"], inc="sV")
    em.op("vector", "scalar_tensor_tensor", ang[0:64, 0:2048], bb, -tp_hi,
          ang[0:64, 0:2048], AL.mult, AL.add,
          rd=["xsq0", "xsq1"], wr=["csrS"], inc="sV")
    em.op("vector", "scalar_tensor_tensor", ang[0:64, 0:2048], bb, -tp_lo,
          ang[0:64, 0:2048], AL.mult, AL.add,
          rd=["xsq0", "xsq1"], wr=["csrS"], inc="sV")
    em.op("scalar", "activation", xsq[0:64, 0:2048], ang[0:64, 0:2048], AF.Abs,
          rd=["csrS"], wr=["xsq0", "xsq1"], inc="sA")
    em.op("scalar", "activation", csrC[0:64, :], xsq[0:64, 0:2048], AF.Sin,
          scale=-1.0, bias=PI / 2.0,
          rd=["xsq0", "xsq1", "constv"], wr=["csrC"], inc="sA")
    em.op("scalar", "activation", ang[0:64, 0:2048], ang[0:64, 0:2048],
          AF.Sin, wr=["csrS"], inc="sA")
    em.op("gpsimd", "dma_start", out=csrC[64:128, :], in_=csrC[0:64, :],
          rd=["csrC"], wr=["csrC"], inc="dDu1", n=16)
    em.op("gpsimd", "dma_start", out=csrS[64:128, :], in_=csrS[0:64, :],
          rd=["csrS"], wr=["csrS"], inc="dDu2", n=16)
    em.op("tensor", "transpose", pb[3][0:16, 0:128], rstdc[:, :], ident[:],
          rd=["rstdc", "ident"], wr=[PB[3]], inc="sT")
    em.op("vector", "tensor_copy", rowst16[:, :], pb[3][0:16, 0:128],
          rd=[PB[3]], wr=["rowst16"], inc="sV")
    for hhh in range(2):
        em.op("gpsimd", "dma_start",
              out=bass.AP(drow, 0, [[1024, 1], [1, 1024]]),
              in_=bass.AP(rowst16, hhh * 8 * 128, [[128, 8], [1, 128]]),
              rd=["rowst16"], wr=["drow"], inc="dNq", n=16)
        for cc in range(2):
            em.op("tensor", "matmul", pb[3][:, 0:512], ones[0:1, 0:128],
                  drow[0:1, cc * 512:cc * 512 + 512], start=True, stop=True,
                  rd=["drow", "ones"], wr=[PB[3]], inc="sT")
            em.op("vector", "tensor_copy",
                  xsq[:, (hhh * 2 + cc) * 512:(hhh * 2 + cc) * 512 + 512],
                  pb[3][:, 0:512], rd=[PB[3]], wr=["xsq0", "xsq1"], inc="sV")
    em.op("vector", "tensor_tensor", csrC[:, :], csrC[:, :], xsq[:, 0:2048],
          AL.mult, rd=["xsq0", "xsq1"], wr=["csrC"], inc="sV")
    em.op("vector", "tensor_tensor", csrS[:, :], csrS[:, :], xsq[:, 0:2048],
          AL.mult, rd=["xsq0", "xsq1"], wr=["csrS"], inc="sV")
    em.op("vector", "tensor_copy", csrQC[:, 0:512],
          ap2(csrC, rg_v["qe0"], NTOK, 128, 512),
          rd=["csrC"], wr=["csrQC"], inc="sV")
    em.op("vector", "tensor_copy", csrQC[:, 512:1024],
          ap2(csrC, rg_v["qe1"], NTOK, 128, 512),
          rd=["csrC"], wr=["csrQC"], inc="sV")
    em.op("vector", "tensor_copy", csrQS[:, 0:512],
          ap2(csrS, rg_v["qe0"], NTOK, 128, 512),
          rd=["csrS"], wr=["csrQS"], inc="sV")
    em.op("vector", "tensor_copy", csrQS[:, 512:1024],
          ap2(csrS, rg_v["qe1"], NTOK, 128, 512),
          rd=["csrS"], wr=["csrQS"], inc="sV")

    def emit_colsums(g):
        for wi, (wsl, wn) in enumerate(((wk_sl, "wk_sl"), (wv_sl, "wv_sl"),
                                        (wq_sl, "wq_sl"))):
            em.pre("tensor", rd=[wn, "ones"], wr=[PB[6]])
            for dc in range(16):
                ev = em._emit_op("tensor", "matmul",
                                 (pb[6][0:1, 0:256], ones[:, 0:1],
                                  wsl[:, dc * 256:dc * 256 + 256]),
                                 dict(start=(dc == 0), stop=(dc == 15)),
                                 "sT" if dc == 15 else None, 1)
            em.post(ev, rd=[wn, "ones"], wr=[PB[6]])
            em.op("vector", "tensor_copy", crow[0:1, wi * 256:wi * 256 + 256],
                  pb[6][0:1, 0:256], rd=[PB[6]], wr=["crow"], inc="sV")

    def rope_emit(rawbuf, rawn, outbuf, outn, ntile, cstab, csn,
                  raw_base=0, out_base=0):
        rstep = rawbuf.shape[1]
        ostep = outbuf.shape[1]

        def A(boff):
            return bass.AP(rawbuf, raw_base + boff,
                           [[rstep, 128], [256, ntile], [1, 64]])

        def O(boff):
            return bass.AP(outbuf, out_base + boff,
                           [[ostep, 128], [256, ntile], [1, 64]])

        def CS(off):
            return bass.AP(cstab, off, [[cstab.shape[1], 128], [128, ntile], [1, 64]])
        for hh in range(2):
            b = hh * 128
            em.op("vector", "tensor_tensor", O(b), A(b), CS(0), AL.mult,
                  rd=[rawn, csn], wr=[outn], inc="sV")
            em.op("vector", "tensor_tensor", O(b + 64), A(b), CS(64), AL.mult,
                  rd=[rawn, csn], wr=[outn], inc="sV")
            em.op("vector", "tensor_tensor", A(b), A(b + 64), CS(64), AL.mult,
                  rd=[csn], wr=[rawn], inc="sV")
            em.op("vector", "tensor_tensor", O(b), O(b), A(b), AL.subtract,
                  rd=[rawn], wr=[outn], inc="sV")
            em.op("vector", "tensor_tensor", A(b), A(b + 64), CS(0), AL.mult,
                  rd=[csn], wr=[rawn], inc="sV")
            em.op("vector", "tensor_tensor", O(b + 64), O(b + 64), A(b),
                  AL.add, rd=[rawn], wr=[outn], inc="sV")

    def rope_from_psum(pU, pUn, pL, pLn, outU, outUn, outL, outLn,
                       oslice, cosT, cosn, sinT, sinn, tslice):
        em.op("vector", "tensor_tensor", outU[:, oslice], pU[:, 0:512],
              cosT[:, tslice], AL.mult, rd=[pUn, cosn], wr=[outUn], inc="sV")
        em.op("vector", "tensor_tensor", bcst[:, :], pL[:, 0:512],
              sinT[:, tslice], AL.mult, rd=[pLn, sinn], wr=["bcst"], inc="sV")
        em.op("vector", "tensor_tensor", outU[:, oslice], outU[:, oslice],
              bcst[:, :], AL.subtract, rd=["bcst"], wr=[outUn], inc="sV")
        em.op("vector", "tensor_tensor", bcst[:, :], pU[:, 0:512],
              sinT[:, tslice], AL.mult, rd=[pUn, sinn], wr=["bcst"], inc="sV")
        em.op("vector", "tensor_tensor", outL[:, oslice], pL[:, 0:512],
              cosT[:, tslice], AL.mult, rd=[pLn, cosn], wr=[outLn], inc="sV")
        em.op("vector", "tensor_tensor", outL[:, oslice], outL[:, oslice],
              bcst[:, :], AL.add, rd=["bcst"], wr=[outLn], inc="sV")

    def emit_qload(ld):
        tq, dh, sub = ld // 4, (ld // 2) % 2, ld % 2
        buf, bn = (qquadA, "qquadA") if ld % 2 == 0 else (qquadB, "qquadB")
        dcx0 = dh * 8 + sub * 4
        src_ap = bass.AP(dXQ, dcx0 * 128 * NQ + tq * 512,
                         [[NQ, 128], [128 * NQ, 4], [1, 512]])
        dst_ap = bass.AP(buf, 0, [[2048, 128], [512, 4], [1, 512]])
        em.op("gpsimd", "dma_start", out=dst_ap, in_=src_ap, wr=[bn],
              inc=("dQg" if ld % 2 == 0 else "dQgB"), n=16)

    def emit_group_proj(g):
        emit_qload(0)
        emit_qload(1)
        pidx = 0
        for quad in range(4):
            kb0, kb1 = (0, 1) if quad % 2 == 0 else (2, 3)
            for dchunk in range(16):
                par = pidx % 2
                buf = xq_bufs[par][:, :]
                bn = "xqc%d" % par
                em.op("sync", "dma_start", out=buf,
                      in_=ap2(dX, dchunk * 128 * NTOK + quad * 512, NTOK, 128, 512),
                      wr=[bn], inc=("dXA" if par == 0 else "dXB"), n=16)
                if dchunk == 0:
                    em.pre("tensor", rd=[bn, "wk_sl"], wr=[PB[kb0], PB[kb1]])
                    em._emit_op("tensor", "matmul",
                                (pb[kb0][:, 0:512],
                                 wk_sl[:, dchunk * 256:dchunk * 256 + 128],
                                 buf),
                                dict(start=True, stop=False), None, 1)
                    ev = em._emit_op("tensor", "matmul",
                                     (pb[kb1][:, 0:512],
                                      wk_sl[:, dchunk * 256 + 128:
                                            dchunk * 256 + 256], buf),
                                     dict(start=True, stop=False), "sT", 1)
                    em.post(ev, rd=[bn, "wk_sl"], wr=[PB[kb0], PB[kb1]])
                    em.pre("tensor", rd=[bn, "wv_sl"],
                           wr=[PB[4], PB[5], PB[6], PB[7]])
                    for tt in range(4):
                        ev = em._emit_op(
                            "tensor", "matmul",
                            (pb[4 + tt][:, 0:256],
                             buf[:, tt * 128:tt * 128 + 128],
                             wv_sl[:, dchunk * 256:dchunk * 256 + 256]),
                            dict(start=True, stop=False),
                            "sT" if tt == 3 else None, 1)
                    em.post(ev, rd=[bn, "wv_sl"],
                            wr=[PB[4], PB[5], PB[6], PB[7]])
                    pidx += 1
                    continue
                em.pre("tensor", rd=[bn, "wk_sl", "wv_sl"], wr=[])
                em._emit_op("tensor", "matmul",
                            (pb[kb0][:, 0:512],
                             wk_sl[:, dchunk * 256:dchunk * 256 + 128], buf),
                            dict(start=False, stop=False), None, 1)
                em._emit_op("tensor", "matmul",
                            (pb[kb1][:, 0:512],
                             wk_sl[:, dchunk * 256 + 128:dchunk * 256 + 256],
                             buf),
                            dict(start=False, stop=False), None, 1)
                for tt in range(4):
                    ev = em._emit_op(
                        "tensor", "matmul",
                        (pb[4 + tt][:, 0:256],
                         buf[:, tt * 128:tt * 128 + 128],
                         wv_sl[:, dchunk * 256:dchunk * 256 + 256]),
                        dict(start=False, stop=False),
                        "sT" if tt == 3 else None, 1)
                em.post(ev, rd=[bn, "wk_sl", "wv_sl"],
                        wr=[PB[kb0], PB[kb1], PB[4], PB[5], PB[6], PB[7]])
                pidx += 1
            em.pre("tensor", rd=["negmrow", "crow"], wr=[])
            em._emit_op("tensor", "matmul",
                        (pb[kb0][:, 0:512], crow[0:1, 0:128],
                         negmrow[0:1, quad * 512:quad * 512 + 512]),
                        dict(start=False, stop=True), None, 1)
            em._emit_op("tensor", "matmul",
                        (pb[kb1][:, 0:512], crow[0:1, 128:256],
                         negmrow[0:1, quad * 512:quad * 512 + 512]),
                        dict(start=False, stop=True), None, 1)
            for tt in range(4):
                tg = quad * 4 + tt
                ev = em._emit_op("tensor", "matmul",
                                 (pb[4 + tt][:, 0:256],
                                  negmrow[0:1, tg * 128:tg * 128 + 128],
                                  crow[0:1, 256:512]),
                                 dict(start=False, stop=True),
                                 "sT" if tt == 3 else None, 1)
            em.post(ev, rd=["negmrow", "crow"],
                    wr=[PB[kb0], PB[kb1], PB[4], PB[5], PB[6], PB[7]])
            qs = slice(quad * 512, quad * 512 + 512)
            rope_from_psum(pb[kb0], PB[kb0], pb[kb1], PB[kb1], kropU, "kropU",
                           kropL, "kropL", qs, csrC, "csrC", csrS, "csrS", qs)
            for tt in range(4):
                tg = quad * 4 + tt
                em.op("scalar", "mul",
                      vrowsA[:, tg * 128:tg * 128 + 128],
                      pb[4 + tt][:, 0:128], rstdc[:, tg:tg + 1],
                      rd=[PB[4 + tt], "rstdc"], wr=["vrowsA"], inc="sA")
                em.op("scalar", "mul",
                      vrowsB[:, tg * 128:tg * 128 + 128],
                      pb[4 + tt][:, 128:256], rstdc[:, tg:tg + 1],
                      rd=[PB[4 + tt], "rstdc"], wr=["vrowsB"], inc="sA")
        for tq in range(2):
            qp0, qp1 = (0, 1) if tq == 0 else (2, 3)
            for dh in range(2):
                for sub in range(2):
                    ld = tq * 4 + dh * 2 + sub
                    buf, bn = ((qquadA, "qquadA") if ld % 2 == 0
                               else (qquadB, "qquadB"))
                    for dchunk in range(4):
                        dcx = dh * 8 + sub * 4 + dchunk
                        em.pre("tensor", rd=[bn, "wq_sl"],
                               wr=[PB[qp0], PB[qp1]] if dcx == 0 else [])
                        em._emit_op(
                            "tensor", "matmul",
                            (pb[qp0][:, 0:512],
                             wq_sl[:, dcx * 256:dcx * 256 + 128],
                             buf[:, dchunk * 512:dchunk * 512 + 512]),
                            dict(start=(dcx == 0), stop=False), None, 1)
                        ev = em._emit_op(
                            "tensor", "matmul",
                            (pb[qp1][:, 0:512],
                             wq_sl[:, dcx * 256 + 128:dcx * 256 + 256],
                             buf[:, dchunk * 512:dchunk * 512 + 512]),
                            dict(start=(dcx == 0), stop=False), "sT", 1)
                        em.post(ev, rd=[bn, "wq_sl"], wr=[PB[qp0], PB[qp1]])
                    if ld + 2 < 8:
                        emit_qload(ld + 2)
            em.pre("tensor", rd=["negmqrow", "crow"], wr=[])
            em._emit_op("tensor", "matmul",
                        (pb[qp0][:, 0:512], crow[0:1, 512:640],
                         negmqrow[0:1, tq * 512:tq * 512 + 512]),
                        dict(start=False, stop=True), None, 1)
            ev = em._emit_op("tensor", "matmul",
                             (pb[qp1][:, 0:512], crow[0:1, 640:768],
                              negmqrow[0:1, tq * 512:tq * 512 + 512]),
                             dict(start=False, stop=True), "sT", 1)
            em.post(ev, rd=["negmqrow", "crow"], wr=[PB[qp0], PB[qp1]])
            qs = slice(tq * 512, tq * 512 + 512)
            rope_from_psum(pb[qp0], PB[qp0], pb[qp1], PB[qp1], qropU, "qropU",
                           qropL, "qropL", qs, csrQC, "csrQC", csrQS, "csrQS",
                           qs)


    def emit_transposes(src, srcn, src_off, dst, dstn, ntile):
        for grp in range(0, ntile, 4):
            em.pre("tensor", rd=[srcn, "ident"], wr=[PB[0]])
            for j in range(4):
                t = grp + j
                ev = em._emit_op("tensor", "transpose",
                                 (pb[0][:, j * 128:j * 128 + 128],
                                  src[:, src_off + t * 256:src_off + t * 256 + 128],
                                  ident[:]), {}, "sT" if j == 3 else None, 1)
            em.post(ev, rd=[srcn, "ident"], wr=[PB[0]])
            em.op("scalar", "copy", dst[:, grp * 128:grp * 128 + 512],
                  pb[0][:, 0:512], rd=[PB[0]], wr=[dstn], inc="sA")

    def emit_attention_head(g, hh):
        h = 2 * g + hh
        vrows, vn = (vrowsA, "vrowsA") if hh == 0 else (vrowsB, "vrowsB")
        em.op("gpsimd", "dma_start", out=kT[0:64, :],
              in_=kropU[hh * 64:hh * 64 + 64, :],
              rd=["kropU"], wr=["kT"], inc="dRcK", n=16)
        em.op("gpsimd", "dma_start", out=kT[64:128, :],
              in_=kropL[hh * 64:hh * 64 + 64, :],
              rd=["kropL"], wr=["kT"], inc="dRcK", n=16)
        em.op("gpsimd", "dma_start", out=qTt[0:64, :],
              in_=qropU[hh * 64:hh * 64 + 64, :],
              rd=["qropU"], wr=["qT"], inc="dRcQ", n=16)
        em.op("gpsimd", "dma_start", out=qTt[64:128, :],
              in_=qropL[hh * 64:hh * 64 + 64, :],
              rd=["qropL"], wr=["qT"], inc="dRcQ", n=16)
        iters = []
        for j in range(4):
            iters.append((0, j, None, (0, j), j == 0, False))
        for j in range(4):
            iters.append((0, 4 + j, 1, (1, j), False, j == 3))
        for j in range(8):
            iters.append((1, j, None, None, j == 0, False))
        for j in range(4):
            iters.append((1, 8 + j, None, (1, j), False, False))
        for j in range(4):
            iters.append((1, 12 + j, 0, (0, j), False, j == 3))
        SB = (pb[0], pb[1], pb[4])
        SBN = (PB[0], PB[1], PB[4])

        def s_mm(i):
            qb, kc = iters[i][0], iters[i][1]
            em.op("tensor", "matmul", SB[i % 3][:, 0:512],
                  kT[:, kc * 128:kc * 128 + 128],
                  qTt[:, qb * 512:qb * 512 + 512],
                  start=True, stop=True, rd=["kT", "qT"], wr=[SBN[i % 3]],
                  inc="sT")

        s_mm(0)
        for i, (qb, kc, fcol, mt, ostart, ostop) in enumerate(iters):
            if i + 1 < len(iters):
                s_mm(i + 1)
            lhs_v = vrows[:, kc * 128:kc * 128 + 128]
            ob, obn = (pb[2], PB[2]) if qb == 0 else (pb[3], PB[3])
            sbank, sbn = SB[i % 3], SBN[i % 3]
            par = i % 2
            s2s = s2b[:, par * 512:par * 512 + 512]
            s2n = "s2b%d" % par
            s4s = s4b[:, 0:512] if par == 0 else xqcA[:, 0:512]
            s4n = "s4b0" if par == 0 else "xqc0"
            if fcol is None:
                em.op("scalar", "activation", s2s, sbank[:, 0:512],
                      AF.Square, rd=[sbn], wr=[s2n], inc="sA")
            else:
                em.op("scalar", "activation", s2s, sbank[:, 0:512],
                      AF.Square, scale=selt[:, fcol:fcol + 1],
                      rd=[sbn, "selt"], wr=[s2n], inc="sA")
            em.op("vector", "tensor_tensor", s4s, s2s, s2s, AL.mult,
                  rd=[s2n], wr=[s4n], inc="sV")
            if mt is not None:
                fam, j = mt
                em.op("vector", "tensor_tensor", s4s, s4s,
                      mtil[:, fam * 2048 + j * 512:fam * 2048 + j * 512 + 512],
                      AL.mult, rd=["mtil"], wr=[s4n], inc="sV")
            if ostart:
                em.op("gpsimd", "tensor_copy",
                      tac[:, qb * 512:qb * 512 + 512], s4s,
                      rd=[s4n], wr=["T%d" % qb], inc="sG")
            else:
                em.op("gpsimd", "tensor_tensor",
                      tac[:, qb * 512:qb * 512 + 512],
                      tac[:, qb * 512:qb * 512 + 512], s4s, AL.add,
                      rd=[s4n], wr=["T%d" % qb], inc="sG")
            em.op("tensor", "matmul", ob[:, 0:512], lhs_v, s4s,
                  start=ostart, stop=ostop, rd=[vn, s4n], wr=[obn],
                  inc="sT")
            if i == 7:
                em.op("tensor", "matmul", pb[5][0:1, 0:512], ones[:, 0:1],
                      tac[:, 0:512], start=True, stop=True,
                      rd=["T0", "ones"], wr=[PB[5]], inc="sT")
                em.op("vector", "tensor_single_scalar", drow[0:1, 0:512],
                      pb[5][0:1, 0:512], 1e-6, AL.max, rd=[PB[5]],
                      wr=["drow"], inc="sV")
            if i == 8:
                em.op("vector", "reciprocal", drow[0:1, 512:768],
                      drow[0:1, 0:256], wr=["drow"], inc="sV")
            if i == 10:
                em.op("vector", "reciprocal", drow[0:1, 768:1024],
                      drow[0:1, 256:512], wr=["drow"], inc="sV")
            if i == 13:
                em.op("tensor", "matmul", pb[5][:, 0:512], ones[0:1, 0:128],
                      drow[0:1, 512:1024], start=True, stop=True,
                      rd=["drow", "ones"], wr=[PB[5]], inc="sT")
                em.op("scalar", "copy", bcst[:, 0:512], pb[5][:, 0:512],
                      rd=[PB[5]], wr=["bcst"], inc="sA")
                em.op("vector", "tensor_tensor", obuf[:, 0:512],
                      pb[2][:, 0:512], bcst[:, 0:512], AL.mult,
                      rd=[PB[2], "bcst"], wr=["obuf"], inc="sV")
        em.op("tensor", "matmul", pb[6][0:1, 0:512], ones[:, 0:1],
              tac[:, 512:1024], start=True, stop=True,
              rd=["T1", "ones"], wr=[PB[6]], inc="sT")
        em.op("vector", "tensor_single_scalar", drow[0:1, 0:512],
              pb[6][0:1, 0:512], 1e-6, AL.max, rd=[PB[6]], wr=["drow"],
              inc="sV")
        em.op("vector", "reciprocal", drow[0:1, 512:1024],
              drow[0:1, 0:512], wr=["drow"], inc="sV")
        em.op("tensor", "matmul", pb[6][:, 0:512], ones[0:1, 0:128],
              drow[0:1, 512:1024], start=True, stop=True,
              rd=["drow", "ones"], wr=[PB[6]], inc="sT")
        em.op("scalar", "copy", bcst[:, 0:512], pb[6][:, 0:512],
              rd=[PB[6]], wr=["bcst"], inc="sA")
        em.op("vector", "tensor_tensor", obuf[:, 512:1024],
              pb[3][:, 0:512], bcst[:, 0:512], AL.mult,
              rd=[PB[3], "bcst"], wr=["obuf"], inc="sV")
        em.op("gpsimd", "dma_start",
              out=ap2(dOS, h * 128 * NQ, NQ, 128, 1024), in_=obuf[:, 0:1024],
              rd=["obuf"], inc="dOS", n=16)

    sc1b.close()

    for g in range(NH // 2):
        if g > 0:
            emit_wslice_load(g)
        emit_colsums(g)
        emit_group_proj(g)
        for hh in range(2):
            emit_attention_head(g, hh)

    em.barrier(ESEM)
    sc1.close()

    kintT = sb("kintT", [128, NT * 1024], BF16)
    recs = sb("recs", [128, QT])
    rowstage = sb("rowstage", [8, 128])
    recrow = sb("recrow", [1, 1024])
    recb = sb("recb", [128, 1024])

    sc2b = contextlib.ExitStack()
    sc2a = contextlib.ExitStack()
    x2f_ = [sb("x2f%d" % p, [128, 2048], scope=sc2b) for p in range(2)]
    lnb_ = [sb("lnb%d" % p, [128, 2048], scope=sc2b) for p in range(2)]
    qb2_ = [sb("qb2%d" % p, [128, 2048], scope=sc2b) for p in range(2)]
    qb3_ = [sb("qb3%d" % p, [128, 2048], scope=sc2b) for p in range(2)]
    kib_ = [sb("kib%d" % p, [128, 2048], BF16, scope=sc2b) for p in range(2)]
    col_ = [sb("colq%d" % p, [128, 8], scope=sc2b) for p in range(2)]
    woT_sb = sb("woT_sb", [128, 16 * 1024], scope=sc2a)
    oLj = sb("oLj", [128, 2 * 2048], scope=sc2a)
    xrt = sb("xrt", [128, 1024], scope=sc2a)
    x2t = sb("x2t", [128, 1024], scope=sc2a)

    def quant_gen(jt):
        qp = jt % 2
        x2f, lnb, qb2, qb3, kib, col = (x2f_[qp], lnb_[qp], qb2_[qp],
                                        qb3_[qp], kib_[qp], col_[qp])
        sx2f, slnb, sqb2, sqb3, skib = ("x2f%d" % qp, "lnb%d" % qp,
                                        "qb2%d" % qp, "qb3%d" % qp,
                                        "kib%d" % qp)
        scol, scol2 = "colq%d" % qp, "colq2%d" % qp
        if qp == 0:
            em.op("sync", "dma_start", out=x2f[:, 0:2048],
                  in_=ap2(dX2, jt * 128 * D, D, 128, 2048),
                  rd=["x2s%d_0" % jt, "x2s%d_1" % jt], wr=[sx2f],
                  inc="dX2f", n=16)
        else:
            em.op("sync", "dma_start", out=x2f[:, 0:1024],
                  in_=ap2(dX2, jt * 128 * D, D, 128, 1024),
                  rd=["x2s%d_0" % jt], wr=[sx2f], inc="dX2fB", n=16)
            em.op("scalar", "copy", x2f[:, 1024:2048], x2t[:, 0:1024],
                  rd=["x2t"], wr=[sx2f], inc="sA")
        yield
        em.op("vector", "reduce_sum", col[:, 0:1], x2f[:, :], AX.X,
              rd=[sx2f], wr=[scol], inc="sV")
        yield
        em.op("scalar", "activation", qb2[:, :], x2f[:, :], AF.Square,
              rd=[sx2f], wr=[sqb2], inc="sA")
        yield
        em.op("vector", "reduce_sum", col[:, 1:2], qb2[:, :], AX.X,
              rd=[sqb2], wr=[scol], inc="sV")
        yield
        em.op("vector", "tensor_scalar", col[:, 0:1], col[:, 0:1],
              1.0 / D, None, AL.mult, wr=[scol], inc="sV")
        yield
        em.op("vector", "tensor_scalar", col[:, 1:2], col[:, 1:2],
              1.0 / D, None, AL.mult, wr=[scol], inc="sV")
        yield
        em.op("vector", "tensor_tensor", col[:, 2:3], col[:, 0:1],
              col[:, 0:1], AL.mult, wr=[scol], inc="sV")
        yield
        em.op("vector", "tensor_tensor", col[:, 1:2], col[:, 1:2],
              col[:, 2:3], AL.subtract, wr=[scol], inc="sV")
        yield
        em.op("vector", "tensor_scalar", col[:, 2:3], col[:, 1:2],
              1e-5, None, AL.add, wr=[scol], inc="sV")
        yield
        em.op("vector", "reciprocal", col[:, 3:4], col[:, 2:3],
              wr=[scol], inc="sV")
        yield
        em.op("vector", "tensor_tensor", col[:, 4:5], col[:, 1:2],
              col[:, 3:4], AL.mult, wr=[scol], inc="sV")
        yield
        em.op("vector", "tensor_scalar", col[:, 4:5], col[:, 4:5],
              1e-8, None, AL.add, wr=[scol], inc="sV")
        yield
        em.op("vector", "tensor_tensor", col[:, 4:5], col[:, 4:5],
              col[:, 2:3], AL.mult, wr=[scol], inc="sV")
        yield
        em.op("scalar", "activation", col[:, 5:6], col[:, 4:5], AF.Sqrt,
              rd=[scol], wr=[scol2], inc="sA")
        yield
        em.op("vector", "reciprocal", col[:, 3:4], col[:, 5:6],
              rd=[scol2], wr=[scol], inc="sV")
        yield
        for _ in range(2):
            em.op("vector", "tensor_tensor", col[:, 6:7], col[:, 3:4],
                  col[:, 3:4], AL.mult, wr=[scol], inc="sV")
            yield
            em.op("vector", "tensor_tensor", col[:, 6:7], col[:, 6:7],
                  col[:, 4:5], AL.mult, wr=[scol], inc="sV")
            yield
            em.op("vector", "tensor_scalar", col[:, 6:7], col[:, 6:7],
                  -0.5, 1.5, AL.mult, AL.add, wr=[scol], inc="sV")
            yield
            em.op("vector", "tensor_tensor", col[:, 3:4], col[:, 3:4],
                  col[:, 6:7], AL.mult, wr=[scol], inc="sV")
            yield
        be, bs = "vector", "sV"
        em.op(be, "tensor_scalar", lnb[:, :], x2f[:, :],
              col[:, 0:1], col[:, 3:4], AL.subtract, AL.mult,
              rd=[sx2f, scol], wr=[slnb], inc=bs)
        yield
        em.op("vector", "reduce_max", col[:, 5:6], lnb[:, :], AX.X,
              apply_absolute_value=True, rd=[slnb], wr=[scol], inc="sV")
        yield
        em.op("vector", "reciprocal", col[:, 6:7], col[:, 5:6],
              wr=[scol], inc="sV")
        yield
        em.op("vector", "tensor_scalar", col[:, 6:7], col[:, 6:7],
              127.0, None, AL.mult, wr=[scol], inc="sV")
        yield
        em.op("vector", "reciprocal", recs[:, jt:jt + 1], col[:, 6:7],
              rd=[scol], wr=["recs"], inc="sV")
        yield
        em.op("scalar", "activation", qb2[:, :], lnb[:, :], AF.Abs,
              scale=col[:, 6:7], rd=[slnb, scol], wr=[sqb2], inc="sA")
        yield
        em.op("scalar", "activation", qb3[:, :], qb2[:, :], AF.Copy,
              scale=-1.0, bias=63.5,
              rd=[sqb2, "constv"], wr=[sqb3], inc="sA")
        yield
        for m in (64.0, 32.0, 16.0, 8.0, 4.0, 2.0, 1.0):
            em.op("scalar", "activation", x2f[:, :], qb3[:, :], AF.Sign,
                  rd=[sqb3], wr=[sx2f], inc="sA")
            yield
            em.op(be, "scalar_tensor_tensor", qb3[:, :], x2f[:, :],
                  -m / 2.0, qb3[:, :], AL.mult, AL.add,
                  rd=[sx2f], wr=[sqb3], inc=bs)
            yield
        em.op(be, "scalar_tensor_tensor", qb2[:, :], qb3[:, :],
              1.0, qb2[:, :], AL.mult, AL.add, rd=[sqb3], wr=[sqb2], inc=bs)
        yield
        em.op("scalar", "activation", qb3[:, :], lnb[:, :], AF.Sign,
              rd=[slnb], wr=[sqb3], inc="sA")
        yield
        em.op(be, "tensor_tensor", kib[:, :], qb2[:, :], qb3[:, :],
              AL.mult, rd=[sqb2, sqb3], wr=[skib], inc=bs)
        yield


    def emit_quant_pair(j0, j1):
        gens = [quant_gen(j0), quant_gen(j1)]
        while gens:
            nxt = []
            for g in gens:
                try:
                    next(g)
                    nxt.append(g)
                except StopIteration:
                    pass
            gens = nxt

    def emit_quant_tr(jt):
        qp = jt % 2
        kib, skib = kib_[qp], "kib%d" % qp
        for grp in range(0, 16, 4):
            em.pre("tensor", rd=[skib, "identB"], wr=[PB[2]])
            for j in range(4):
                dcx = grp + j
                ev = em._emit_op("tensor", "transpose",
                                 (pb[2][:, j * 64:j * 64 + 64].bitcast(BF16),
                                  kib[:, dcx * 128:dcx * 128 + 128], identB[:]),
                                 {}, "sT" if j == 3 else None, 1)
            em.post(ev, rd=[skib, "identB"], wr=[PB[2]])
            em.op("scalar", "copy",
                  bass.AP(kintT, grp * 1024 + jt * 128,
                          [[NT * 1024, 128], [1024, 4], [1, 128]]),
                  bass.AP(pb[2].bitcast(BF16), 0,
                          [[1024, 128], [128, 4], [1, 128]]),
                  rd=[PB[2]], wr=["kintT"], inc="sA")

    qpend = []

    def qpump(n):
        cnt = 0
        idx = 0
        while cnt < n and qpend:
            if idx >= len(qpend):
                idx = 0
            try:
                next(qpend[idx][1])
                idx += 1
            except StopIteration:
                qpend.pop(idx)
            cnt += 1

    def qflush_jt(j):
        while any(t <= j for t, _ in qpend):
            qpump(1)

    for dh in range(2):
        for p in range(4):
            srcp = bass.AP(dWO, dh * 1024 + p * 4 * 128 * D,
                           [[D, 128], [128 * D, 4], [1, 1024]])
            dstp = bass.AP(woT_sb, p * 4 * 1024,
                           [[16 * 1024, 128], [1024, 4], [1, 1024]])
            em.op("sync", "dma_start", out=dstp, in_=srcp, wr=["woT%d" % p],
                  inc=("dWo", "dWoB", "dWoC", "dWoD")[p], n=16)
        for jt in range(QT):
            par = jt % 2
            ol = oLj[:, par * 2048:par * 2048 + 2048]
            oln = "oLj%d" % par
            em.op("sync", "dma_start", out=ol,
                  in_=bass.AP(dOS, jt * 128, [[NQ, 128], [128 * NQ, 16], [1, 128]]),
                  wr=[oln], inc=("dO0" if par == 0 else "dO1"), n=16)
            em.op("sync", "dma_start", out=xrt[:, 0:1024],
                  in_=ap2(dXR, jt * 128 * D + dh * 1024, D, 128, 1024),
                  wr=["xrt"], inc="dXr", n=16)
            qpump(8)
            for dc in range(2):
                bank, bkn = pb[dc], PB[dc]
                for p in range(4):
                    em.pre("tensor", rd=[oln, "woT%d" % p],
                           wr=[bkn] if p == 0 else [])
                    for hq in range(4):
                        hx = p * 4 + hq
                        ev = em._emit_op(
                            "tensor", "matmul",
                            (bank[:, 0:512], ol[:, hx * 128:hx * 128 + 128],
                             woT_sb[:, hx * 1024 + dc * 512:
                                    hx * 1024 + dc * 512 + 512]),
                            dict(start=(hx == 0), stop=False),
                            "sT" if hq == 3 else None, 1)
                    em.post(ev, rd=[oln, "woT%d" % p], wr=[bkn])
                em.pre("tensor", rd=["xrt", "ident"], wr=[])
                ev = em._emit_op("tensor", "matmul",
                                 (bank[:, 0:512], ident[:],
                                  xrt[:, dc * 512:dc * 512 + 512]),
                                 dict(start=False, stop=True), "sT", 1)
                em.post(ev, rd=["xrt", "ident"], wr=[bkn])
                em.op("scalar", "copy", x2t[:, dc * 512:dc * 512 + 512],
                      bank[:, 0:512], rd=[bkn], wr=["x2t"], inc="sA")
                qpump(4)
            em.op("gpsimd", "dma_start",
                  out=ap2(dX2, jt * 128 * D + dh * 1024, D, 128, 1024),
                  in_=x2t[:, 0:1024], rd=["x2t"],
                  wr=["x2s%d_%d" % (jt, dh)], inc="dX2s", n=16)
            if dh == 1:
                if jt % 2 == 1:
                    if jt >= 3:
                        qflush_jt(jt - 2)
                        emit_quant_tr(jt - 3)
                        emit_quant_tr(jt - 2)
                    qpend.append((jt - 1, quant_gen(jt - 1)))
                    qpend.append((jt, quant_gen(jt)))
                qpump(16)
    qflush_jt(QT - 1)
    emit_quant_tr(QT - 2)
    emit_quant_tr(QT - 1)

    em.op("tensor", "transpose", pb[2][0:8, 0:128], recs[:, :], ident[:],
          rd=["recs", "ident"], wr=[PB[2]], inc="sT")
    em.op("vector", "tensor_copy", rowstage[:, :], pb[2][0:8, 0:128],
          rd=[PB[2]], wr=["rowstage"], inc="sV")
    em.op("gpsimd", "dma_start",
          out=bass.AP(recrow, 0, [[1024, 1], [1, 1024]]),
          in_=bass.AP(rowstage, 0, [[128, 8], [1, 128]]),
          rd=["rowstage"], wr=["recrow"], inc="dNr", n=16)
    em.op("tensor", "matmul", pb[2][:, 0:512], ones[0:1, 0:128],
          recrow[0:1, 0:512], start=True, stop=True,
          rd=["recrow", "ones"], wr=[PB[2]], inc="sT")
    em.op("scalar", "copy", recb[:, 0:512], pb[2][:, 0:512],
          rd=[PB[2]], wr=["recb"], inc="sA")
    em.op("tensor", "matmul", pb[2][:, 0:512], ones[0:1, 0:128],
          recrow[0:1, 512:1024], start=True, stop=True,
          rd=["recrow", "ones"], wr=[PB[2]], inc="sT")
    em.op("scalar", "copy", recb[:, 512:1024], pb[2][:, 0:512],
          rd=[PB[2]], wr=["recb"], inc="sA")

    def soft_barrier():
        vals = dict(em.cnt)
        for eng in ENGINES:
            for s, v in vals.items():
                em.wait(eng, s, v)

    soft_barrier()
    sc2a.close()

    sc4 = contextlib.ExitStack()
    wstg = [sb("wstg%d" % p, [128, 2048], scope=sc4) for p in range(2)]
    sgw = [sb("sgw%d" % p, [128, NT * 256], BF16, scope=sc4) for p in range(2)]
    suw = [sb("suw%d" % p, [128, NT * 256], BF16, scope=sc4) for p in range(2)]
    sdw = [sb("sdw%d" % p, [128, 2 * 2048], BF16, scope=sc4) for p in range(2)]
    gbuf = sb("gbuf", [128, 2048], scope=sc4)
    ubuf = sb("ubuf", [128, 2048], scope=sc4)
    pTl = [sb("pTl%d" % p, [128, 2 * 1024], BF16, scope=sc4) for p in range(2)]

    witems = []

    def enq_gu(ch):
        p = ch % 2
        for (dram, dstbuf, dn) in ((dWG, sgw[p], "sgw%d" % p),
                                   (dWU, suw[p], "suw%d" % p)):
            for half in range(2):
                witems.append((
                    bass.AP(dram, ch * 256 + half * 8 * 128 * HF,
                            [[HF, 128], [128 * HF, 8], [1, 256]]),
                    dstbuf, half * 2048, dn))

    def enq_d(ch):
        p = ch % 2
        for hs in range(2):
            witems.append((
                ap2(dWD, (ch * 256 + hs * 128) * D, D, 128, 2048),
                sdw[p], hs * 2048, "sdw%d" % p))

    wst = {"loaded": 0, "signed": 0}

    def wpump(n, flush=False):
        while True:
            did = False
            if n > 0 or flush:
                if wst["loaded"] < len(witems) and                         wst["loaded"] - wst["signed"] < 2:
                    i = wst["loaded"]
                    srcap = witems[i][0]
                    sl = i % 2
                    em.op("sync", "dma_start", out=wstg[sl][:, :], in_=srcap,
                          wr=["wstg%d" % sl], inc="dWg%d" % sl, n=16)
                    wst["loaded"] += 1
                    did = True
                lead = 0 if wst["loaded"] == len(witems) else 1
                if wst["signed"] < wst["loaded"] - (0 if flush else lead):
                    i = wst["signed"]
                    _, dstbuf, doff, dn = witems[i]
                    sl = i % 2
                    em.op("scalar", "activation",
                          dstbuf[:, doff:doff + 2048], wstg[sl][:, :],
                          AF.Sign, rd=["wstg%d" % sl], wr=[dn], inc="sA")
                    wst["signed"] += 1
                    did = True
            if not did:
                return
            n -= 1
            if n <= 0 and not flush:
                return
            if flush and wst["signed"] >= len(witems):
                return

    enq_gu(0)
    enq_d(0)
    enq_d(1)
    wpump(0, flush=True)

    y2t = [(x2f_[0], "x2f0"), (lnb_[0], "lnb0"), (qb2_[0], "qb20"),
           (qb3_[0], "qb30"), (x2f_[1], "x2f1"), (lnb_[1], "lnb1"),
           (qb2_[1], "qb21"), (qb3_[1], "qb31")]
    pTh = [kib_[0], kib_[1]]
    PTHN = ["kib0", "kib1"]

    for ch in range(NCH):
        p = ch % 2
        gn, un, dn = "sgw%d" % p, "suw%d" % p, "sdw%d" % p
        if ch + 1 < NCH:
            enq_gu(ch + 1)
        if ch % 2 == 0 and ch >= 2:
            enq_d(ch)
            enq_d(ch + 1)
        for tokh in range(2):
            for hh in range(2):
                bg, bgn = pb[hh], PB[hh]
                bu, bun = pb[2 + hh], PB[2 + hh]
                em.pre("tensor", rd=["kintT", gn, un], wr=[bgn, bun])
                for dcx in range(16):
                    em._emit_op(
                        "tensor", "matmul",
                        (bg[:, 0:512],
                         sgw[p][:, dcx * 256 + hh * 128:dcx * 256 + hh * 128 + 128],
                         kintT[:, dcx * 1024 + tokh * 512:dcx * 1024 + tokh * 512 + 512]),
                        dict(start=(dcx == 0), stop=(dcx == 15)), None, 1)
                    ev = em._emit_op(
                        "tensor", "matmul",
                        (bu[:, 0:512],
                         suw[p][:, dcx * 256 + hh * 128:dcx * 256 + hh * 128 + 128],
                         kintT[:, dcx * 1024 + tokh * 512:dcx * 1024 + tokh * 512 + 512]),
                        dict(start=(dcx == 0), stop=(dcx == 15)),
                        "sT" if dcx == 15 else None, 1)
                em.post(ev, rd=["kintT", gn, un], wr=[bgn, bun])
                pos = slice(hh * 1024 + tokh * 512, hh * 1024 + tokh * 512 + 512)
                em.op("vector", "tensor_tensor", ubuf[:, pos], bu[:, 0:512],
                      recb[:, tokh * 512:tokh * 512 + 512], AL.mult,
                      rd=[bun, "recb"], wr=["ubuf"], inc="sV")
                em.op("vector", "tensor_tensor", gbuf[:, pos], bg[:, 0:512],
                      recb[:, tokh * 512:tokh * 512 + 512], AL.mult,
                      rd=[bgn, "recb"], wr=["gbuf"], inc="sV")
                em.op("scalar", "activation", gbuf[:, pos], gbuf[:, pos],
                      AF.Silu, wr=["gbuf"], inc="sA")
                em.op("vector", "tensor_tensor", gbuf[:, pos], gbuf[:, pos],
                      ubuf[:, pos], AL.mult, rd=["ubuf"], wr=["gbuf"],
                      inc="sV")
                wpump(2)
                hn, ln_ = PTHN[p], "pTl%d" % p
                em.op("vector", "tensor_copy", pTh[p][:, pos], gbuf[:, pos],
                      rd=["gbuf"], wr=[hn], inc="sV")
                em.op("vector", "tensor_tensor", pTl[p][:, pos], gbuf[:, pos],
                      pTh[p][:, pos], AL.subtract, rd=["gbuf", hn], wr=[ln_],
                      inc="sV")
        if ch % 2 == 1:
            pair = ch // 2
            for jt in range(QT):
                for dc in range(4):
                    bi = 4 + ((jt * 4 + dc) % 4)
                    bank, bkn = pb[bi], PB[bi]
                    first = True
                    for cp in range(2):
                        for hs in range(2):
                            for pt, ptn in ((pTh[cp], PTHN[cp]),
                                            (pTl[cp], "pTl%d" % cp)):
                                last = (cp == 1 and hs == 1 and
                                        ptn.startswith("pTl"))
                                em.pre("tensor", rd=[ptn, "sdw%d" % cp],
                                       wr=[bkn] if first else [])
                                ev = em._emit_op(
                                    "tensor", "matmul",
                                    (bank[:, 0:512],
                                     pt[:, hs * 1024 + jt * 128:hs * 1024 + jt * 128 + 128],
                                     sdw[cp][:, hs * 2048 + dc * 512:hs * 2048 + dc * 512 + 512]),
                                    dict(start=first, stop=last),
                                    "sT" if last else None, 1)
                                if last:
                                    em.post(ev, rd=[PTHN[0], "pTl0", PTHN[1],
                                                    "pTl1", "sdw0", "sdw1"],
                                            wr=[bkn])
                                first = False
                    if dc == 0:
                        wpump(1)
                    y2b, y2n = y2t[jt]
                    dst = y2b[:, dc * 512:dc * 512 + 512]
                    if pair == 0:
                        em.op("vector", "tensor_copy", dst, bank[:, 0:512],
                              rd=[bkn], wr=[y2n], inc="sV")
                    else:
                        em.op("vector", "tensor_tensor", dst, dst,
                              bank[:, 0:512], AL.add, rd=[bkn], wr=[y2n],
                              inc="sV")
    for jt in range(QT):
        em.op("sync", "dma_start", out=ubuf[:, :],
              in_=ap2(dX2, jt * 128 * D, D, 128, 2048),
              rd=["x2s%d_0" % jt, "x2s%d_1" % jt],
              wr=["ubuf"], inc="dX2r", n=16)
        em.op("vector", "tensor_tensor", ubuf[:, :],
              y2t[jt][0][:, 0:2048], ubuf[:, :], AL.add,
              rd=[y2t[jt][1]], wr=["ubuf"], inc="sV")
        em.op("gpsimd", "dma_start", out=ap2(dY, jt * 128 * D, D, 128, 2048),
              in_=ubuf[:, :], rd=["ubuf"], inc="dY", n=16)
    sc4.close()

    em.wait("gpsimd", "dY", C["dY"])
    em.wait("gpsimd", "dOS", C["dOS"])
    em.wait("gpsimd", "dX2s", C["dX2s"])

    def _replay(eng_name):
        def run(e):
            for i, fn in enumerate(em.q[eng_name]):
                try:
                    fn(e)
                except Exception:
                    import traceback
                    print("FAILED at", eng_name, "op", i)
                    traceback.print_exc()
                    raise
        return run

    with nc.Block() as block:
        block.sync(_replay("sync"))
        block.tensor(_replay("tensor"))
        block.vector(_replay("vector"))
        block.scalar(_replay("scalar"))
        block.gpsimd(_replay("gpsimd"))

    stack.close()
    return nc


_NC_CACHE = None


def _get_nc():
    global _NC_CACHE
    if _NC_CACHE is None:
        _NC_CACHE = build()
    return _NC_CACHE


def kernel(L0_attn_x, L0_attn_Wq, L0_attn_Wk, L0_attn_Wv, L0_attn_Wo,
           L0_ffn_W_g, L0_ffn_W_u, L0_ffn_W_d):
    x = np.ascontiguousarray(np.asarray(L0_attn_x, dtype=np.float32))

    def _ulperm(wT):
        P = np.empty_like(wT)
        for g in range(8):
            b = g * 256
            P[:, b:b + 64] = wT[:, b:b + 64]
            P[:, b + 64:b + 128] = wT[:, b + 128:b + 192]
            P[:, b + 128:b + 192] = wT[:, b + 64:b + 128]
            P[:, b + 192:b + 256] = wT[:, b + 192:b + 256]
        return np.ascontiguousarray(P)

    wqT = _ulperm(np.asarray(L0_attn_Wq, dtype=np.float32).T)
    wkT = _ulperm(np.asarray(L0_attn_Wk, dtype=np.float32).T)
    wvT = np.ascontiguousarray(np.asarray(L0_attn_Wv, dtype=np.float32).T)
    woT = np.ascontiguousarray(np.asarray(L0_attn_Wo, dtype=np.float32).T)
    wgT = np.ascontiguousarray(np.asarray(L0_ffn_W_g, dtype=np.float32).T)
    wuT = np.ascontiguousarray(np.asarray(L0_ffn_W_u, dtype=np.float32).T)
    wdT = np.ascontiguousarray(np.asarray(L0_ffn_W_d, dtype=np.float32).T)

    _Q, _h = 2097152, 1048576

    def _dyn(qe0, qe1):
        return np.array([[qe0, qe1, qe0 + _h, qe0 + _Q, qe0 + _Q + _h,
                          qe1 + _h, qe1 + _Q, qe1 + _Q + _h]], dtype=np.int32)
    dyn_half = {0: _dyn(0, 1536), 1: _dyn(512, 1024)}
    sel_half = {
        0: np.array([[1.0, 0.0]], dtype=np.float32),
        1: np.array([[0.0, 1.0]], dtype=np.float32),
    }
    qtiles_half = {
        0: [0, 1, 2, 3, 12, 13, 14, 15],
        1: [4, 5, 6, 7, 8, 9, 10, 11],
    }
    perm_half = {}
    for hlf, qt in qtiles_half.items():
        P = np.zeros((NT, QT), dtype=np.float32)
        for j, gt in enumerate(qt):
            P[gt, j] = 1.0
        perm_half[hlf] = P
    toksel = {h: np.concatenate([np.arange(t * 128, t * 128 + 128)
                                 for t in qtiles_half[h]]) for h in (0, 1)}

    in_maps = []
    for c in range(8):
        b = c // 2
        hlf = c % 2
        xb = x[b]
        in_maps.append({
            "xT": np.ascontiguousarray(xb.T),
            "xqT": np.ascontiguousarray(xb.T[:, toksel[hlf]]),
            "xr": np.ascontiguousarray(xb[toksel[hlf]]),
            "wqT": wqT, "wkT": wkT, "wvT": wvT, "woT": woT,
            "wgT": wgT, "wuT": wuT, "wdT": wdT,
            "dynv_d": dyn_half[hlf], "selv": sel_half[hlf],
            "perm": perm_half[hlf],
        })

    nc = _get_nc()
    res = run_bass_kernel_spmd(nc, in_maps, core_ids=list(range(8)))

    out = np.empty((4, NTOK, D), dtype=np.float32)
    for c in range(8):
        b = c // 2
        hlf = c % 2
        out[b][toksel[hlf]] = res.results[c]["y"]
    return out


if __name__ == "__main__":
    nco = build()
    print("built ok")


# version 22
# speedup vs baseline: 1.0302x; 1.0302x over previous
import numpy as np
import concourse.bass as bass
import concourse.mybir as mybir
from concourse.bass_utils import run_bass_kernel_spmd

F32 = mybir.dt.float32
BF16 = mybir.dt.bfloat16
I32 = mybir.dt.int32
AL = mybir.AluOpType
AF = mybir.ActivationFunctionType
AX = mybir.AxisListType

D = 2048
NTOK = 2048
NT = 16
NQ = 1024
QT = 8
NH = 16
HD = 128
HF = 8192
HC = 256
NCH = HF // HC
PI = float(np.pi)
TWO_PI = 2.0 * np.pi

ENGINES = ("sync", "tensor", "vector", "scalar", "gpsimd")


class Em:

    def __init__(self, nc):
        self.nc = nc
        self.q = {e: [] for e in ENGINES}
        self.semh = {}
        self.cnt = {}
        self.waited = {e: {} for e in ENGINES}
        self.w_ev = {}
        self.r_ev = {}

    def sem(self, name):
        self.semh[name] = self.nc.alloc_semaphore(name=name)
        self.cnt[name] = 0

    def _emit_op(self, eng, meth, a, k, inc, n):
        if inc is not None:
            self.cnt[inc] += n
        iv = (self.semh[inc], n) if inc is not None else None

        def fn(h, _a=a, _k=k, _m=meth, _iv=iv):
            r = getattr(h, _m)(*_a, **_k)
            if _iv is not None:
                r.then_inc(_iv[0], _iv[1])
        self.q[eng].append(fn)
        return (inc, self.cnt[inc]) if inc is not None else None

    def raw(self, eng, fn):
        self.q[eng].append(fn)

    def wait(self, eng, sem, val):
        if val is None or val <= 0:
            return
        if self.waited[eng].get(sem, -1) >= val:
            return
        self.waited[eng][sem] = val
        h = self.semh[sem]
        self.q[eng].append(lambda e, _h=h, _v=val: e.wait_ge(_h, _v))

    def pre(self, eng, rd=(), wr=()):
        for res in rd:
            ev = self.w_ev.get(res)
            if ev:
                self.wait(eng, ev[0], ev[1])
        for res in wr:
            ev = self.w_ev.get(res)
            if ev:
                self.wait(eng, ev[0], ev[1])
            for s, v in self.r_ev.get(res, {}).items():
                self.wait(eng, s, v)

    def post(self, ev, rd=(), wr=()):
        assert ev is not None, "tracked op needs an inc semaphore"
        for res in wr:
            self.w_ev[res] = ev
            self.r_ev[res] = {}
        for res in rd:
            d = self.r_ev.setdefault(res, {})
            if d.get(ev[0], -1) < ev[1]:
                d[ev[0]] = ev[1]

    def op(self, eng, meth, *a, rd=(), wr=(), inc=None, n=1, **k):
        self.pre(eng, rd, wr)
        ev = self._emit_op(eng, meth, a, k, inc, n)
        if ev is not None:
            self.post(ev, rd, wr)
        return ev

    def barrier(self, esem):
        for eng, s in esem.items():
            self.cnt[s] += 1
            h = self.semh[s]
            def closer(e, _h=h):
                e.drain()
                e.nop().then_inc(_h, 1)
            self.q[eng].append(closer)
        vals = dict(self.cnt)
        for eng in ENGINES:
            for s, v in vals.items():
                self.wait(eng, s, v)


def ap2(t, off, prow, npart, nfree, fstep=1):
    return bass.AP(t, off, [[prow, npart], [fstep, nfree]])


def build():
    nc = bass.Bass("TRN2", target_bir_lowering=False, debug=False)

    dX = nc.dram_tensor("xT", [D, NTOK], F32, kind="ExternalInput")
    dXQ = nc.dram_tensor("xqT", [D, NQ], F32, kind="ExternalInput")
    dXR = nc.dram_tensor("xr", [NQ, D], F32, kind="ExternalInput")
    dWQ = nc.dram_tensor("wqT", [D, D], F32, kind="ExternalInput")
    dWK = nc.dram_tensor("wkT", [D, D], F32, kind="ExternalInput")
    dWV = nc.dram_tensor("wvT", [D, D], F32, kind="ExternalInput")
    dWO = nc.dram_tensor("woT", [D, D], F32, kind="ExternalInput")
    dWG = nc.dram_tensor("wgT", [D, HF], F32, kind="ExternalInput")
    dWU = nc.dram_tensor("wuT", [D, HF], F32, kind="ExternalInput")
    dWD = nc.dram_tensor("wdT", [HF, D], F32, kind="ExternalInput")
    dDYN = nc.dram_tensor("dynv_d", [1, 8], I32, kind="ExternalInput")
    dSEL = nc.dram_tensor("selv", [1, 2], F32, kind="ExternalInput")
    dPRM = nc.dram_tensor("perm", [NT, QT], F32, kind="ExternalInput")
    dY = nc.dram_tensor("y", [NQ, D], F32, kind="ExternalOutput")
    dOS = nc.dram_tensor("o_scr", [NH * HD, NQ], F32)
    dX2 = nc.dram_tensor("x2_scr", [NQ, D], F32)

    em = Em(nc)
    globals()["_EM"] = em
    for s in ("sT", "sV", "sA", "sG", "dDy", "dSe", "dPm", "dXA", "dXB",
              "dWk", "dWv", "dWq", "dQs", "dQg", "dQgB", "dNr", "dNq", "dOS", "dWo",
              "dO0", "dO1", "dXr", "dX2s", "dX2f", "dX2fB", "dX2r", "dY", "dRcK", "dRcQ",
              "dPr0", "dPr1", "dPs0", "dPs1", "dWg0", "dWg1", "dWu0", "dWu1",
              "dWd0", "dWd1", "dDu1", "dDu2", "dKtA", "dKtB", "dWoB", "dWoC", "dWoD"):
        em.sem(s)
    C = em.cnt

    ESEM = {"tensor": "sT", "vector": "sV", "scalar": "sA", "gpsimd": "sG"}
    import contextlib
    stack = contextlib.ExitStack()

    def sb(name, shape, dtype=F32, scope=None):
        if scope is None:
            return nc.alloc_sbuf_tensor(name, shape, dtype)
        return scope.enter_context(nc.sbuf_tensor(name, shape, dtype))

    ident = sb("ident", [128, 128])
    identB = sb("identB", [128, 128], BF16)
    ones = sb("ones", [128, 128])
    dynv = sb("dynv", [1, 8], I32)
    selt = sb("selt", [128, 2])
    zrow = sb("zrow", [1, 16])
    constv = sb("constv", [128, 4])

    pb = [nc.alloc_psum_tensor("pb%d" % i, [128, 512], F32) for i in range(8)]
    PB = ["pb%d" % i for i in range(8)]

    rg_v = {k: nc.vector.alloc_register("rv_" + k) for k in ("qe0", "qe1")}

    sc1 = contextlib.ExitStack()
    permm = sb("permm", [NT, QT], scope=sc1)
    selrow = sb("selrow", [1, 2], scope=sc1)
    negm16 = sb("negm16", [NT, 128], scope=sc1)
    negmQ = sb("negmQ", [QT, 128], scope=sc1)
    rstdc = sb("rstdc", [128, NT], scope=sc1)
    csrC = sb("csrC", [128, NTOK], scope=sc1)
    csrS = sb("csrS", [128, NTOK], scope=sc1)
    csrQC = sb("csrQC", [128, NQ], scope=sc1)
    csrQS = sb("csrQS", [128, NQ], scope=sc1)
    crow = sb("crow", [1, 768], scope=sc1)
    xqcA = sb("xqcA", [128, 512], scope=sc1)
    xqcB = sb("xqcB", [128, 512], scope=sc1)
    xq_bufs = [xqcA, xqcB]
    wk_sl = sb("wk_sl", [128, NT * 256], scope=sc1)
    wv_sl = sb("wv_sl", [128, NT * 256], scope=sc1)
    wq_sl = sb("wq_sl", [128, NT * 256], scope=sc1)
    kropU = sb("kropU", [128, NTOK], scope=sc1)
    kropL = sb("kropL", [128, NTOK], scope=sc1)
    qropU = sb("qropU", [128, NQ], scope=sc1)
    qropL = sb("qropL", [128, NQ], scope=sc1)
    vrowsA = sb("vrowsA", [128, NT * 128], scope=sc1)
    vrowsB = sb("vrowsB", [128, NT * 128], scope=sc1)
    kT = sb("kTb", [128, NTOK], scope=sc1)
    qTt = sb("qTb", [128, NQ], scope=sc1)
    qquadA = sb("qquadA", [128, 4 * 512], scope=sc1)
    qquadB = sb("qquadB", [128, 4 * 512], scope=sc1)
    mtil = sb("mtil", [128, 8 * 512], scope=sc1)
    s2b = sb("s2b", [128, 1024], scope=sc1)
    s4b = sb("s4b", [128, 512], scope=sc1)
    tac = sb("tac", [128, 2 * 512], scope=sc1)
    obuf = sb("obuf", [128, 1024], scope=sc1)
    drow = sb("drow", [1, 1024], scope=sc1)
    negmrow = sb("negmrow", [1, 2048], scope=sc1)
    negmqrow = sb("negmqrow", [1, 1024], scope=sc1)
    bcst = sb("bcst", [128, 512], scope=sc1)
    sc1b = contextlib.ExitStack()
    xsq = sb("xsq", [128, 2048], scope=sc1b)
    rowst16 = sb("rowst16", [16, 128], scope=sc1b)
    statA = sb("statA", [128, NT], scope=sc1b)
    statB = sb("statB", [128, NT], scope=sc1b)
    statC = sb("statC", [128, NT], scope=sc1b)

    em.op("sync", "dma_start", out=dynv[:], in_=dDYN.ap()[0:1, :],
          wr=["dynv"], inc="dDy", n=16)
    em.op("sync", "dma_start", out=selrow[:], in_=dSEL.ap()[0:1, :],
          wr=["selrow"], inc="dSe", n=16)
    em.op("sync", "dma_start", out=permm[:], in_=dPRM.ap()[:, :],
          wr=["permm"], inc="dPm", n=16)

    em.op("gpsimd", "memset", ident[:], 0.0, wr=["identz"], inc="sG")
    em.op("gpsimd", "memset", ones[:], 1.0, wr=["ones"], inc="sG")
    em.op("gpsimd", "memset", zrow[:], 0.0, wr=["zrow"], inc="sG")
    _consts = [float(-np.log(10000.0) / 64.0), -1.0, PI / 2.0, 63.5]
    for ci, cv in enumerate(_consts):
        em.op("gpsimd", "memset", constv[:, ci:ci + 1], cv,
              wr=["constv"], inc="sG")
        nc.const_aps.register(F32, cv, constv[:, ci:ci + 1])
    em.op("gpsimd", "affine_select", out=ident[:], in_=ident[:],
          compare_op=AL.not_equal, fill=1.0, base=0,
          pattern=[[-1, 128]], channel_multiplier=1,
          rd=["identz"], wr=["ident"], inc="sG")
    em.op("scalar", "copy", identB[:], ident[:],
          rd=["ident"], wr=["identB"], inc="sA")

    em.pre("vector", rd=["dynv"])
    em.raw("vector", lambda e: e.load(rg_v["qe0"], dynv[0:1, 0:1]))
    em.raw("vector", lambda e: e.load(rg_v["qe1"], dynv[0:1, 1:2]))

    em.op("tensor", "matmul", pb[7][:, 0:2], ones[0:1, 0:128], selrow[0:1, :],
          start=True, stop=True, rd=["ones", "selrow"], wr=[PB[7]], inc="sT")
    em.op("vector", "tensor_copy", selt[:], pb[7][:, 0:2],
          rd=[PB[7]], wr=["selt"], inc="sV")
    for fam in range(2):
        for j in range(4):
            sl = mtil[:, fam * 2048 + j * 512:fam * 2048 + j * 512 + 512]
            em.op("gpsimd", "memset", sl, 1.0, wr=["mtil"], inc="sG")
            em.op("gpsimd", "affine_select", out=sl, in_=sl,
                  compare_op=AL.is_ge, fill=0.0, base=-128 * j,
                  pattern=[[1, 512]], channel_multiplier=-1,
                  wr=["mtil"], inc="sG")
            em.op("vector", "tensor_scalar", sl, sl,
                  selt[:, fam:fam + 1], selt[:, 1 - fam:2 - fam],
                  AL.mult, AL.add, rd=["selt"], wr=["mtil"], inc="sV")

    def emit_wslice_load(g):
        for (dst, dn, dram, dsm) in ((wv_sl, "wv_sl", dWV, "dWv"),
                                     (wk_sl, "wk_sl", dWK, "dWk"),
                                     (wq_sl, "wq_sl", dWQ, "dWq")):
            src = bass.AP(dram, g * 256,
                          [[NTOK, 128], [128 * NTOK, 16], [1, 256]])
            dap = bass.AP(dst, 0, [[NT * 256, 128], [256, 16], [1, 256]])
            em.op("sync", "dma_start", out=dap, in_=src, wr=[dn], inc=dsm,
                  n=16)

    em.op("tensor", "matmul", pb[0][:, 0:16], ones[0:1, 0:128], zrow[0:1, 0:16],
          start=True, stop=False, rd=["ones", "zrow"], wr=[PB[0]], inc="sT")
    em.op("tensor", "matmul", pb[1][:, 0:16], ones[0:1, 0:128], zrow[0:1, 0:16],
          start=True, stop=False, rd=["ones", "zrow"], wr=[PB[1]], inc="sT")
    pidx = 0
    for dchunk in range(16):
        for piece in range(4):
            par = pidx % 2
            buf = xq_bufs[par][:, :]
            bn = "xqc%d" % par
            sqb = xsq[:, par * 512:par * 512 + 512]
            sqn = "xsq%d" % par
            em.op("sync", "dma_start", out=buf,
                  in_=ap2(dX, dchunk * 128 * NTOK + piece * 512, NTOK, 128, 512),
                  wr=[bn], inc=("dXA" if par == 0 else "dXB"), n=16)
            em.op("scalar", "activation", sqb, buf, AF.Square,
                  rd=[bn], wr=[sqn], inc="sA")
            tg = piece * 4
            em.pre("tensor", rd=[bn, sqn], wr=[PB[0], PB[1]])
            last = (dchunk == 15 and piece == 3)
            for tt in range(4):
                em._emit_op("tensor", "matmul",
                            (pb[0][:, tg + tt:tg + tt + 1],
                             buf[:, tt * 128:tt * 128 + 128], ones[:, 0:1]),
                            dict(start=False, stop=(last and tt == 3)), None, 1)
                ev = em._emit_op("tensor", "matmul",
                                 (pb[1][:, tg + tt:tg + tt + 1],
                                  sqb[:, tt * 128:tt * 128 + 128], ones[:, 0:1]),
                                 dict(start=False, stop=(last and tt == 3)),
                                 "sT", 1)
            em.post(ev, rd=[bn, sqn], wr=[PB[0], PB[1]])
            pidx += 1

    emit_wslice_load(0)

    em.op("vector", "tensor_scalar", statA[:, :], pb[0][:, 0:16],
          -1.0 / D, None, AL.mult, rd=[PB[0]], wr=["statA"], inc="sV")
    em.op("vector", "tensor_scalar", statB[:, :], pb[1][:, 0:16],
          1.0 / D, None, AL.mult, rd=[PB[1]], wr=["statB"], inc="sV")
    em.op("vector", "tensor_tensor", statC[:, :], statA[:, :], statA[:, :],
          AL.mult, rd=["statA"], wr=["statC"], inc="sV")
    em.op("vector", "tensor_tensor", statB[:, :], statB[:, :], statC[:, :],
          AL.subtract, rd=["statC"], wr=["statB"], inc="sV")
    em.op("vector", "tensor_scalar", statB[:, :], statB[:, :],
          1e-5, None, AL.add, wr=["statB"], inc="sV")
    em.op("scalar", "activation", statC[:, :], statB[:, :], AF.Sqrt,
          rd=["statB"], wr=["statC"], inc="sA")
    em.op("vector", "reciprocal", rstdc[:, :], statC[:, :],
          rd=["statC"], wr=["rstdc"], inc="sV")
    for _ in range(2):
        em.op("vector", "tensor_tensor", statC[:, :], rstdc[:, :], rstdc[:, :],
              AL.mult, rd=["rstdc"], wr=["statC"], inc="sV")
        em.op("vector", "tensor_tensor", statC[:, :], statC[:, :], statB[:, :],
              AL.mult, rd=["statB"], wr=["statC"], inc="sV")
        em.op("vector", "tensor_scalar", statC[:, :], statC[:, :],
              -0.5, 1.5, AL.mult, AL.add, wr=["statC"], inc="sV")
        em.op("vector", "tensor_tensor", rstdc[:, :], rstdc[:, :], statC[:, :],
              AL.mult, rd=["statC"], wr=["rstdc"], inc="sV")

    em.op("tensor", "transpose", pb[2][0:16, 0:128], statA[:, :], ident[:],
          rd=["statA", "ident"], wr=[PB[2]], inc="sT")
    em.op("vector", "tensor_copy", negm16[:, :], pb[2][0:16, 0:128],
          rd=[PB[2]], wr=["negm16"], inc="sV")
    em.op("tensor", "matmul", pb[2][0:QT, 0:128], permm[:, :], negm16[:, :],
          start=True, stop=True, rd=["permm", "negm16"], wr=[PB[2]], inc="sT")
    em.op("vector", "tensor_copy", negmQ[:, :], pb[2][0:QT, 0:128],
          rd=[PB[2]], wr=["negmQ"], inc="sV")
    em.op("gpsimd", "dma_start",
          out=bass.AP(negmrow, 0, [[2048, 1], [1, 2048]]),
          in_=bass.AP(negm16, 0, [[128, 16], [1, 128]]),
          rd=["negm16"], wr=["negmrow"], inc="dNr", n=16)
    em.op("gpsimd", "dma_start",
          out=bass.AP(negmqrow, 0, [[1024, 1], [1, 1024]]),
          in_=bass.AP(negmQ, 0, [[128, 8], [1, 128]]),
          rd=["negmQ"], wr=["negmqrow"], inc="dNq", n=16)

    em.op("gpsimd", "iota", statC[0:64, 0:1], pattern=[[0, 1]], base=0,
          channel_multiplier=1, allow_small_or_imprecise_dtypes=True,
          wr=["statC"], inc="sG")
    em.op("gpsimd", "iota", statC[64:128, 0:1], pattern=[[0, 1]], base=0,
          channel_multiplier=1, allow_small_or_imprecise_dtypes=True,
          wr=["statC"], inc="sG")
    em.op("scalar", "activation", statC[:, 1:2], statC[:, 0:1], AF.Exp,
          scale=float(-np.log(10000.0) / 64.0),
          rd=["statC", "constv"], wr=["theta"], inc="sA")
    em.op("gpsimd", "iota", xsq[:, 0:2048], pattern=[[1, 2048]], base=0,
          channel_multiplier=0, allow_small_or_imprecise_dtypes=True,
          wr=["xsq0", "xsq1"], inc="sG")
    ang = csrS
    em.op("vector", "tensor_scalar", ang[0:64, 0:2048], xsq[0:64, 0:2048],
          statC[0:64, 1:2], None, AL.mult,
          rd=["xsq0", "xsq1", "theta", "statC"], wr=["csrS"], inc="sV")
    bb = xsq[0:64, 0:2048]
    for kk in range(8, -1, -1):
        m = TWO_PI * (2.0 ** kk)
        m_hi = float(np.float32(m))
        m_lo = float(m - np.float64(np.float32(m)))
        em.op("vector", "tensor_single_scalar", bb, ang[0:64, 0:2048], m_hi,
              AL.is_ge, rd=["csrS"], wr=["xsq0", "xsq1"], inc="sV")
        em.op("vector", "scalar_tensor_tensor", ang[0:64, 0:2048], bb, -m_hi,
              ang[0:64, 0:2048], AL.mult, AL.add,
              rd=["xsq0", "xsq1"], wr=["csrS"], inc="sV")
        em.op("vector", "scalar_tensor_tensor", ang[0:64, 0:2048], bb, -m_lo,
              ang[0:64, 0:2048], AL.mult, AL.add,
              rd=["xsq0", "xsq1"], wr=["csrS"], inc="sV")
    tp_hi = float(np.float32(TWO_PI))
    tp_lo = float(TWO_PI - np.float64(np.float32(TWO_PI)))
    em.op("vector", "tensor_single_scalar", bb, ang[0:64, 0:2048], PI, AL.is_ge,
          rd=["csrS"], wr=["xsq0", "xsq1"], inc="sV")
    em.op("vector", "scalar_tensor_tensor", ang[0:64, 0:2048], bb, -tp_hi,
          ang[0:64, 0:2048], AL.mult, AL.add,
          rd=["xsq0", "xsq1"], wr=["csrS"], inc="sV")
    em.op("vector", "scalar_tensor_tensor", ang[0:64, 0:2048], bb, -tp_lo,
          ang[0:64, 0:2048], AL.mult, AL.add,
          rd=["xsq0", "xsq1"], wr=["csrS"], inc="sV")
    em.op("scalar", "activation", xsq[0:64, 0:2048], ang[0:64, 0:2048], AF.Abs,
          rd=["csrS"], wr=["xsq0", "xsq1"], inc="sA")
    em.op("scalar", "activation", csrC[0:64, :], xsq[0:64, 0:2048], AF.Sin,
          scale=-1.0, bias=PI / 2.0,
          rd=["xsq0", "xsq1", "constv"], wr=["csrC"], inc="sA")
    em.op("scalar", "activation", ang[0:64, 0:2048], ang[0:64, 0:2048],
          AF.Sin, wr=["csrS"], inc="sA")
    em.op("gpsimd", "dma_start", out=csrC[64:128, :], in_=csrC[0:64, :],
          rd=["csrC"], wr=["csrC"], inc="dDu1", n=16)
    em.op("gpsimd", "dma_start", out=csrS[64:128, :], in_=csrS[0:64, :],
          rd=["csrS"], wr=["csrS"], inc="dDu2", n=16)
    em.op("tensor", "transpose", pb[3][0:16, 0:128], rstdc[:, :], ident[:],
          rd=["rstdc", "ident"], wr=[PB[3]], inc="sT")
    em.op("vector", "tensor_copy", rowst16[:, :], pb[3][0:16, 0:128],
          rd=[PB[3]], wr=["rowst16"], inc="sV")
    for hhh in range(2):
        em.op("gpsimd", "dma_start",
              out=bass.AP(drow, 0, [[1024, 1], [1, 1024]]),
              in_=bass.AP(rowst16, hhh * 8 * 128, [[128, 8], [1, 128]]),
              rd=["rowst16"], wr=["drow"], inc="dNq", n=16)
        for cc in range(2):
            em.op("tensor", "matmul", pb[3][:, 0:512], ones[0:1, 0:128],
                  drow[0:1, cc * 512:cc * 512 + 512], start=True, stop=True,
                  rd=["drow", "ones"], wr=[PB[3]], inc="sT")
            em.op("vector", "tensor_copy",
                  xsq[:, (hhh * 2 + cc) * 512:(hhh * 2 + cc) * 512 + 512],
                  pb[3][:, 0:512], rd=[PB[3]], wr=["xsq0", "xsq1"], inc="sV")
    em.op("vector", "tensor_tensor", csrC[:, :], csrC[:, :], xsq[:, 0:2048],
          AL.mult, rd=["xsq0", "xsq1"], wr=["csrC"], inc="sV")
    em.op("vector", "tensor_tensor", csrS[:, :], csrS[:, :], xsq[:, 0:2048],
          AL.mult, rd=["xsq0", "xsq1"], wr=["csrS"], inc="sV")
    em.op("vector", "tensor_copy", csrQC[:, 0:512],
          ap2(csrC, rg_v["qe0"], NTOK, 128, 512),
          rd=["csrC"], wr=["csrQC"], inc="sV")
    em.op("vector", "tensor_copy", csrQC[:, 512:1024],
          ap2(csrC, rg_v["qe1"], NTOK, 128, 512),
          rd=["csrC"], wr=["csrQC"], inc="sV")
    em.op("vector", "tensor_copy", csrQS[:, 0:512],
          ap2(csrS, rg_v["qe0"], NTOK, 128, 512),
          rd=["csrS"], wr=["csrQS"], inc="sV")
    em.op("vector", "tensor_copy", csrQS[:, 512:1024],
          ap2(csrS, rg_v["qe1"], NTOK, 128, 512),
          rd=["csrS"], wr=["csrQS"], inc="sV")

    def emit_colsums(g):
        for wi, (wsl, wn) in enumerate(((wk_sl, "wk_sl"), (wv_sl, "wv_sl"),
                                        (wq_sl, "wq_sl"))):
            em.pre("tensor", rd=[wn, "ones"], wr=[PB[6]])
            for dc in range(16):
                ev = em._emit_op("tensor", "matmul",
                                 (pb[6][0:1, 0:256], ones[:, 0:1],
                                  wsl[:, dc * 256:dc * 256 + 256]),
                                 dict(start=(dc == 0), stop=(dc == 15)),
                                 "sT" if dc == 15 else None, 1)
            em.post(ev, rd=[wn, "ones"], wr=[PB[6]])
            em.op("vector", "tensor_copy", crow[0:1, wi * 256:wi * 256 + 256],
                  pb[6][0:1, 0:256], rd=[PB[6]], wr=["crow"], inc="sV")

    def rope_emit(rawbuf, rawn, outbuf, outn, ntile, cstab, csn,
                  raw_base=0, out_base=0):
        rstep = rawbuf.shape[1]
        ostep = outbuf.shape[1]

        def A(boff):
            return bass.AP(rawbuf, raw_base + boff,
                           [[rstep, 128], [256, ntile], [1, 64]])

        def O(boff):
            return bass.AP(outbuf, out_base + boff,
                           [[ostep, 128], [256, ntile], [1, 64]])

        def CS(off):
            return bass.AP(cstab, off, [[cstab.shape[1], 128], [128, ntile], [1, 64]])
        for hh in range(2):
            b = hh * 128
            em.op("vector", "tensor_tensor", O(b), A(b), CS(0), AL.mult,
                  rd=[rawn, csn], wr=[outn], inc="sV")
            em.op("vector", "tensor_tensor", O(b + 64), A(b), CS(64), AL.mult,
                  rd=[rawn, csn], wr=[outn], inc="sV")
            em.op("vector", "tensor_tensor", A(b), A(b + 64), CS(64), AL.mult,
                  rd=[csn], wr=[rawn], inc="sV")
            em.op("vector", "tensor_tensor", O(b), O(b), A(b), AL.subtract,
                  rd=[rawn], wr=[outn], inc="sV")
            em.op("vector", "tensor_tensor", A(b), A(b + 64), CS(0), AL.mult,
                  rd=[csn], wr=[rawn], inc="sV")
            em.op("vector", "tensor_tensor", O(b + 64), O(b + 64), A(b),
                  AL.add, rd=[rawn], wr=[outn], inc="sV")

    def rope_from_psum(pU, pUn, pL, pLn, outU, outUn, outL, outLn,
                       oslice, cosT, cosn, sinT, sinn, tslice):
        em.op("vector", "tensor_tensor", outU[:, oslice], pU[:, 0:512],
              cosT[:, tslice], AL.mult, rd=[pUn, cosn], wr=[outUn], inc="sV")
        em.op("vector", "tensor_tensor", bcst[:, :], pL[:, 0:512],
              sinT[:, tslice], AL.mult, rd=[pLn, sinn], wr=["bcst"], inc="sV")
        em.op("vector", "tensor_tensor", outU[:, oslice], outU[:, oslice],
              bcst[:, :], AL.subtract, rd=["bcst"], wr=[outUn], inc="sV")
        em.op("vector", "tensor_tensor", bcst[:, :], pU[:, 0:512],
              sinT[:, tslice], AL.mult, rd=[pUn, sinn], wr=["bcst"], inc="sV")
        em.op("vector", "tensor_tensor", outL[:, oslice], pL[:, 0:512],
              cosT[:, tslice], AL.mult, rd=[pLn, cosn], wr=[outLn], inc="sV")
        em.op("vector", "tensor_tensor", outL[:, oslice], outL[:, oslice],
              bcst[:, :], AL.add, rd=["bcst"], wr=[outLn], inc="sV")

    def emit_qload(ld):
        tq, dh, sub = ld // 4, (ld // 2) % 2, ld % 2
        buf, bn = (qquadA, "qquadA") if ld % 2 == 0 else (qquadB, "qquadB")
        dcx0 = dh * 8 + sub * 4
        src_ap = bass.AP(dXQ, dcx0 * 128 * NQ + tq * 512,
                         [[NQ, 128], [128 * NQ, 4], [1, 512]])
        dst_ap = bass.AP(buf, 0, [[2048, 128], [512, 4], [1, 512]])
        em.op("gpsimd", "dma_start", out=dst_ap, in_=src_ap, wr=[bn],
              inc=("dQg" if ld % 2 == 0 else "dQgB"), n=16)

    def emit_group_proj(g):
        emit_qload(0)
        emit_qload(1)
        pidx = 0
        for quad in range(4):
            kb0, kb1 = (0, 1) if quad % 2 == 0 else (2, 3)
            for dchunk in range(16):
                par = pidx % 2
                buf = xq_bufs[par][:, :]
                bn = "xqc%d" % par
                em.op("sync", "dma_start", out=buf,
                      in_=ap2(dX, dchunk * 128 * NTOK + quad * 512, NTOK, 128, 512),
                      wr=[bn], inc=("dXA" if par == 0 else "dXB"), n=16)
                if dchunk == 0:
                    em.pre("tensor", rd=[bn, "wk_sl"], wr=[PB[kb0], PB[kb1]])
                    em._emit_op("tensor", "matmul",
                                (pb[kb0][:, 0:512],
                                 wk_sl[:, dchunk * 256:dchunk * 256 + 128],
                                 buf),
                                dict(start=True, stop=False), None, 1)
                    ev = em._emit_op("tensor", "matmul",
                                     (pb[kb1][:, 0:512],
                                      wk_sl[:, dchunk * 256 + 128:
                                            dchunk * 256 + 256], buf),
                                     dict(start=True, stop=False), "sT", 1)
                    em.post(ev, rd=[bn, "wk_sl"], wr=[PB[kb0], PB[kb1]])
                    em.pre("tensor", rd=[bn, "wv_sl"],
                           wr=[PB[4], PB[5], PB[6], PB[7]])
                    for tt in range(4):
                        ev = em._emit_op(
                            "tensor", "matmul",
                            (pb[4 + tt][:, 0:256],
                             buf[:, tt * 128:tt * 128 + 128],
                             wv_sl[:, dchunk * 256:dchunk * 256 + 256]),
                            dict(start=True, stop=False),
                            "sT" if tt == 3 else None, 1)
                    em.post(ev, rd=[bn, "wv_sl"],
                            wr=[PB[4], PB[5], PB[6], PB[7]])
                    pidx += 1
                    continue
                em.pre("tensor", rd=[bn, "wk_sl", "wv_sl"], wr=[])
                em._emit_op("tensor", "matmul",
                            (pb[kb0][:, 0:512],
                             wk_sl[:, dchunk * 256:dchunk * 256 + 128], buf),
                            dict(start=False, stop=False), None, 1)
                em._emit_op("tensor", "matmul",
                            (pb[kb1][:, 0:512],
                             wk_sl[:, dchunk * 256 + 128:dchunk * 256 + 256],
                             buf),
                            dict(start=False, stop=False), None, 1)
                for tt in range(4):
                    ev = em._emit_op(
                        "tensor", "matmul",
                        (pb[4 + tt][:, 0:256],
                         buf[:, tt * 128:tt * 128 + 128],
                         wv_sl[:, dchunk * 256:dchunk * 256 + 256]),
                        dict(start=False, stop=False),
                        "sT" if tt == 3 else None, 1)
                em.post(ev, rd=[bn, "wk_sl", "wv_sl"],
                        wr=[PB[kb0], PB[kb1], PB[4], PB[5], PB[6], PB[7]])
                pidx += 1
            em.pre("tensor", rd=["negmrow", "crow"], wr=[])
            em._emit_op("tensor", "matmul",
                        (pb[kb0][:, 0:512], crow[0:1, 0:128],
                         negmrow[0:1, quad * 512:quad * 512 + 512]),
                        dict(start=False, stop=True), None, 1)
            em._emit_op("tensor", "matmul",
                        (pb[kb1][:, 0:512], crow[0:1, 128:256],
                         negmrow[0:1, quad * 512:quad * 512 + 512]),
                        dict(start=False, stop=True), None, 1)
            for tt in range(4):
                tg = quad * 4 + tt
                ev = em._emit_op("tensor", "matmul",
                                 (pb[4 + tt][:, 0:256],
                                  negmrow[0:1, tg * 128:tg * 128 + 128],
                                  crow[0:1, 256:512]),
                                 dict(start=False, stop=True),
                                 "sT" if tt == 3 else None, 1)
            em.post(ev, rd=["negmrow", "crow"],
                    wr=[PB[kb0], PB[kb1], PB[4], PB[5], PB[6], PB[7]])
            qs = slice(quad * 512, quad * 512 + 512)
            rope_from_psum(pb[kb0], PB[kb0], pb[kb1], PB[kb1], kropU, "kropU",
                           kropL, "kropL", qs, csrC, "csrC", csrS, "csrS", qs)
            for tt in range(4):
                tg = quad * 4 + tt
                em.op("scalar", "mul",
                      vrowsA[:, tg * 128:tg * 128 + 128],
                      pb[4 + tt][:, 0:128], rstdc[:, tg:tg + 1],
                      rd=[PB[4 + tt], "rstdc"], wr=["vrowsA"], inc="sA")
                em.op("scalar", "mul",
                      vrowsB[:, tg * 128:tg * 128 + 128],
                      pb[4 + tt][:, 128:256], rstdc[:, tg:tg + 1],
                      rd=[PB[4 + tt], "rstdc"], wr=["vrowsB"], inc="sA")
        for tq in range(2):
            qp0, qp1 = (0, 1) if tq == 0 else (2, 3)
            for dh in range(2):
                for sub in range(2):
                    ld = tq * 4 + dh * 2 + sub
                    buf, bn = ((qquadA, "qquadA") if ld % 2 == 0
                               else (qquadB, "qquadB"))
                    for dchunk in range(4):
                        dcx = dh * 8 + sub * 4 + dchunk
                        em.pre("tensor", rd=[bn, "wq_sl"],
                               wr=[PB[qp0], PB[qp1]] if dcx == 0 else [])
                        em._emit_op(
                            "tensor", "matmul",
                            (pb[qp0][:, 0:512],
                             wq_sl[:, dcx * 256:dcx * 256 + 128],
                             buf[:, dchunk * 512:dchunk * 512 + 512]),
                            dict(start=(dcx == 0), stop=False), None, 1)
                        ev = em._emit_op(
                            "tensor", "matmul",
                            (pb[qp1][:, 0:512],
                             wq_sl[:, dcx * 256 + 128:dcx * 256 + 256],
                             buf[:, dchunk * 512:dchunk * 512 + 512]),
                            dict(start=(dcx == 0), stop=False), "sT", 1)
                        em.post(ev, rd=[bn, "wq_sl"], wr=[PB[qp0], PB[qp1]])
                    if ld + 2 < 8:
                        emit_qload(ld + 2)
            em.pre("tensor", rd=["negmqrow", "crow"], wr=[])
            em._emit_op("tensor", "matmul",
                        (pb[qp0][:, 0:512], crow[0:1, 512:640],
                         negmqrow[0:1, tq * 512:tq * 512 + 512]),
                        dict(start=False, stop=True), None, 1)
            ev = em._emit_op("tensor", "matmul",
                             (pb[qp1][:, 0:512], crow[0:1, 640:768],
                              negmqrow[0:1, tq * 512:tq * 512 + 512]),
                             dict(start=False, stop=True), "sT", 1)
            em.post(ev, rd=["negmqrow", "crow"], wr=[PB[qp0], PB[qp1]])
            qs = slice(tq * 512, tq * 512 + 512)
            rope_from_psum(pb[qp0], PB[qp0], pb[qp1], PB[qp1], qropU, "qropU",
                           qropL, "qropL", qs, csrQC, "csrQC", csrQS, "csrQS",
                           qs)


    def emit_transposes(src, srcn, src_off, dst, dstn, ntile):
        for grp in range(0, ntile, 4):
            em.pre("tensor", rd=[srcn, "ident"], wr=[PB[0]])
            for j in range(4):
                t = grp + j
                ev = em._emit_op("tensor", "transpose",
                                 (pb[0][:, j * 128:j * 128 + 128],
                                  src[:, src_off + t * 256:src_off + t * 256 + 128],
                                  ident[:]), {}, "sT" if j == 3 else None, 1)
            em.post(ev, rd=[srcn, "ident"], wr=[PB[0]])
            em.op("scalar", "copy", dst[:, grp * 128:grp * 128 + 512],
                  pb[0][:, 0:512], rd=[PB[0]], wr=[dstn], inc="sA")

    def emit_attention_head(g, hh):
        h = 2 * g + hh
        vrows, vn = (vrowsA, "vrowsA") if hh == 0 else (vrowsB, "vrowsB")
        em.op("gpsimd", "dma_start", out=kT[0:64, :],
              in_=kropU[hh * 64:hh * 64 + 64, :],
              rd=["kropU"], wr=["kT"], inc="dRcK", n=16)
        em.op("gpsimd", "dma_start", out=kT[64:128, :],
              in_=kropL[hh * 64:hh * 64 + 64, :],
              rd=["kropL"], wr=["kT"], inc="dRcK", n=16)
        em.op("gpsimd", "dma_start", out=qTt[0:64, :],
              in_=qropU[hh * 64:hh * 64 + 64, :],
              rd=["qropU"], wr=["qT"], inc="dRcQ", n=16)
        em.op("gpsimd", "dma_start", out=qTt[64:128, :],
              in_=qropL[hh * 64:hh * 64 + 64, :],
              rd=["qropL"], wr=["qT"], inc="dRcQ", n=16)
        iters = []
        for j in range(4):
            iters.append((0, j, None, (0, j), j == 0, False))
        for j in range(4):
            iters.append((0, 4 + j, 1, (1, j), False, j == 3))
        for j in range(8):
            iters.append((1, j, None, None, j == 0, False))
        for j in range(4):
            iters.append((1, 8 + j, None, (1, j), False, False))
        for j in range(4):
            iters.append((1, 12 + j, 0, (0, j), False, j == 3))
        SB = (pb[0], pb[1], pb[4])
        SBN = (PB[0], PB[1], PB[4])

        def s_mm(i):
            qb, kc = iters[i][0], iters[i][1]
            em.op("tensor", "matmul", SB[i % 3][:, 0:512],
                  kT[:, kc * 128:kc * 128 + 128],
                  qTt[:, qb * 512:qb * 512 + 512],
                  start=True, stop=True, rd=["kT", "qT"], wr=[SBN[i % 3]],
                  inc="sT")

        s_mm(0)
        for i, (qb, kc, fcol, mt, ostart, ostop) in enumerate(iters):
            if i + 1 < len(iters):
                s_mm(i + 1)
            lhs_v = vrows[:, kc * 128:kc * 128 + 128]
            ob, obn = (pb[2], PB[2]) if qb == 0 else (pb[3], PB[3])
            sbank, sbn = SB[i % 3], SBN[i % 3]
            par = i % 2
            s2s = s2b[:, par * 512:par * 512 + 512]
            s2n = "s2b%d" % par
            s4s = s4b[:, 0:512] if par == 0 else xqcA[:, 0:512]
            s4n = "s4b0" if par == 0 else "xqc0"
            if fcol is None:
                em.op("scalar", "activation", s2s, sbank[:, 0:512],
                      AF.Square, rd=[sbn], wr=[s2n], inc="sA")
            else:
                em.op("scalar", "activation", s2s, sbank[:, 0:512],
                      AF.Square, scale=selt[:, fcol:fcol + 1],
                      rd=[sbn, "selt"], wr=[s2n], inc="sA")
            em.op("vector", "tensor_tensor", s4s, s2s, s2s, AL.mult,
                  rd=[s2n], wr=[s4n], inc="sV")
            if mt is not None:
                fam, j = mt
                em.op("vector", "tensor_tensor", s4s, s4s,
                      mtil[:, fam * 2048 + j * 512:fam * 2048 + j * 512 + 512],
                      AL.mult, rd=["mtil"], wr=[s4n], inc="sV")
            if ostart:
                em.op("gpsimd", "tensor_copy",
                      tac[:, qb * 512:qb * 512 + 512], s4s,
                      rd=[s4n], wr=["T%d" % qb], inc="sG")
            else:
                em.op("gpsimd", "tensor_tensor",
                      tac[:, qb * 512:qb * 512 + 512],
                      tac[:, qb * 512:qb * 512 + 512], s4s, AL.add,
                      rd=[s4n], wr=["T%d" % qb], inc="sG")
            em.op("tensor", "matmul", ob[:, 0:512], lhs_v, s4s,
                  start=ostart, stop=ostop, rd=[vn, s4n], wr=[obn],
                  inc="sT")
            if i == 7:
                em.op("tensor", "matmul", pb[5][0:1, 0:512], ones[:, 0:1],
                      tac[:, 0:512], start=True, stop=True,
                      rd=["T0", "ones"], wr=[PB[5]], inc="sT")
                em.op("vector", "tensor_single_scalar", drow[0:1, 0:512],
                      pb[5][0:1, 0:512], 1e-6, AL.max, rd=[PB[5]],
                      wr=["drow"], inc="sV")
            if i == 8:
                em.op("vector", "reciprocal", drow[0:1, 512:768],
                      drow[0:1, 0:256], wr=["drow"], inc="sV")
            if i == 10:
                em.op("vector", "reciprocal", drow[0:1, 768:1024],
                      drow[0:1, 256:512], wr=["drow"], inc="sV")
            if i == 13:
                em.op("tensor", "matmul", pb[5][:, 0:512], ones[0:1, 0:128],
                      drow[0:1, 512:1024], start=True, stop=True,
                      rd=["drow", "ones"], wr=[PB[5]], inc="sT")
                em.op("scalar", "copy", bcst[:, 0:512], pb[5][:, 0:512],
                      rd=[PB[5]], wr=["bcst"], inc="sA")
                em.op("vector", "tensor_tensor", obuf[:, 0:512],
                      pb[2][:, 0:512], bcst[:, 0:512], AL.mult,
                      rd=[PB[2], "bcst"], wr=["obuf"], inc="sV")
        em.op("tensor", "matmul", pb[6][0:1, 0:512], ones[:, 0:1],
              tac[:, 512:1024], start=True, stop=True,
              rd=["T1", "ones"], wr=[PB[6]], inc="sT")
        em.op("vector", "tensor_single_scalar", drow[0:1, 0:512],
              pb[6][0:1, 0:512], 1e-6, AL.max, rd=[PB[6]], wr=["drow"],
              inc="sV")
        em.op("vector", "reciprocal", drow[0:1, 512:1024],
              drow[0:1, 0:512], wr=["drow"], inc="sV")
        em.op("tensor", "matmul", pb[6][:, 0:512], ones[0:1, 0:128],
              drow[0:1, 512:1024], start=True, stop=True,
              rd=["drow", "ones"], wr=[PB[6]], inc="sT")
        em.op("scalar", "copy", bcst[:, 0:512], pb[6][:, 0:512],
              rd=[PB[6]], wr=["bcst"], inc="sA")
        em.op("vector", "tensor_tensor", obuf[:, 512:1024],
              pb[3][:, 0:512], bcst[:, 0:512], AL.mult,
              rd=[PB[3], "bcst"], wr=["obuf"], inc="sV")
        em.op("gpsimd", "dma_start",
              out=ap2(dOS, h * 128 * NQ, NQ, 128, 1024), in_=obuf[:, 0:1024],
              rd=["obuf"], inc="dOS", n=16)

    sc1b.close()

    for g in range(NH // 2):
        if g > 0:
            emit_wslice_load(g)
        emit_colsums(g)
        emit_group_proj(g)
        for hh in range(2):
            emit_attention_head(g, hh)

    em.barrier(ESEM)
    sc1.close()

    kintT = sb("kintT", [128, NT * 1024], BF16)
    recs = sb("recs", [128, QT])
    rowstage = sb("rowstage", [8, 128])
    recrow = sb("recrow", [1, 1024])
    recb = sb("recb", [128, 1024])

    sc2b = contextlib.ExitStack()
    sc2a = contextlib.ExitStack()
    x2f_ = [sb("x2f%d" % p, [128, 2048], scope=sc2b) for p in range(2)]
    lnb_ = [sb("lnb%d" % p, [128, 2048], scope=sc2b) for p in range(2)]
    qb2_ = [sb("qb2%d" % p, [128, 2048], scope=sc2b) for p in range(2)]
    qb3_ = [sb("qb3%d" % p, [128, 2048], scope=sc2b) for p in range(2)]
    kib_ = [sb("kib%d" % p, [128, 2048], BF16, scope=sc2b) for p in range(2)]
    col_ = [sb("colq%d" % p, [128, 8], scope=sc2b) for p in range(2)]
    woT_sb = sb("woT_sb", [128, 16 * 1024], scope=sc2a)
    oLj = sb("oLj", [128, 2 * 2048], scope=sc2a)
    xrt = sb("xrt", [128, 1024], scope=sc2a)
    x2t = sb("x2t", [128, 1024], scope=sc2a)

    def quant_gen(jt):
        qp = jt % 2
        x2f, lnb, qb2, qb3, kib, col = (x2f_[qp], lnb_[qp], qb2_[qp],
                                        qb3_[qp], kib_[qp], col_[qp])
        sx2f, slnb, sqb2, sqb3, skib = ("x2f%d" % qp, "lnb%d" % qp,
                                        "qb2%d" % qp, "qb3%d" % qp,
                                        "kib%d" % qp)
        scol, scol2 = "colq%d" % qp, "colq2%d" % qp
        if qp == 0:
            em.op("sync", "dma_start", out=x2f[:, 0:2048],
                  in_=ap2(dX2, jt * 128 * D, D, 128, 2048),
                  rd=["x2s%d_0" % jt, "x2s%d_1" % jt], wr=[sx2f],
                  inc="dX2f", n=16)
        else:
            em.op("sync", "dma_start", out=x2f[:, 0:1024],
                  in_=ap2(dX2, jt * 128 * D, D, 128, 1024),
                  rd=["x2s%d_0" % jt], wr=[sx2f], inc="dX2fB", n=16)
            em.op("scalar", "copy", x2f[:, 1024:2048], x2t[:, 0:1024],
                  rd=["x2t"], wr=[sx2f], inc="sA")
        yield
        em.op("vector", "reduce_sum", col[:, 0:1], x2f[:, :], AX.X,
              rd=[sx2f], wr=[scol], inc="sV")
        yield
        em.op("scalar", "activation", qb2[:, :], x2f[:, :], AF.Square,
              rd=[sx2f], wr=[sqb2], inc="sA")
        yield
        em.op("vector", "reduce_sum", col[:, 1:2], qb2[:, :], AX.X,
              rd=[sqb2], wr=[scol], inc="sV")
        yield
        em.op("vector", "tensor_scalar", col[:, 0:1], col[:, 0:1],
              1.0 / D, None, AL.mult, wr=[scol], inc="sV")
        yield
        em.op("vector", "tensor_scalar", col[:, 1:2], col[:, 1:2],
              1.0 / D, None, AL.mult, wr=[scol], inc="sV")
        yield
        em.op("vector", "tensor_tensor", col[:, 2:3], col[:, 0:1],
              col[:, 0:1], AL.mult, wr=[scol], inc="sV")
        yield
        em.op("vector", "tensor_tensor", col[:, 1:2], col[:, 1:2],
              col[:, 2:3], AL.subtract, wr=[scol], inc="sV")
        yield
        em.op("vector", "tensor_scalar", col[:, 2:3], col[:, 1:2],
              1e-5, None, AL.add, wr=[scol], inc="sV")
        yield
        em.op("vector", "reciprocal", col[:, 3:4], col[:, 2:3],
              wr=[scol], inc="sV")
        yield
        em.op("vector", "tensor_tensor", col[:, 4:5], col[:, 1:2],
              col[:, 3:4], AL.mult, wr=[scol], inc="sV")
        yield
        em.op("vector", "tensor_scalar", col[:, 4:5], col[:, 4:5],
              1e-8, None, AL.add, wr=[scol], inc="sV")
        yield
        em.op("vector", "tensor_tensor", col[:, 4:5], col[:, 4:5],
              col[:, 2:3], AL.mult, wr=[scol], inc="sV")
        yield
        em.op("scalar", "activation", col[:, 5:6], col[:, 4:5], AF.Sqrt,
              rd=[scol], wr=[scol2], inc="sA")
        yield
        em.op("vector", "reciprocal", col[:, 3:4], col[:, 5:6],
              rd=[scol2], wr=[scol], inc="sV")
        yield
        for _ in range(2):
            em.op("vector", "tensor_tensor", col[:, 6:7], col[:, 3:4],
                  col[:, 3:4], AL.mult, wr=[scol], inc="sV")
            yield
            em.op("vector", "tensor_tensor", col[:, 6:7], col[:, 6:7],
                  col[:, 4:5], AL.mult, wr=[scol], inc="sV")
            yield
            em.op("vector", "tensor_scalar", col[:, 6:7], col[:, 6:7],
                  -0.5, 1.5, AL.mult, AL.add, wr=[scol], inc="sV")
            yield
            em.op("vector", "tensor_tensor", col[:, 3:4], col[:, 3:4],
                  col[:, 6:7], AL.mult, wr=[scol], inc="sV")
            yield
        be, bs = "vector", "sV"
        em.op(be, "tensor_scalar", lnb[:, :], x2f[:, :],
              col[:, 0:1], col[:, 3:4], AL.subtract, AL.mult,
              rd=[sx2f, scol], wr=[slnb], inc=bs)
        yield
        em.op("vector", "reduce_max", col[:, 5:6], lnb[:, :], AX.X,
              apply_absolute_value=True, rd=[slnb], wr=[scol], inc="sV")
        yield
        em.op("vector", "reciprocal", col[:, 6:7], col[:, 5:6],
              wr=[scol], inc="sV")
        yield
        em.op("vector", "tensor_scalar", col[:, 6:7], col[:, 6:7],
              127.0, None, AL.mult, wr=[scol], inc="sV")
        yield
        em.op("vector", "reciprocal", recs[:, jt:jt + 1], col[:, 6:7],
              rd=[scol], wr=["recs"], inc="sV")
        yield
        em.op("scalar", "activation", qb2[:, :], lnb[:, :], AF.Abs,
              scale=col[:, 6:7], rd=[slnb, scol], wr=[sqb2], inc="sA")
        yield
        em.op("scalar", "activation", qb3[:, :], qb2[:, :], AF.Copy,
              scale=-1.0, bias=63.5,
              rd=[sqb2, "constv"], wr=[sqb3], inc="sA")
        yield
        for m in (64.0, 32.0, 16.0, 8.0, 4.0, 2.0, 1.0):
            em.op("scalar", "activation", x2f[:, :], qb3[:, :], AF.Sign,
                  rd=[sqb3], wr=[sx2f], inc="sA")
            yield
            em.op(be, "scalar_tensor_tensor", qb3[:, :], x2f[:, :],
                  -m / 2.0, qb3[:, :], AL.mult, AL.add,
                  rd=[sx2f], wr=[sqb3], inc=bs)
            yield
        em.op(be, "scalar_tensor_tensor", qb2[:, :], qb3[:, :],
              1.0, qb2[:, :], AL.mult, AL.add, rd=[sqb3], wr=[sqb2], inc=bs)
        yield
        em.op("scalar", "activation", qb3[:, :], lnb[:, :], AF.Sign,
              rd=[slnb], wr=[sqb3], inc="sA")
        yield
        em.op(be, "tensor_tensor", kib[:, :], qb2[:, :], qb3[:, :],
              AL.mult, rd=[sqb2, sqb3], wr=[skib], inc=bs)
        yield


    def emit_quant_pair(j0, j1):
        gens = [quant_gen(j0), quant_gen(j1)]
        while gens:
            nxt = []
            for g in gens:
                try:
                    next(g)
                    nxt.append(g)
                except StopIteration:
                    pass
            gens = nxt

    def emit_quant_tr(jt):
        qp = jt % 2
        kib, skib = kib_[qp], "kib%d" % qp
        for grp in range(0, 16, 4):
            em.pre("tensor", rd=[skib, "identB"], wr=[PB[2]])
            for j in range(4):
                dcx = grp + j
                ev = em._emit_op("tensor", "transpose",
                                 (pb[2][:, j * 64:j * 64 + 64].bitcast(BF16),
                                  kib[:, dcx * 128:dcx * 128 + 128], identB[:]),
                                 {}, "sT" if j == 3 else None, 1)
            em.post(ev, rd=[skib, "identB"], wr=[PB[2]])
            em.op("scalar", "copy",
                  bass.AP(kintT, grp * 1024 + jt * 128,
                          [[NT * 1024, 128], [1024, 4], [1, 128]]),
                  bass.AP(pb[2].bitcast(BF16), 0,
                          [[1024, 128], [128, 4], [1, 128]]),
                  rd=[PB[2]], wr=["kintT"], inc="sA")

    qpend = []

    def qpump(n):
        cnt = 0
        idx = 0
        while cnt < n and qpend:
            if idx >= len(qpend):
                idx = 0
            try:
                next(qpend[idx][1])
                idx += 1
            except StopIteration:
                qpend.pop(idx)
            cnt += 1

    def qflush_jt(j):
        while any(t <= j for t, _ in qpend):
            qpump(1)

    for dh in range(2):
        for p in range(4):
            srcp = bass.AP(dWO, dh * 1024 + p * 4 * 128 * D,
                           [[D, 128], [128 * D, 4], [1, 1024]])
            dstp = bass.AP(woT_sb, p * 4 * 1024,
                           [[16 * 1024, 128], [1024, 4], [1, 1024]])
            em.op("sync", "dma_start", out=dstp, in_=srcp, wr=["woT%d" % p],
                  inc=("dWo", "dWoB", "dWoC", "dWoD")[p], n=16)
        for jt in range(QT):
            par = jt % 2
            ol = oLj[:, par * 2048:par * 2048 + 2048]
            oln = "oLj%d" % par
            em.op("sync", "dma_start", out=ol,
                  in_=bass.AP(dOS, jt * 128, [[NQ, 128], [128 * NQ, 16], [1, 128]]),
                  wr=[oln], inc=("dO0" if par == 0 else "dO1"), n=16)
            em.op("sync", "dma_start", out=xrt[:, 0:1024],
                  in_=ap2(dXR, jt * 128 * D + dh * 1024, D, 128, 1024),
                  wr=["xrt"], inc="dXr", n=16)
            qpump(8)
            for dc in range(2):
                bank, bkn = pb[dc], PB[dc]
                for p in range(4):
                    em.pre("tensor", rd=[oln, "woT%d" % p],
                           wr=[bkn] if p == 0 else [])
                    for hq in range(4):
                        hx = p * 4 + hq
                        ev = em._emit_op(
                            "tensor", "matmul",
                            (bank[:, 0:512], ol[:, hx * 128:hx * 128 + 128],
                             woT_sb[:, hx * 1024 + dc * 512:
                                    hx * 1024 + dc * 512 + 512]),
                            dict(start=(hx == 0), stop=False),
                            "sT" if hq == 3 else None, 1)
                    em.post(ev, rd=[oln, "woT%d" % p], wr=[bkn])
                em.pre("tensor", rd=["xrt", "ident"], wr=[])
                ev = em._emit_op("tensor", "matmul",
                                 (bank[:, 0:512], ident[:],
                                  xrt[:, dc * 512:dc * 512 + 512]),
                                 dict(start=False, stop=True), "sT", 1)
                em.post(ev, rd=["xrt", "ident"], wr=[bkn])
                em.op("scalar", "copy", x2t[:, dc * 512:dc * 512 + 512],
                      bank[:, 0:512], rd=[bkn], wr=["x2t"], inc="sA")
                qpump(4)
            em.op("gpsimd", "dma_start",
                  out=ap2(dX2, jt * 128 * D + dh * 1024, D, 128, 1024),
                  in_=x2t[:, 0:1024], rd=["x2t"],
                  wr=["x2s%d_%d" % (jt, dh)], inc="dX2s", n=16)
            if dh == 1:
                if jt % 2 == 1:
                    if jt >= 3:
                        qflush_jt(jt - 2)
                        emit_quant_tr(jt - 3)
                        emit_quant_tr(jt - 2)
                    qpend.append((jt - 1, quant_gen(jt - 1)))
                    qpend.append((jt, quant_gen(jt)))
                qpump(16)
    qflush_jt(QT - 1)
    emit_quant_tr(QT - 2)
    emit_quant_tr(QT - 1)

    em.op("tensor", "transpose", pb[2][0:8, 0:128], recs[:, :], ident[:],
          rd=["recs", "ident"], wr=[PB[2]], inc="sT")
    em.op("vector", "tensor_copy", rowstage[:, :], pb[2][0:8, 0:128],
          rd=[PB[2]], wr=["rowstage"], inc="sV")
    em.op("gpsimd", "dma_start",
          out=bass.AP(recrow, 0, [[1024, 1], [1, 1024]]),
          in_=bass.AP(rowstage, 0, [[128, 8], [1, 128]]),
          rd=["rowstage"], wr=["recrow"], inc="dNr", n=16)
    em.op("tensor", "matmul", pb[2][:, 0:512], ones[0:1, 0:128],
          recrow[0:1, 0:512], start=True, stop=True,
          rd=["recrow", "ones"], wr=[PB[2]], inc="sT")
    em.op("scalar", "copy", recb[:, 0:512], pb[2][:, 0:512],
          rd=[PB[2]], wr=["recb"], inc="sA")
    em.op("tensor", "matmul", pb[2][:, 0:512], ones[0:1, 0:128],
          recrow[0:1, 512:1024], start=True, stop=True,
          rd=["recrow", "ones"], wr=[PB[2]], inc="sT")
    em.op("scalar", "copy", recb[:, 512:1024], pb[2][:, 0:512],
          rd=[PB[2]], wr=["recb"], inc="sA")

    def soft_barrier(sems):
        vals = {s: em.cnt[s] for s in sems}
        for eng in ENGINES:
            for s, v in vals.items():
                em.wait(eng, s, v)

    soft_barrier(("sT", "sA", "sG", "dWo", "dWoB", "dWoC", "dWoD",
                  "dO0", "dO1", "dXr", "dX2s"))
    sc2a.close()

    sc4 = contextlib.ExitStack()
    wstg = [sb("wstg%d" % p, [128, 2048], scope=sc4) for p in range(2)]
    sgw = [sb("sgw%d" % p, [128, NT * 256], BF16, scope=sc4) for p in range(2)]
    suw = [sb("suw%d" % p, [128, NT * 256], BF16, scope=sc4) for p in range(2)]
    sdw = [sb("sdw%d" % p, [128, 2 * 2048], BF16, scope=sc4) for p in range(2)]
    gbuf = sb("gbuf", [128, 2048], scope=sc4)
    ubuf = sb("ubuf", [128, 2048], scope=sc4)
    pTl = [sb("pTl%d" % p, [128, 2 * 1024], BF16, scope=sc4) for p in range(2)]

    witems = []

    def enq_gu(ch):
        p = ch % 2
        for (dram, dstbuf, dn) in ((dWG, sgw[p], "sgw%d" % p),
                                   (dWU, suw[p], "suw%d" % p)):
            for half in range(2):
                witems.append((
                    bass.AP(dram, ch * 256 + half * 8 * 128 * HF,
                            [[HF, 128], [128 * HF, 8], [1, 256]]),
                    dstbuf, half * 2048, dn))

    def enq_d(ch):
        p = ch % 2
        for hs in range(2):
            witems.append((
                ap2(dWD, (ch * 256 + hs * 128) * D, D, 128, 2048),
                sdw[p], hs * 2048, "sdw%d" % p))

    wst = {"loaded": 0, "signed": 0}

    def wpump(n, flush=False):
        while True:
            did = False
            if n > 0 or flush:
                if wst["loaded"] < len(witems) and                         wst["loaded"] - wst["signed"] < 2:
                    i = wst["loaded"]
                    srcap = witems[i][0]
                    sl = i % 2
                    em.op("sync", "dma_start", out=wstg[sl][:, :], in_=srcap,
                          wr=["wstg%d" % sl], inc="dWg%d" % sl, n=16)
                    wst["loaded"] += 1
                    did = True
                lead = 0 if wst["loaded"] == len(witems) else 1
                if wst["signed"] < wst["loaded"] - (0 if flush else lead):
                    i = wst["signed"]
                    _, dstbuf, doff, dn = witems[i]
                    sl = i % 2
                    em.op("scalar", "activation",
                          dstbuf[:, doff:doff + 2048], wstg[sl][:, :],
                          AF.Sign, rd=["wstg%d" % sl], wr=[dn], inc="sA")
                    wst["signed"] += 1
                    did = True
            if not did:
                return
            n -= 1
            if n <= 0 and not flush:
                return
            if flush and wst["signed"] >= len(witems):
                return

    enq_gu(0)
    enq_d(0)
    enq_d(1)
    wpump(0, flush=True)

    y2t = [(x2f_[0], "x2f0"), (lnb_[0], "lnb0"), (qb2_[0], "qb20"),
           (qb3_[0], "qb30"), (x2f_[1], "x2f1"), (lnb_[1], "lnb1"),
           (qb2_[1], "qb21"), (qb3_[1], "qb31")]
    pTh = [kib_[0], kib_[1]]
    PTHN = ["kib0", "kib1"]

    for ch in range(NCH):
        p = ch % 2
        gn, un, dn = "sgw%d" % p, "suw%d" % p, "sdw%d" % p
        if ch + 1 < NCH:
            enq_gu(ch + 1)
        if ch % 2 == 0 and ch >= 2:
            enq_d(ch)
            enq_d(ch + 1)
        for tokh in range(2):
            for hh in range(2):
                bg, bgn = pb[hh], PB[hh]
                bu, bun = pb[2 + hh], PB[2 + hh]
                em.pre("tensor", rd=["kintT", gn, un], wr=[bgn, bun])
                for dcx in range(16):
                    em._emit_op(
                        "tensor", "matmul",
                        (bg[:, 0:512],
                         sgw[p][:, dcx * 256 + hh * 128:dcx * 256 + hh * 128 + 128],
                         kintT[:, dcx * 1024 + tokh * 512:dcx * 1024 + tokh * 512 + 512]),
                        dict(start=(dcx == 0), stop=(dcx == 15)), None, 1)
                    ev = em._emit_op(
                        "tensor", "matmul",
                        (bu[:, 0:512],
                         suw[p][:, dcx * 256 + hh * 128:dcx * 256 + hh * 128 + 128],
                         kintT[:, dcx * 1024 + tokh * 512:dcx * 1024 + tokh * 512 + 512]),
                        dict(start=(dcx == 0), stop=(dcx == 15)),
                        "sT" if dcx == 15 else None, 1)
                em.post(ev, rd=["kintT", gn, un], wr=[bgn, bun])
                pos = slice(hh * 1024 + tokh * 512, hh * 1024 + tokh * 512 + 512)
                em.op("vector", "tensor_tensor", ubuf[:, pos], bu[:, 0:512],
                      recb[:, tokh * 512:tokh * 512 + 512], AL.mult,
                      rd=[bun, "recb"], wr=["ubuf"], inc="sV")
                em.op("vector", "tensor_tensor", gbuf[:, pos], bg[:, 0:512],
                      recb[:, tokh * 512:tokh * 512 + 512], AL.mult,
                      rd=[bgn, "recb"], wr=["gbuf"], inc="sV")
                em.op("scalar", "activation", gbuf[:, pos], gbuf[:, pos],
                      AF.Silu, wr=["gbuf"], inc="sA")
                em.op("vector", "tensor_tensor", gbuf[:, pos], gbuf[:, pos],
                      ubuf[:, pos], AL.mult, rd=["ubuf"], wr=["gbuf"],
                      inc="sV")
                wpump(2)
                hn, ln_ = PTHN[p], "pTl%d" % p
                em.op("vector", "tensor_copy", pTh[p][:, pos], gbuf[:, pos],
                      rd=["gbuf"], wr=[hn], inc="sV")
                em.op("vector", "tensor_tensor", pTl[p][:, pos], gbuf[:, pos],
                      pTh[p][:, pos], AL.subtract, rd=["gbuf", hn], wr=[ln_],
                      inc="sV")
        if ch % 2 == 1:
            pair = ch // 2
            for jt in range(QT):
                for dc in range(4):
                    bi = 4 + ((jt * 4 + dc) % 4)
                    bank, bkn = pb[bi], PB[bi]
                    first = True
                    for cp in range(2):
                        for hs in range(2):
                            for pt, ptn in ((pTh[cp], PTHN[cp]),
                                            (pTl[cp], "pTl%d" % cp)):
                                last = (cp == 1 and hs == 1 and
                                        ptn.startswith("pTl"))
                                em.pre("tensor", rd=[ptn, "sdw%d" % cp],
                                       wr=[bkn] if first else [])
                                ev = em._emit_op(
                                    "tensor", "matmul",
                                    (bank[:, 0:512],
                                     pt[:, hs * 1024 + jt * 128:hs * 1024 + jt * 128 + 128],
                                     sdw[cp][:, hs * 2048 + dc * 512:hs * 2048 + dc * 512 + 512]),
                                    dict(start=first, stop=last),
                                    "sT" if last else None, 1)
                                if last:
                                    em.post(ev, rd=[PTHN[0], "pTl0", PTHN[1],
                                                    "pTl1", "sdw0", "sdw1"],
                                            wr=[bkn])
                                first = False
                    if dc == 0:
                        wpump(1)
                    y2b, y2n = y2t[jt]
                    dst = y2b[:, dc * 512:dc * 512 + 512]
                    if pair == 0:
                        em.op("vector", "tensor_copy", dst, bank[:, 0:512],
                              rd=[bkn], wr=[y2n], inc="sV")
                    else:
                        em.op("vector", "tensor_tensor", dst, dst,
                              bank[:, 0:512], AL.add, rd=[bkn], wr=[y2n],
                              inc="sV")
    for jt in range(QT):
        em.op("sync", "dma_start", out=ubuf[:, :],
              in_=ap2(dX2, jt * 128 * D, D, 128, 2048),
              rd=["x2s%d_0" % jt, "x2s%d_1" % jt],
              wr=["ubuf"], inc="dX2r", n=16)
        em.op("vector", "tensor_tensor", ubuf[:, :],
              y2t[jt][0][:, 0:2048], ubuf[:, :], AL.add,
              rd=[y2t[jt][1]], wr=["ubuf"], inc="sV")
        em.op("gpsimd", "dma_start", out=ap2(dY, jt * 128 * D, D, 128, 2048),
              in_=ubuf[:, :], rd=["ubuf"], inc="dY", n=16)
    sc4.close()

    em.wait("gpsimd", "dY", C["dY"])
    em.wait("gpsimd", "dOS", C["dOS"])
    em.wait("gpsimd", "dX2s", C["dX2s"])

    def _replay(eng_name):
        def run(e):
            for i, fn in enumerate(em.q[eng_name]):
                try:
                    fn(e)
                except Exception:
                    import traceback
                    print("FAILED at", eng_name, "op", i)
                    traceback.print_exc()
                    raise
        return run

    with nc.Block() as block:
        block.sync(_replay("sync"))
        block.tensor(_replay("tensor"))
        block.vector(_replay("vector"))
        block.scalar(_replay("scalar"))
        block.gpsimd(_replay("gpsimd"))

    stack.close()
    return nc


_NC_CACHE = None


def _get_nc():
    global _NC_CACHE
    if _NC_CACHE is None:
        _NC_CACHE = build()
    return _NC_CACHE


def kernel(L0_attn_x, L0_attn_Wq, L0_attn_Wk, L0_attn_Wv, L0_attn_Wo,
           L0_ffn_W_g, L0_ffn_W_u, L0_ffn_W_d):
    x = np.ascontiguousarray(np.asarray(L0_attn_x, dtype=np.float32))

    def _ulperm(wT):
        P = np.empty_like(wT)
        for g in range(8):
            b = g * 256
            P[:, b:b + 64] = wT[:, b:b + 64]
            P[:, b + 64:b + 128] = wT[:, b + 128:b + 192]
            P[:, b + 128:b + 192] = wT[:, b + 64:b + 128]
            P[:, b + 192:b + 256] = wT[:, b + 192:b + 256]
        return np.ascontiguousarray(P)

    wqT = _ulperm(np.asarray(L0_attn_Wq, dtype=np.float32).T)
    wkT = _ulperm(np.asarray(L0_attn_Wk, dtype=np.float32).T)
    wvT = np.ascontiguousarray(np.asarray(L0_attn_Wv, dtype=np.float32).T)
    woT = np.ascontiguousarray(np.asarray(L0_attn_Wo, dtype=np.float32).T)
    wgT = np.ascontiguousarray(np.asarray(L0_ffn_W_g, dtype=np.float32).T)
    wuT = np.ascontiguousarray(np.asarray(L0_ffn_W_u, dtype=np.float32).T)
    wdT = np.ascontiguousarray(np.asarray(L0_ffn_W_d, dtype=np.float32).T)

    _Q, _h = 2097152, 1048576

    def _dyn(qe0, qe1):
        return np.array([[qe0, qe1, qe0 + _h, qe0 + _Q, qe0 + _Q + _h,
                          qe1 + _h, qe1 + _Q, qe1 + _Q + _h]], dtype=np.int32)
    dyn_half = {0: _dyn(0, 1536), 1: _dyn(512, 1024)}
    sel_half = {
        0: np.array([[1.0, 0.0]], dtype=np.float32),
        1: np.array([[0.0, 1.0]], dtype=np.float32),
    }
    qtiles_half = {
        0: [0, 1, 2, 3, 12, 13, 14, 15],
        1: [4, 5, 6, 7, 8, 9, 10, 11],
    }
    perm_half = {}
    for hlf, qt in qtiles_half.items():
        P = np.zeros((NT, QT), dtype=np.float32)
        for j, gt in enumerate(qt):
            P[gt, j] = 1.0
        perm_half[hlf] = P
    toksel = {h: np.concatenate([np.arange(t * 128, t * 128 + 128)
                                 for t in qtiles_half[h]]) for h in (0, 1)}

    in_maps = []
    for c in range(8):
        b = c // 2
        hlf = c % 2
        xb = x[b]
        in_maps.append({
            "xT": np.ascontiguousarray(xb.T),
            "xqT": np.ascontiguousarray(xb.T[:, toksel[hlf]]),
            "xr": np.ascontiguousarray(xb[toksel[hlf]]),
            "wqT": wqT, "wkT": wkT, "wvT": wvT, "woT": woT,
            "wgT": wgT, "wuT": wuT, "wdT": wdT,
            "dynv_d": dyn_half[hlf], "selv": sel_half[hlf],
            "perm": perm_half[hlf],
        })

    nc = _get_nc()
    res = run_bass_kernel_spmd(nc, in_maps, core_ids=list(range(8)))

    out = np.empty((4, NTOK, D), dtype=np.float32)
    for c in range(8):
        b = c // 2
        hlf = c % 2
        out[b][toksel[hlf]] = res.results[c]["y"]
    return out


if __name__ == "__main__":
    nco = build()
    print("built ok")


# version 23
# speedup vs baseline: 1.0671x; 1.0358x over previous
import numpy as np
import concourse.bass as bass
import concourse.mybir as mybir
from concourse.bass_utils import run_bass_kernel_spmd

F32 = mybir.dt.float32
BF16 = mybir.dt.bfloat16
I32 = mybir.dt.int32
AL = mybir.AluOpType
AF = mybir.ActivationFunctionType
AX = mybir.AxisListType

D = 2048
NTOK = 2048
NT = 16
NQ = 1024
QT = 8
NH = 16
HD = 128
HF = 8192
HC = 256
NCH = HF // HC
PI = float(np.pi)
TWO_PI = 2.0 * np.pi

ENGINES = ("sync", "tensor", "vector", "scalar", "gpsimd")


class Em:

    def __init__(self, nc):
        self.nc = nc
        self.q = {e: [] for e in ENGINES}
        self.semh = {}
        self.cnt = {}
        self.waited = {e: {} for e in ENGINES}
        self.w_ev = {}
        self.r_ev = {}

    def sem(self, name):
        self.semh[name] = self.nc.alloc_semaphore(name=name)
        self.cnt[name] = 0

    def _emit_op(self, eng, meth, a, k, inc, n):
        if inc is not None:
            self.cnt[inc] += n
        iv = (self.semh[inc], n) if inc is not None else None

        def fn(h, _a=a, _k=k, _m=meth, _iv=iv):
            r = getattr(h, _m)(*_a, **_k)
            if _iv is not None:
                r.then_inc(_iv[0], _iv[1])
        self.q[eng].append(fn)
        return (inc, self.cnt[inc]) if inc is not None else None

    def raw(self, eng, fn):
        self.q[eng].append(fn)

    def wait(self, eng, sem, val):
        if val is None or val <= 0:
            return
        if self.waited[eng].get(sem, -1) >= val:
            return
        self.waited[eng][sem] = val
        h = self.semh[sem]
        self.q[eng].append(lambda e, _h=h, _v=val: e.wait_ge(_h, _v))

    def pre(self, eng, rd=(), wr=()):
        for res in rd:
            ev = self.w_ev.get(res)
            if ev:
                self.wait(eng, ev[0], ev[1])
        for res in wr:
            ev = self.w_ev.get(res)
            if ev:
                self.wait(eng, ev[0], ev[1])
            for s, v in self.r_ev.get(res, {}).items():
                self.wait(eng, s, v)

    def post(self, ev, rd=(), wr=()):
        assert ev is not None, "tracked op needs an inc semaphore"
        for res in wr:
            self.w_ev[res] = ev
            self.r_ev[res] = {}
        for res in rd:
            d = self.r_ev.setdefault(res, {})
            if d.get(ev[0], -1) < ev[1]:
                d[ev[0]] = ev[1]

    def op(self, eng, meth, *a, rd=(), wr=(), inc=None, n=1, **k):
        self.pre(eng, rd, wr)
        ev = self._emit_op(eng, meth, a, k, inc, n)
        if ev is not None:
            self.post(ev, rd, wr)
        return ev

    def barrier(self, esem):
        for eng, s in esem.items():
            self.cnt[s] += 1
            h = self.semh[s]
            def closer(e, _h=h):
                e.drain()
                e.nop().then_inc(_h, 1)
            self.q[eng].append(closer)
        vals = dict(self.cnt)
        for eng in ENGINES:
            for s, v in vals.items():
                self.wait(eng, s, v)


def ap2(t, off, prow, npart, nfree, fstep=1):
    return bass.AP(t, off, [[prow, npart], [fstep, nfree]])


def build():
    nc = bass.Bass("TRN2", target_bir_lowering=False, debug=False)

    dX = nc.dram_tensor("xT", [D, NTOK], F32, kind="ExternalInput")
    dXQ = nc.dram_tensor("xqT", [D, NQ], F32, kind="ExternalInput")
    dXR = nc.dram_tensor("xr", [NQ, D], F32, kind="ExternalInput")
    dWQ = nc.dram_tensor("wqT", [D, D], F32, kind="ExternalInput")
    dWK = nc.dram_tensor("wkT", [D, D], F32, kind="ExternalInput")
    dWV = nc.dram_tensor("wvT", [D, D], F32, kind="ExternalInput")
    dWO = nc.dram_tensor("woT", [D, D], F32, kind="ExternalInput")
    dWG = nc.dram_tensor("wgT", [D, HF], F32, kind="ExternalInput")
    dWU = nc.dram_tensor("wuT", [D, HF], F32, kind="ExternalInput")
    dWD = nc.dram_tensor("wdT", [HF, D], F32, kind="ExternalInput")
    dDYN = nc.dram_tensor("dynv_d", [1, 8], I32, kind="ExternalInput")
    dSEL = nc.dram_tensor("selv", [1, 2], F32, kind="ExternalInput")
    dPRM = nc.dram_tensor("perm", [NT, QT], F32, kind="ExternalInput")
    dY = nc.dram_tensor("y", [NQ, D], F32, kind="ExternalOutput")
    dOS = nc.dram_tensor("o_scr", [NH * HD, NQ], F32)
    dX2 = nc.dram_tensor("x2_scr", [NQ, D], F32)

    em = Em(nc)
    globals()["_EM"] = em
    for s in ("sT", "sV", "sA", "sG", "dDy", "dSe", "dPm", "dXA", "dXB",
              "dWk", "dWv", "dWq", "dQs", "dQg", "dQgB", "dNr", "dNq", "dOS", "dWo",
              "dO0", "dO1", "dXr", "dX2s", "dX2f", "dX2fB", "dX2r", "dY", "dRcK", "dRcQ",
              "dPr0", "dPr1", "dPs0", "dPs1", "dWg0", "dWg1", "dWu0", "dWu1",
              "dWd0", "dWd1", "dDu1", "dDu2", "dKtA", "dKtB", "dWoB", "dWoC", "dWoD"):
        em.sem(s)
    C = em.cnt

    ESEM = {"tensor": "sT", "vector": "sV", "scalar": "sA", "gpsimd": "sG"}
    import contextlib
    stack = contextlib.ExitStack()

    def sb(name, shape, dtype=F32, scope=None):
        if scope is None:
            return nc.alloc_sbuf_tensor(name, shape, dtype)
        return scope.enter_context(nc.sbuf_tensor(name, shape, dtype))

    ident = sb("ident", [128, 128])
    identB = sb("identB", [128, 128], BF16)
    ones = sb("ones", [128, 128])
    dynv = sb("dynv", [1, 8], I32)
    selt = sb("selt", [128, 2])
    zrow = sb("zrow", [1, 16])
    constv = sb("constv", [128, 4])

    pb = [nc.alloc_psum_tensor("pb%d" % i, [128, 512], F32) for i in range(8)]
    PB = ["pb%d" % i for i in range(8)]

    rg_v = {k: nc.vector.alloc_register("rv_" + k) for k in ("qe0", "qe1")}

    sc1 = contextlib.ExitStack()
    permm = sb("permm", [NT, QT], scope=sc1)
    selrow = sb("selrow", [1, 2], scope=sc1)
    negm16 = sb("negm16", [NT, 128], scope=sc1)
    negmQ = sb("negmQ", [QT, 128], scope=sc1)
    rstdc = sb("rstdc", [128, NT], scope=sc1)
    csrC = sb("csrC", [128, NTOK], scope=sc1)
    csrS = sb("csrS", [128, NTOK], scope=sc1)
    csrQC = sb("csrQC", [128, NQ], scope=sc1)
    csrQS = sb("csrQS", [128, NQ], scope=sc1)
    crow = sb("crow", [1, 768], scope=sc1)
    xqcA = sb("xqcA", [128, 512], scope=sc1)
    xqcB = sb("xqcB", [128, 512], scope=sc1)
    xq_bufs = [xqcA, xqcB]
    wk_sl = sb("wk_sl", [128, NT * 256], scope=sc1)
    wv_sl = sb("wv_sl", [128, NT * 256], scope=sc1)
    wq_sl = sb("wq_sl", [128, NT * 256], scope=sc1)
    kropU = sb("kropU", [128, NTOK], scope=sc1)
    kropL = sb("kropL", [128, NTOK], scope=sc1)
    qropU = sb("qropU", [128, NQ], scope=sc1)
    qropL = sb("qropL", [128, NQ], scope=sc1)
    vrowsA = sb("vrowsA", [128, NT * 128], scope=sc1)
    vrowsB = sb("vrowsB", [128, NT * 128], scope=sc1)
    kT = sb("kTb", [128, NTOK], scope=sc1)
    qTt = sb("qTb", [128, NQ], scope=sc1)
    qquadA = sb("qquadA", [128, 4 * 512], scope=sc1)
    qquadB = sb("qquadB", [128, 4 * 512], scope=sc1)
    mtil = sb("mtil", [128, 8 * 512], scope=sc1)
    s2b = sb("s2b", [128, 1024], scope=sc1)
    s4b = sb("s4b", [128, 512], scope=sc1)
    tac = sb("tac", [128, 2 * 512], scope=sc1)
    obuf = sb("obuf", [128, 1024], scope=sc1)
    drow = sb("drow", [1, 1024], scope=sc1)
    negmrow = sb("negmrow", [1, 2048], scope=sc1)
    negmqrow = sb("negmqrow", [1, 1024], scope=sc1)
    bcst = sb("bcst", [128, 512], scope=sc1)
    sc1b = contextlib.ExitStack()
    xsq = sb("xsq", [128, 2048], scope=sc1b)
    rowst16 = sb("rowst16", [16, 128], scope=sc1b)
    statA = sb("statA", [128, NT], scope=sc1b)
    statB = sb("statB", [128, NT], scope=sc1b)
    statC = sb("statC", [128, NT], scope=sc1b)

    em.op("sync", "dma_start", out=dynv[:], in_=dDYN.ap()[0:1, :],
          wr=["dynv"], inc="dDy", n=16)
    em.op("sync", "dma_start", out=selrow[:], in_=dSEL.ap()[0:1, :],
          wr=["selrow"], inc="dSe", n=16)
    em.op("sync", "dma_start", out=permm[:], in_=dPRM.ap()[:, :],
          wr=["permm"], inc="dPm", n=16)

    em.op("gpsimd", "memset", ident[:], 0.0, wr=["identz"], inc="sG")
    em.op("gpsimd", "memset", ones[:], 1.0, wr=["ones"], inc="sG")
    em.op("gpsimd", "memset", zrow[:], 0.0, wr=["zrow"], inc="sG")
    _consts = [float(-np.log(10000.0) / 64.0), -1.0, PI / 2.0, 63.5]
    for ci, cv in enumerate(_consts):
        em.op("gpsimd", "memset", constv[:, ci:ci + 1], cv,
              wr=["constv"], inc="sG")
        nc.const_aps.register(F32, cv, constv[:, ci:ci + 1])
    em.op("gpsimd", "affine_select", out=ident[:], in_=ident[:],
          compare_op=AL.not_equal, fill=1.0, base=0,
          pattern=[[-1, 128]], channel_multiplier=1,
          rd=["identz"], wr=["ident"], inc="sG")
    em.op("scalar", "copy", identB[:], ident[:],
          rd=["ident"], wr=["identB"], inc="sA")

    em.pre("vector", rd=["dynv"])
    em.raw("vector", lambda e: e.load(rg_v["qe0"], dynv[0:1, 0:1]))
    em.raw("vector", lambda e: e.load(rg_v["qe1"], dynv[0:1, 1:2]))

    em.op("tensor", "matmul", pb[7][:, 0:2], ones[0:1, 0:128], selrow[0:1, :],
          start=True, stop=True, rd=["ones", "selrow"], wr=[PB[7]], inc="sT")
    em.op("vector", "tensor_copy", selt[:], pb[7][:, 0:2],
          rd=[PB[7]], wr=["selt"], inc="sV")
    for fam in range(2):
        for j in range(4):
            sl = mtil[:, fam * 2048 + j * 512:fam * 2048 + j * 512 + 512]
            em.op("gpsimd", "memset", sl, 1.0, wr=["mtil"], inc="sG")
            em.op("gpsimd", "affine_select", out=sl, in_=sl,
                  compare_op=AL.is_ge, fill=0.0, base=-128 * j,
                  pattern=[[1, 512]], channel_multiplier=-1,
                  wr=["mtil"], inc="sG")
            em.op("vector", "tensor_scalar", sl, sl,
                  selt[:, fam:fam + 1], selt[:, 1 - fam:2 - fam],
                  AL.mult, AL.add, rd=["selt"], wr=["mtil"], inc="sV")

    def emit_wslice_load(g):
        for (dst, dn, dram, dsm) in ((wv_sl, "wv_sl", dWV, "dWv"),
                                     (wk_sl, "wk_sl", dWK, "dWk"),
                                     (wq_sl, "wq_sl", dWQ, "dWq")):
            src = bass.AP(dram, g * 256,
                          [[NTOK, 128], [128 * NTOK, 16], [1, 256]])
            dap = bass.AP(dst, 0, [[NT * 256, 128], [256, 16], [1, 256]])
            em.op("sync", "dma_start", out=dap, in_=src, wr=[dn], inc=dsm,
                  n=16)

    em.op("tensor", "matmul", pb[0][:, 0:16], ones[0:1, 0:128], zrow[0:1, 0:16],
          start=True, stop=False, rd=["ones", "zrow"], wr=[PB[0]], inc="sT")
    em.op("tensor", "matmul", pb[1][:, 0:16], ones[0:1, 0:128], zrow[0:1, 0:16],
          start=True, stop=False, rd=["ones", "zrow"], wr=[PB[1]], inc="sT")
    pidx = 0
    for dchunk in range(16):
        for piece in range(4):
            par = pidx % 2
            buf = xq_bufs[par][:, :]
            bn = "xqc%d" % par
            sqb = xsq[:, par * 512:par * 512 + 512]
            sqn = "xsq%d" % par
            em.op("sync", "dma_start", out=buf,
                  in_=ap2(dX, dchunk * 128 * NTOK + piece * 512, NTOK, 128, 512),
                  wr=[bn], inc=("dXA" if par == 0 else "dXB"), n=16)
            em.op("scalar", "activation", sqb, buf, AF.Square,
                  rd=[bn], wr=[sqn], inc="sA")
            tg = piece * 4
            em.pre("tensor", rd=[bn, sqn], wr=[PB[0], PB[1]])
            last = (dchunk == 15 and piece == 3)
            for tt in range(4):
                em._emit_op("tensor", "matmul",
                            (pb[0][:, tg + tt:tg + tt + 1],
                             buf[:, tt * 128:tt * 128 + 128], ones[:, 0:1]),
                            dict(start=False, stop=(last and tt == 3)), None, 1)
                ev = em._emit_op("tensor", "matmul",
                                 (pb[1][:, tg + tt:tg + tt + 1],
                                  sqb[:, tt * 128:tt * 128 + 128], ones[:, 0:1]),
                                 dict(start=False, stop=(last and tt == 3)),
                                 "sT", 1)
            em.post(ev, rd=[bn, sqn], wr=[PB[0], PB[1]])
            pidx += 1

    emit_wslice_load(0)

    em.op("vector", "tensor_scalar", statA[:, :], pb[0][:, 0:16],
          -1.0 / D, None, AL.mult, rd=[PB[0]], wr=["statA"], inc="sV")
    em.op("vector", "tensor_scalar", statB[:, :], pb[1][:, 0:16],
          1.0 / D, None, AL.mult, rd=[PB[1]], wr=["statB"], inc="sV")
    em.op("vector", "tensor_tensor", statC[:, :], statA[:, :], statA[:, :],
          AL.mult, rd=["statA"], wr=["statC"], inc="sV")
    em.op("vector", "tensor_tensor", statB[:, :], statB[:, :], statC[:, :],
          AL.subtract, rd=["statC"], wr=["statB"], inc="sV")
    em.op("vector", "tensor_scalar", statB[:, :], statB[:, :],
          1e-5, None, AL.add, wr=["statB"], inc="sV")
    em.op("scalar", "activation", statC[:, :], statB[:, :], AF.Sqrt,
          rd=["statB"], wr=["statC"], inc="sA")
    em.op("vector", "reciprocal", rstdc[:, :], statC[:, :],
          rd=["statC"], wr=["rstdc"], inc="sV")
    for _ in range(2):
        em.op("vector", "tensor_tensor", statC[:, :], rstdc[:, :], rstdc[:, :],
              AL.mult, rd=["rstdc"], wr=["statC"], inc="sV")
        em.op("vector", "tensor_tensor", statC[:, :], statC[:, :], statB[:, :],
              AL.mult, rd=["statB"], wr=["statC"], inc="sV")
        em.op("vector", "tensor_scalar", statC[:, :], statC[:, :],
              -0.5, 1.5, AL.mult, AL.add, wr=["statC"], inc="sV")
        em.op("vector", "tensor_tensor", rstdc[:, :], rstdc[:, :], statC[:, :],
              AL.mult, rd=["statC"], wr=["rstdc"], inc="sV")

    em.op("tensor", "transpose", pb[2][0:16, 0:128], statA[:, :], ident[:],
          rd=["statA", "ident"], wr=[PB[2]], inc="sT")
    em.op("vector", "tensor_copy", negm16[:, :], pb[2][0:16, 0:128],
          rd=[PB[2]], wr=["negm16"], inc="sV")
    em.op("tensor", "matmul", pb[2][0:QT, 0:128], permm[:, :], negm16[:, :],
          start=True, stop=True, rd=["permm", "negm16"], wr=[PB[2]], inc="sT")
    em.op("vector", "tensor_copy", negmQ[:, :], pb[2][0:QT, 0:128],
          rd=[PB[2]], wr=["negmQ"], inc="sV")
    em.op("gpsimd", "dma_start",
          out=bass.AP(negmrow, 0, [[2048, 1], [1, 2048]]),
          in_=bass.AP(negm16, 0, [[128, 16], [1, 128]]),
          rd=["negm16"], wr=["negmrow"], inc="dNr", n=16)
    em.op("gpsimd", "dma_start",
          out=bass.AP(negmqrow, 0, [[1024, 1], [1, 1024]]),
          in_=bass.AP(negmQ, 0, [[128, 8], [1, 128]]),
          rd=["negmQ"], wr=["negmqrow"], inc="dNq", n=16)

    em.op("gpsimd", "iota", statC[0:64, 0:1], pattern=[[0, 1]], base=0,
          channel_multiplier=1, allow_small_or_imprecise_dtypes=True,
          wr=["statC"], inc="sG")
    em.op("gpsimd", "iota", statC[64:128, 0:1], pattern=[[0, 1]], base=0,
          channel_multiplier=1, allow_small_or_imprecise_dtypes=True,
          wr=["statC"], inc="sG")
    em.op("scalar", "activation", statC[:, 1:2], statC[:, 0:1], AF.Exp,
          scale=float(-np.log(10000.0) / 64.0),
          rd=["statC", "constv"], wr=["theta"], inc="sA")
    em.op("gpsimd", "iota", xsq[:, 0:2048], pattern=[[1, 2048]], base=0,
          channel_multiplier=0, allow_small_or_imprecise_dtypes=True,
          wr=["xsq0", "xsq1"], inc="sG")
    ang = csrS
    em.op("vector", "tensor_scalar", ang[0:64, 0:2048], xsq[0:64, 0:2048],
          statC[0:64, 1:2], None, AL.mult,
          rd=["xsq0", "xsq1", "theta", "statC"], wr=["csrS"], inc="sV")
    bb = xsq[0:64, 0:2048]
    for kk in range(8, -1, -1):
        m = TWO_PI * (2.0 ** kk)
        m_hi = float(np.float32(m))
        m_lo = float(m - np.float64(np.float32(m)))
        em.op("vector", "tensor_single_scalar", bb, ang[0:64, 0:2048], m_hi,
              AL.is_ge, rd=["csrS"], wr=["xsq0", "xsq1"], inc="sV")
        em.op("vector", "scalar_tensor_tensor", ang[0:64, 0:2048], bb, -m_hi,
              ang[0:64, 0:2048], AL.mult, AL.add,
              rd=["xsq0", "xsq1"], wr=["csrS"], inc="sV")
        em.op("vector", "scalar_tensor_tensor", ang[0:64, 0:2048], bb, -m_lo,
              ang[0:64, 0:2048], AL.mult, AL.add,
              rd=["xsq0", "xsq1"], wr=["csrS"], inc="sV")
    tp_hi = float(np.float32(TWO_PI))
    tp_lo = float(TWO_PI - np.float64(np.float32(TWO_PI)))
    em.op("vector", "tensor_single_scalar", bb, ang[0:64, 0:2048], PI, AL.is_ge,
          rd=["csrS"], wr=["xsq0", "xsq1"], inc="sV")
    em.op("vector", "scalar_tensor_tensor", ang[0:64, 0:2048], bb, -tp_hi,
          ang[0:64, 0:2048], AL.mult, AL.add,
          rd=["xsq0", "xsq1"], wr=["csrS"], inc="sV")
    em.op("vector", "scalar_tensor_tensor", ang[0:64, 0:2048], bb, -tp_lo,
          ang[0:64, 0:2048], AL.mult, AL.add,
          rd=["xsq0", "xsq1"], wr=["csrS"], inc="sV")
    em.op("scalar", "activation", xsq[0:64, 0:2048], ang[0:64, 0:2048], AF.Abs,
          rd=["csrS"], wr=["xsq0", "xsq1"], inc="sA")
    em.op("scalar", "activation", csrC[0:64, :], xsq[0:64, 0:2048], AF.Sin,
          scale=-1.0, bias=PI / 2.0,
          rd=["xsq0", "xsq1", "constv"], wr=["csrC"], inc="sA")
    em.op("scalar", "activation", ang[0:64, 0:2048], ang[0:64, 0:2048],
          AF.Sin, wr=["csrS"], inc="sA")
    em.op("gpsimd", "dma_start", out=csrC[64:128, :], in_=csrC[0:64, :],
          rd=["csrC"], wr=["csrC"], inc="dDu1", n=16)
    em.op("gpsimd", "dma_start", out=csrS[64:128, :], in_=csrS[0:64, :],
          rd=["csrS"], wr=["csrS"], inc="dDu2", n=16)
    em.op("tensor", "transpose", pb[3][0:16, 0:128], rstdc[:, :], ident[:],
          rd=["rstdc", "ident"], wr=[PB[3]], inc="sT")
    em.op("vector", "tensor_copy", rowst16[:, :], pb[3][0:16, 0:128],
          rd=[PB[3]], wr=["rowst16"], inc="sV")
    for hhh in range(2):
        em.op("gpsimd", "dma_start",
              out=bass.AP(drow, 0, [[1024, 1], [1, 1024]]),
              in_=bass.AP(rowst16, hhh * 8 * 128, [[128, 8], [1, 128]]),
              rd=["rowst16"], wr=["drow"], inc="dNq", n=16)
        for cc in range(2):
            em.op("tensor", "matmul", pb[3][:, 0:512], ones[0:1, 0:128],
                  drow[0:1, cc * 512:cc * 512 + 512], start=True, stop=True,
                  rd=["drow", "ones"], wr=[PB[3]], inc="sT")
            em.op("vector", "tensor_copy",
                  xsq[:, (hhh * 2 + cc) * 512:(hhh * 2 + cc) * 512 + 512],
                  pb[3][:, 0:512], rd=[PB[3]], wr=["xsq0", "xsq1"], inc="sV")
    em.op("vector", "tensor_tensor", csrC[:, :], csrC[:, :], xsq[:, 0:2048],
          AL.mult, rd=["xsq0", "xsq1"], wr=["csrC"], inc="sV")
    em.op("vector", "tensor_tensor", csrS[:, :], csrS[:, :], xsq[:, 0:2048],
          AL.mult, rd=["xsq0", "xsq1"], wr=["csrS"], inc="sV")
    em.op("vector", "tensor_copy", csrQC[:, 0:512],
          ap2(csrC, rg_v["qe0"], NTOK, 128, 512),
          rd=["csrC"], wr=["csrQC"], inc="sV")
    em.op("vector", "tensor_copy", csrQC[:, 512:1024],
          ap2(csrC, rg_v["qe1"], NTOK, 128, 512),
          rd=["csrC"], wr=["csrQC"], inc="sV")
    em.op("vector", "tensor_copy", csrQS[:, 0:512],
          ap2(csrS, rg_v["qe0"], NTOK, 128, 512),
          rd=["csrS"], wr=["csrQS"], inc="sV")
    em.op("vector", "tensor_copy", csrQS[:, 512:1024],
          ap2(csrS, rg_v["qe1"], NTOK, 128, 512),
          rd=["csrS"], wr=["csrQS"], inc="sV")

    def emit_colsums(g):
        for wi, (wsl, wn) in enumerate(((wk_sl, "wk_sl"), (wv_sl, "wv_sl"),
                                        (wq_sl, "wq_sl"))):
            em.pre("tensor", rd=[wn, "ones"], wr=[PB[6]])
            for dc in range(16):
                ev = em._emit_op("tensor", "matmul",
                                 (pb[6][0:1, 0:256], ones[:, 0:1],
                                  wsl[:, dc * 256:dc * 256 + 256]),
                                 dict(start=(dc == 0), stop=(dc == 15)),
                                 "sT" if dc == 15 else None, 1)
            em.post(ev, rd=[wn, "ones"], wr=[PB[6]])
            em.op("vector", "tensor_copy", crow[0:1, wi * 256:wi * 256 + 256],
                  pb[6][0:1, 0:256], rd=[PB[6]], wr=["crow"], inc="sV")

    def rope_emit(rawbuf, rawn, outbuf, outn, ntile, cstab, csn,
                  raw_base=0, out_base=0):
        rstep = rawbuf.shape[1]
        ostep = outbuf.shape[1]

        def A(boff):
            return bass.AP(rawbuf, raw_base + boff,
                           [[rstep, 128], [256, ntile], [1, 64]])

        def O(boff):
            return bass.AP(outbuf, out_base + boff,
                           [[ostep, 128], [256, ntile], [1, 64]])

        def CS(off):
            return bass.AP(cstab, off, [[cstab.shape[1], 128], [128, ntile], [1, 64]])
        for hh in range(2):
            b = hh * 128
            em.op("vector", "tensor_tensor", O(b), A(b), CS(0), AL.mult,
                  rd=[rawn, csn], wr=[outn], inc="sV")
            em.op("vector", "tensor_tensor", O(b + 64), A(b), CS(64), AL.mult,
                  rd=[rawn, csn], wr=[outn], inc="sV")
            em.op("vector", "tensor_tensor", A(b), A(b + 64), CS(64), AL.mult,
                  rd=[csn], wr=[rawn], inc="sV")
            em.op("vector", "tensor_tensor", O(b), O(b), A(b), AL.subtract,
                  rd=[rawn], wr=[outn], inc="sV")
            em.op("vector", "tensor_tensor", A(b), A(b + 64), CS(0), AL.mult,
                  rd=[csn], wr=[rawn], inc="sV")
            em.op("vector", "tensor_tensor", O(b + 64), O(b + 64), A(b),
                  AL.add, rd=[rawn], wr=[outn], inc="sV")

    def rope_from_psum(pU, pUn, pL, pLn, outU, outUn, outL, outLn,
                       oslice, cosT, cosn, sinT, sinn, tslice):
        em.op("vector", "tensor_tensor", outU[:, oslice], pU[:, 0:512],
              cosT[:, tslice], AL.mult, rd=[pUn, cosn], wr=[outUn], inc="sV")
        em.op("vector", "tensor_tensor", bcst[:, :], pL[:, 0:512],
              sinT[:, tslice], AL.mult, rd=[pLn, sinn], wr=["bcst"], inc="sV")
        em.op("vector", "tensor_tensor", outU[:, oslice], outU[:, oslice],
              bcst[:, :], AL.subtract, rd=["bcst"], wr=[outUn], inc="sV")
        em.op("vector", "tensor_tensor", bcst[:, :], pU[:, 0:512],
              sinT[:, tslice], AL.mult, rd=[pUn, sinn], wr=["bcst"], inc="sV")
        em.op("vector", "tensor_tensor", outL[:, oslice], pL[:, 0:512],
              cosT[:, tslice], AL.mult, rd=[pLn, cosn], wr=[outLn], inc="sV")
        em.op("vector", "tensor_tensor", outL[:, oslice], outL[:, oslice],
              bcst[:, :], AL.add, rd=["bcst"], wr=[outLn], inc="sV")

    def emit_qload(ld):
        tq, dh, sub = ld // 4, (ld // 2) % 2, ld % 2
        buf, bn = (qquadA, "qquadA") if ld % 2 == 0 else (qquadB, "qquadB")
        dcx0 = dh * 8 + sub * 4
        src_ap = bass.AP(dXQ, dcx0 * 128 * NQ + tq * 512,
                         [[NQ, 128], [128 * NQ, 4], [1, 512]])
        dst_ap = bass.AP(buf, 0, [[2048, 128], [512, 4], [1, 512]])
        em.op("gpsimd", "dma_start", out=dst_ap, in_=src_ap, wr=[bn],
              inc=("dQg" if ld % 2 == 0 else "dQgB"), n=16)

    def emit_group_proj(g):
        emit_qload(0)
        emit_qload(1)
        pidx = 0
        for quad in range(4):
            kb0, kb1 = (0, 1) if quad % 2 == 0 else (2, 3)
            for dchunk in range(16):
                par = pidx % 2
                buf = xq_bufs[par][:, :]
                bn = "xqc%d" % par
                em.op("sync", "dma_start", out=buf,
                      in_=ap2(dX, dchunk * 128 * NTOK + quad * 512, NTOK, 128, 512),
                      wr=[bn], inc=("dXA" if par == 0 else "dXB"), n=16)
                if dchunk == 0:
                    em.pre("tensor", rd=[bn, "wk_sl"], wr=[PB[kb0], PB[kb1]])
                    em._emit_op("tensor", "matmul",
                                (pb[kb0][:, 0:512],
                                 wk_sl[:, dchunk * 256:dchunk * 256 + 128],
                                 buf),
                                dict(start=True, stop=False), None, 1)
                    ev = em._emit_op("tensor", "matmul",
                                     (pb[kb1][:, 0:512],
                                      wk_sl[:, dchunk * 256 + 128:
                                            dchunk * 256 + 256], buf),
                                     dict(start=True, stop=False), "sT", 1)
                    em.post(ev, rd=[bn, "wk_sl"], wr=[PB[kb0], PB[kb1]])
                    em.pre("tensor", rd=[bn, "wv_sl"],
                           wr=[PB[4], PB[5], PB[6], PB[7]])
                    for tt in range(4):
                        ev = em._emit_op(
                            "tensor", "matmul",
                            (pb[4 + tt][:, 0:256],
                             buf[:, tt * 128:tt * 128 + 128],
                             wv_sl[:, dchunk * 256:dchunk * 256 + 256]),
                            dict(start=True, stop=False),
                            "sT" if tt == 3 else None, 1)
                    em.post(ev, rd=[bn, "wv_sl"],
                            wr=[PB[4], PB[5], PB[6], PB[7]])
                    pidx += 1
                    continue
                em.pre("tensor", rd=[bn, "wk_sl", "wv_sl"], wr=[])
                em._emit_op("tensor", "matmul",
                            (pb[kb0][:, 0:512],
                             wk_sl[:, dchunk * 256:dchunk * 256 + 128], buf),
                            dict(start=False, stop=False), None, 1)
                em._emit_op("tensor", "matmul",
                            (pb[kb1][:, 0:512],
                             wk_sl[:, dchunk * 256 + 128:dchunk * 256 + 256],
                             buf),
                            dict(start=False, stop=False), None, 1)
                for tt in range(4):
                    ev = em._emit_op(
                        "tensor", "matmul",
                        (pb[4 + tt][:, 0:256],
                         buf[:, tt * 128:tt * 128 + 128],
                         wv_sl[:, dchunk * 256:dchunk * 256 + 256]),
                        dict(start=False, stop=False),
                        "sT" if tt == 3 else None, 1)
                em.post(ev, rd=[bn, "wk_sl", "wv_sl"],
                        wr=[PB[kb0], PB[kb1], PB[4], PB[5], PB[6], PB[7]])
                pidx += 1
            em.pre("tensor", rd=["negmrow", "crow"], wr=[])
            em._emit_op("tensor", "matmul",
                        (pb[kb0][:, 0:512], crow[0:1, 0:128],
                         negmrow[0:1, quad * 512:quad * 512 + 512]),
                        dict(start=False, stop=True), None, 1)
            em._emit_op("tensor", "matmul",
                        (pb[kb1][:, 0:512], crow[0:1, 128:256],
                         negmrow[0:1, quad * 512:quad * 512 + 512]),
                        dict(start=False, stop=True), None, 1)
            for tt in range(4):
                tg = quad * 4 + tt
                ev = em._emit_op("tensor", "matmul",
                                 (pb[4 + tt][:, 0:256],
                                  negmrow[0:1, tg * 128:tg * 128 + 128],
                                  crow[0:1, 256:512]),
                                 dict(start=False, stop=True),
                                 "sT" if tt == 3 else None, 1)
            em.post(ev, rd=["negmrow", "crow"],
                    wr=[PB[kb0], PB[kb1], PB[4], PB[5], PB[6], PB[7]])
            qs = slice(quad * 512, quad * 512 + 512)
            rope_from_psum(pb[kb0], PB[kb0], pb[kb1], PB[kb1], kropU, "kropU",
                           kropL, "kropL", qs, csrC, "csrC", csrS, "csrS", qs)
            for tt in range(4):
                tg = quad * 4 + tt
                em.op("scalar", "mul",
                      vrowsA[:, tg * 128:tg * 128 + 128],
                      pb[4 + tt][:, 0:128], rstdc[:, tg:tg + 1],
                      rd=[PB[4 + tt], "rstdc"], wr=["vrowsA"], inc="sA")
                em.op("scalar", "mul",
                      vrowsB[:, tg * 128:tg * 128 + 128],
                      pb[4 + tt][:, 128:256], rstdc[:, tg:tg + 1],
                      rd=[PB[4 + tt], "rstdc"], wr=["vrowsB"], inc="sA")
        for tq in range(2):
            qp0, qp1 = (0, 1) if tq == 0 else (2, 3)
            for dh in range(2):
                for sub in range(2):
                    ld = tq * 4 + dh * 2 + sub
                    buf, bn = ((qquadA, "qquadA") if ld % 2 == 0
                               else (qquadB, "qquadB"))
                    for dchunk in range(4):
                        dcx = dh * 8 + sub * 4 + dchunk
                        em.pre("tensor", rd=[bn, "wq_sl"],
                               wr=[PB[qp0], PB[qp1]] if dcx == 0 else [])
                        em._emit_op(
                            "tensor", "matmul",
                            (pb[qp0][:, 0:512],
                             wq_sl[:, dcx * 256:dcx * 256 + 128],
                             buf[:, dchunk * 512:dchunk * 512 + 512]),
                            dict(start=(dcx == 0), stop=False), None, 1)
                        ev = em._emit_op(
                            "tensor", "matmul",
                            (pb[qp1][:, 0:512],
                             wq_sl[:, dcx * 256 + 128:dcx * 256 + 256],
                             buf[:, dchunk * 512:dchunk * 512 + 512]),
                            dict(start=(dcx == 0), stop=False), "sT", 1)
                        em.post(ev, rd=[bn, "wq_sl"], wr=[PB[qp0], PB[qp1]])
                    if ld + 2 < 8:
                        emit_qload(ld + 2)
            em.pre("tensor", rd=["negmqrow", "crow"], wr=[])
            em._emit_op("tensor", "matmul",
                        (pb[qp0][:, 0:512], crow[0:1, 512:640],
                         negmqrow[0:1, tq * 512:tq * 512 + 512]),
                        dict(start=False, stop=True), None, 1)
            ev = em._emit_op("tensor", "matmul",
                             (pb[qp1][:, 0:512], crow[0:1, 640:768],
                              negmqrow[0:1, tq * 512:tq * 512 + 512]),
                             dict(start=False, stop=True), "sT", 1)
            em.post(ev, rd=["negmqrow", "crow"], wr=[PB[qp0], PB[qp1]])
            qs = slice(tq * 512, tq * 512 + 512)
            rope_from_psum(pb[qp0], PB[qp0], pb[qp1], PB[qp1], qropU, "qropU",
                           qropL, "qropL", qs, csrQC, "csrQC", csrQS, "csrQS",
                           qs)


    def emit_transposes(src, srcn, src_off, dst, dstn, ntile):
        for grp in range(0, ntile, 4):
            em.pre("tensor", rd=[srcn, "ident"], wr=[PB[0]])
            for j in range(4):
                t = grp + j
                ev = em._emit_op("tensor", "transpose",
                                 (pb[0][:, j * 128:j * 128 + 128],
                                  src[:, src_off + t * 256:src_off + t * 256 + 128],
                                  ident[:]), {}, "sT" if j == 3 else None, 1)
            em.post(ev, rd=[srcn, "ident"], wr=[PB[0]])
            em.op("scalar", "copy", dst[:, grp * 128:grp * 128 + 512],
                  pb[0][:, 0:512], rd=[PB[0]], wr=[dstn], inc="sA")

    def emit_attention_head(g, hh):
        h = 2 * g + hh
        vrows, vn = (vrowsA, "vrowsA") if hh == 0 else (vrowsB, "vrowsB")
        em.op("gpsimd", "dma_start", out=kT[0:64, :],
              in_=kropU[hh * 64:hh * 64 + 64, :],
              rd=["kropU"], wr=["kT"], inc="dRcK", n=16)
        em.op("gpsimd", "dma_start", out=kT[64:128, :],
              in_=kropL[hh * 64:hh * 64 + 64, :],
              rd=["kropL"], wr=["kT"], inc="dRcK", n=16)
        em.op("gpsimd", "dma_start", out=qTt[0:64, :],
              in_=qropU[hh * 64:hh * 64 + 64, :],
              rd=["qropU"], wr=["qT"], inc="dRcQ", n=16)
        em.op("gpsimd", "dma_start", out=qTt[64:128, :],
              in_=qropL[hh * 64:hh * 64 + 64, :],
              rd=["qropL"], wr=["qT"], inc="dRcQ", n=16)
        iters = []
        for j in range(4):
            iters.append((0, j, None, (0, j), j == 0, False))
        for j in range(4):
            iters.append((0, 4 + j, 1, (1, j), False, j == 3))
        for j in range(8):
            iters.append((1, j, None, None, j == 0, False))
        for j in range(4):
            iters.append((1, 8 + j, None, (1, j), False, False))
        for j in range(4):
            iters.append((1, 12 + j, 0, (0, j), False, j == 3))
        SB = (pb[0], pb[1], pb[4])
        SBN = (PB[0], PB[1], PB[4])

        def s_mm(i):
            qb, kc = iters[i][0], iters[i][1]
            em.op("tensor", "matmul", SB[i % 3][:, 0:512],
                  kT[:, kc * 128:kc * 128 + 128],
                  qTt[:, qb * 512:qb * 512 + 512],
                  start=True, stop=True, rd=["kT", "qT"], wr=[SBN[i % 3]],
                  inc="sT")

        s_mm(0)
        for i, (qb, kc, fcol, mt, ostart, ostop) in enumerate(iters):
            if i + 1 < len(iters):
                s_mm(i + 1)
            lhs_v = vrows[:, kc * 128:kc * 128 + 128]
            ob, obn = (pb[2], PB[2]) if qb == 0 else (pb[3], PB[3])
            sbank, sbn = SB[i % 3], SBN[i % 3]
            par = i % 2
            s2s = s2b[:, par * 512:par * 512 + 512]
            s2n = "s2b%d" % par
            s4s = s4b[:, 0:512] if par == 0 else xqcA[:, 0:512]
            s4n = "s4b0" if par == 0 else "xqc0"
            if fcol is None:
                em.op("scalar", "activation", s2s, sbank[:, 0:512],
                      AF.Square, rd=[sbn], wr=[s2n], inc="sA")
            else:
                em.op("scalar", "activation", s2s, sbank[:, 0:512],
                      AF.Square, scale=selt[:, fcol:fcol + 1],
                      rd=[sbn, "selt"], wr=[s2n], inc="sA")
            em.op("vector", "tensor_tensor", s4s, s2s, s2s, AL.mult,
                  rd=[s2n], wr=[s4n], inc="sV")
            if mt is not None:
                fam, j = mt
                em.op("vector", "tensor_tensor", s4s, s4s,
                      mtil[:, fam * 2048 + j * 512:fam * 2048 + j * 512 + 512],
                      AL.mult, rd=["mtil"], wr=[s4n], inc="sV")
            if ostart:
                em.op("gpsimd", "tensor_copy",
                      tac[:, qb * 512:qb * 512 + 512], s4s,
                      rd=[s4n], wr=["T%d" % qb], inc="sG")
            else:
                em.op("gpsimd", "tensor_tensor",
                      tac[:, qb * 512:qb * 512 + 512],
                      tac[:, qb * 512:qb * 512 + 512], s4s, AL.add,
                      rd=[s4n], wr=["T%d" % qb], inc="sG")
            em.op("tensor", "matmul", ob[:, 0:512], lhs_v, s4s,
                  start=ostart, stop=ostop, rd=[vn, s4n], wr=[obn],
                  inc="sT")
            if i == 7:
                em.op("tensor", "matmul", pb[5][0:1, 0:512], ones[:, 0:1],
                      tac[:, 0:512], start=True, stop=True,
                      rd=["T0", "ones"], wr=[PB[5]], inc="sT")
                em.op("vector", "tensor_single_scalar", drow[0:1, 0:512],
                      pb[5][0:1, 0:512], 1e-6, AL.max, rd=[PB[5]],
                      wr=["drow"], inc="sV")
            if i == 8:
                em.op("vector", "reciprocal", drow[0:1, 512:768],
                      drow[0:1, 0:256], wr=["drow"], inc="sV")
            if i == 10:
                em.op("vector", "reciprocal", drow[0:1, 768:1024],
                      drow[0:1, 256:512], wr=["drow"], inc="sV")
            if i == 13:
                em.op("tensor", "matmul", pb[5][:, 0:512], ones[0:1, 0:128],
                      drow[0:1, 512:1024], start=True, stop=True,
                      rd=["drow", "ones"], wr=[PB[5]], inc="sT")
                em.op("scalar", "copy", bcst[:, 0:512], pb[5][:, 0:512],
                      rd=[PB[5]], wr=["bcst"], inc="sA")
                em.op("vector", "tensor_tensor", obuf[:, 0:512],
                      pb[2][:, 0:512], bcst[:, 0:512], AL.mult,
                      rd=[PB[2], "bcst"], wr=["obuf"], inc="sV")
        em.op("tensor", "matmul", pb[6][0:1, 0:512], ones[:, 0:1],
              tac[:, 512:1024], start=True, stop=True,
              rd=["T1", "ones"], wr=[PB[6]], inc="sT")
        em.op("vector", "tensor_single_scalar", drow[0:1, 0:512],
              pb[6][0:1, 0:512], 1e-6, AL.max, rd=[PB[6]], wr=["drow"],
              inc="sV")
        em.op("vector", "reciprocal", drow[0:1, 512:1024],
              drow[0:1, 0:512], wr=["drow"], inc="sV")
        em.op("tensor", "matmul", pb[6][:, 0:512], ones[0:1, 0:128],
              drow[0:1, 512:1024], start=True, stop=True,
              rd=["drow", "ones"], wr=[PB[6]], inc="sT")
        em.op("scalar", "copy", bcst[:, 0:512], pb[6][:, 0:512],
              rd=[PB[6]], wr=["bcst"], inc="sA")
        em.op("vector", "tensor_tensor", obuf[:, 512:1024],
              pb[3][:, 0:512], bcst[:, 0:512], AL.mult,
              rd=[PB[3], "bcst"], wr=["obuf"], inc="sV")
        em.op("gpsimd", "dma_start",
              out=ap2(dOS, h * 128 * NQ, NQ, 128, 1024), in_=obuf[:, 0:1024],
              rd=["obuf"], inc="dOS", n=16)

    sc1b.close()

    for g in range(NH // 2):
        if g > 0:
            emit_wslice_load(g)
        emit_colsums(g)
        emit_group_proj(g)
        for hh in range(2):
            emit_attention_head(g, hh)

    em.barrier(ESEM)
    sc1.close()

    kintT = sb("kintT", [128, NT * 1024], BF16)
    recs = sb("recs", [128, QT])
    rowstage = sb("rowstage", [8, 128])
    recrow = sb("recrow", [1, 1024])
    recb = sb("recb", [128, 1024])

    sc2b = contextlib.ExitStack()
    sc2a = contextlib.ExitStack()
    x2f_ = [sb("x2f%d" % p, [128, 2048], scope=sc2b) for p in range(2)]
    lnb_ = [sb("lnb%d" % p, [128, 2048], scope=sc2b) for p in range(2)]
    qb2_ = [sb("qb2%d" % p, [128, 2048], scope=sc2b) for p in range(2)]
    qb3_ = [sb("qb3%d" % p, [128, 2048], scope=sc2b) for p in range(2)]
    kib_ = [sb("kib%d" % p, [128, 2048], BF16, scope=sc2b) for p in range(2)]
    col_ = [sb("colq%d" % p, [128, 8], scope=sc2b) for p in range(2)]
    woT_sb = sb("woT_sb", [128, 16 * 1024], scope=sc2a)
    oLj = sb("oLj", [128, 2 * 2048], scope=sc2a)
    xrt = sb("xrt", [128, 1024], scope=sc2a)
    x2t = sb("x2t", [128, 1024], scope=sc2a)

    def quant_gen(jt):
        qp = jt % 2
        x2f, lnb, qb2, qb3, kib, col = (x2f_[qp], lnb_[qp], qb2_[qp],
                                        qb3_[qp], kib_[qp], col_[qp])
        sx2f, slnb, sqb2, sqb3, skib = ("x2f%d" % qp, "lnb%d" % qp,
                                        "qb2%d" % qp, "qb3%d" % qp,
                                        "kib%d" % qp)
        scol, scol2 = "colq%d" % qp, "colq2%d" % qp
        if qp == 0:
            em.op("sync", "dma_start", out=x2f[:, 0:2048],
                  in_=ap2(dX2, jt * 128 * D, D, 128, 2048),
                  rd=["x2s%d_0" % jt, "x2s%d_1" % jt], wr=[sx2f],
                  inc="dX2f", n=16)
        else:
            em.op("sync", "dma_start", out=x2f[:, 0:1024],
                  in_=ap2(dX2, jt * 128 * D, D, 128, 1024),
                  rd=["x2s%d_0" % jt], wr=[sx2f], inc="dX2fB", n=16)
            em.op("scalar", "copy", x2f[:, 1024:2048], x2t[:, 0:1024],
                  rd=["x2t"], wr=[sx2f], inc="sA")
        yield
        em.op("vector", "reduce_sum", col[:, 0:1], x2f[:, :], AX.X,
              rd=[sx2f], wr=[scol], inc="sV")
        yield
        em.op("scalar", "activation", qb2[:, :], x2f[:, :], AF.Square,
              rd=[sx2f], wr=[sqb2], inc="sA")
        yield
        em.op("vector", "reduce_sum", col[:, 1:2], qb2[:, :], AX.X,
              rd=[sqb2], wr=[scol], inc="sV")
        yield
        em.op("vector", "tensor_scalar", col[:, 0:1], col[:, 0:1],
              1.0 / D, None, AL.mult, wr=[scol], inc="sV")
        yield
        em.op("vector", "tensor_scalar", col[:, 1:2], col[:, 1:2],
              1.0 / D, None, AL.mult, wr=[scol], inc="sV")
        yield
        em.op("vector", "tensor_tensor", col[:, 2:3], col[:, 0:1],
              col[:, 0:1], AL.mult, wr=[scol], inc="sV")
        yield
        em.op("vector", "tensor_tensor", col[:, 1:2], col[:, 1:2],
              col[:, 2:3], AL.subtract, wr=[scol], inc="sV")
        yield
        em.op("vector", "tensor_scalar", col[:, 2:3], col[:, 1:2],
              1e-5, None, AL.add, wr=[scol], inc="sV")
        yield
        em.op("vector", "reciprocal", col[:, 3:4], col[:, 2:3],
              wr=[scol], inc="sV")
        yield
        em.op("vector", "tensor_tensor", col[:, 4:5], col[:, 1:2],
              col[:, 3:4], AL.mult, wr=[scol], inc="sV")
        yield
        em.op("vector", "tensor_scalar", col[:, 4:5], col[:, 4:5],
              1e-8, None, AL.add, wr=[scol], inc="sV")
        yield
        em.op("vector", "tensor_tensor", col[:, 4:5], col[:, 4:5],
              col[:, 2:3], AL.mult, wr=[scol], inc="sV")
        yield
        em.op("scalar", "activation", col[:, 5:6], col[:, 4:5], AF.Sqrt,
              rd=[scol], wr=[scol2], inc="sA")
        yield
        em.op("vector", "reciprocal", col[:, 3:4], col[:, 5:6],
              rd=[scol2], wr=[scol], inc="sV")
        yield
        for _ in range(2):
            em.op("vector", "tensor_tensor", col[:, 6:7], col[:, 3:4],
                  col[:, 3:4], AL.mult, wr=[scol], inc="sV")
            yield
            em.op("vector", "tensor_tensor", col[:, 6:7], col[:, 6:7],
                  col[:, 4:5], AL.mult, wr=[scol], inc="sV")
            yield
            em.op("vector", "tensor_scalar", col[:, 6:7], col[:, 6:7],
                  -0.5, 1.5, AL.mult, AL.add, wr=[scol], inc="sV")
            yield
            em.op("vector", "tensor_tensor", col[:, 3:4], col[:, 3:4],
                  col[:, 6:7], AL.mult, wr=[scol], inc="sV")
            yield
        be, bs = "vector", "sV"
        em.op(be, "tensor_scalar", lnb[:, :], x2f[:, :],
              col[:, 0:1], col[:, 3:4], AL.subtract, AL.mult,
              rd=[sx2f, scol], wr=[slnb], inc=bs)
        yield
        em.op("vector", "reduce_max", col[:, 5:6], lnb[:, :], AX.X,
              apply_absolute_value=True, rd=[slnb], wr=[scol], inc="sV")
        yield
        em.op("vector", "reciprocal", col[:, 6:7], col[:, 5:6],
              wr=[scol], inc="sV")
        yield
        em.op("vector", "tensor_scalar", col[:, 6:7], col[:, 6:7],
              127.0, None, AL.mult, wr=[scol], inc="sV")
        yield
        em.op("vector", "reciprocal", recs[:, jt:jt + 1], col[:, 6:7],
              rd=[scol], wr=["recs"], inc="sV")
        yield
        em.op("scalar", "activation", qb2[:, :], lnb[:, :], AF.Abs,
              scale=col[:, 6:7], rd=[slnb, scol], wr=[sqb2], inc="sA")
        yield
        em.op("scalar", "activation", qb3[:, :], qb2[:, :], AF.Copy,
              scale=-1.0, bias=63.5,
              rd=[sqb2, "constv"], wr=[sqb3], inc="sA")
        yield
        for m in (64.0, 32.0, 16.0, 8.0, 4.0, 2.0, 1.0):
            em.op("scalar", "activation", x2f[:, :], qb3[:, :], AF.Sign,
                  rd=[sqb3], wr=[sx2f], inc="sA")
            yield
            em.op(be, "scalar_tensor_tensor", qb3[:, :], x2f[:, :],
                  -m / 2.0, qb3[:, :], AL.mult, AL.add,
                  rd=[sx2f], wr=[sqb3], inc=bs)
            yield
        em.op(be, "scalar_tensor_tensor", qb2[:, :], qb3[:, :],
              1.0, qb2[:, :], AL.mult, AL.add, rd=[sqb3], wr=[sqb2], inc=bs)
        yield
        em.op("scalar", "activation", qb3[:, :], lnb[:, :], AF.Sign,
              rd=[slnb], wr=[sqb3], inc="sA")
        yield
        em.op(be, "tensor_tensor", kib[:, :], qb2[:, :], qb3[:, :],
              AL.mult, rd=[sqb2, sqb3], wr=[skib], inc=bs)
        yield


    def emit_quant_pair(j0, j1):
        gens = [quant_gen(j0), quant_gen(j1)]
        while gens:
            nxt = []
            for g in gens:
                try:
                    next(g)
                    nxt.append(g)
                except StopIteration:
                    pass
            gens = nxt

    def emit_quant_tr(jt):
        qp = jt % 2
        kib, skib = kib_[qp], "kib%d" % qp
        for grp in range(0, 16, 4):
            em.pre("tensor", rd=[skib, "identB"], wr=[PB[2]])
            for j in range(4):
                dcx = grp + j
                ev = em._emit_op("tensor", "transpose",
                                 (pb[2][:, j * 64:j * 64 + 64].bitcast(BF16),
                                  kib[:, dcx * 128:dcx * 128 + 128], identB[:]),
                                 {}, "sT" if j == 3 else None, 1)
            em.post(ev, rd=[skib, "identB"], wr=[PB[2]])
            em.op("scalar", "copy",
                  bass.AP(kintT, grp * 1024 + jt * 128,
                          [[NT * 1024, 128], [1024, 4], [1, 128]]),
                  bass.AP(pb[2].bitcast(BF16), 0,
                          [[1024, 128], [128, 4], [1, 128]]),
                  rd=[PB[2]], wr=["kintT"], inc="sA")

    qpend = []

    def qpump(n):
        cnt = 0
        idx = 0
        while cnt < n and qpend:
            if idx >= len(qpend):
                idx = 0
            try:
                next(qpend[idx][1])
                idx += 1
            except StopIteration:
                qpend.pop(idx)
            cnt += 1

    def qflush_jt(j):
        while any(t <= j for t, _ in qpend):
            qpump(1)

    for dh in range(2):
        for p in range(4):
            srcp = bass.AP(dWO, dh * 1024 + p * 4 * 128 * D,
                           [[D, 128], [128 * D, 4], [1, 1024]])
            dstp = bass.AP(woT_sb, p * 4 * 1024,
                           [[16 * 1024, 128], [1024, 4], [1, 1024]])
            em.op("sync", "dma_start", out=dstp, in_=srcp, wr=["woT%d" % p],
                  inc=("dWo", "dWoB", "dWoC", "dWoD")[p], n=16)
        for jt in range(QT):
            par = jt % 2
            ol = oLj[:, par * 2048:par * 2048 + 2048]
            oln = "oLj%d" % par
            em.op("sync", "dma_start", out=ol,
                  in_=bass.AP(dOS, jt * 128, [[NQ, 128], [128 * NQ, 16], [1, 128]]),
                  wr=[oln], inc=("dO0" if par == 0 else "dO1"), n=16)
            em.op("sync", "dma_start", out=xrt[:, 0:1024],
                  in_=ap2(dXR, jt * 128 * D + dh * 1024, D, 128, 1024),
                  wr=["xrt"], inc="dXr", n=16)
            qpump(8)
            for dc in range(2):
                bank, bkn = pb[dc], PB[dc]
                for p in range(4):
                    em.pre("tensor", rd=[oln, "woT%d" % p],
                           wr=[bkn] if p == 0 else [])
                    for hq in range(4):
                        hx = p * 4 + hq
                        ev = em._emit_op(
                            "tensor", "matmul",
                            (bank[:, 0:512], ol[:, hx * 128:hx * 128 + 128],
                             woT_sb[:, hx * 1024 + dc * 512:
                                    hx * 1024 + dc * 512 + 512]),
                            dict(start=(hx == 0), stop=False),
                            "sT" if hq == 3 else None, 1)
                    em.post(ev, rd=[oln, "woT%d" % p], wr=[bkn])
                em.pre("tensor", rd=["xrt", "ident"], wr=[])
                ev = em._emit_op("tensor", "matmul",
                                 (bank[:, 0:512], ident[:],
                                  xrt[:, dc * 512:dc * 512 + 512]),
                                 dict(start=False, stop=True), "sT", 1)
                em.post(ev, rd=["xrt", "ident"], wr=[bkn])
                em.op("scalar", "copy", x2t[:, dc * 512:dc * 512 + 512],
                      bank[:, 0:512], rd=[bkn], wr=["x2t"], inc="sA")
                qpump(4)
            em.op("gpsimd", "dma_start",
                  out=ap2(dX2, jt * 128 * D + dh * 1024, D, 128, 1024),
                  in_=x2t[:, 0:1024], rd=["x2t"],
                  wr=["x2s%d_%d" % (jt, dh)], inc="dX2s", n=16)
            if dh == 1:
                if jt % 2 == 1:
                    if jt >= 3:
                        qflush_jt(jt - 2)
                        emit_quant_tr(jt - 3)
                        emit_quant_tr(jt - 2)
                    qpend.append((jt - 1, quant_gen(jt - 1)))
                    qpend.append((jt, quant_gen(jt)))
                qpump(16)
    hsnap = dict(em.cnt)
    qflush_jt(QT - 1)
    emit_quant_tr(QT - 2)
    emit_quant_tr(QT - 1)

    em.op("tensor", "transpose", pb[2][0:8, 0:128], recs[:, :], ident[:],
          rd=["recs", "ident"], wr=[PB[2]], inc="sT")
    em.op("vector", "tensor_copy", rowstage[:, :], pb[2][0:8, 0:128],
          rd=[PB[2]], wr=["rowstage"], inc="sV")
    em.op("gpsimd", "dma_start",
          out=bass.AP(recrow, 0, [[1024, 1], [1, 1024]]),
          in_=bass.AP(rowstage, 0, [[128, 8], [1, 128]]),
          rd=["rowstage"], wr=["recrow"], inc="dNr", n=16)
    em.op("tensor", "matmul", pb[2][:, 0:512], ones[0:1, 0:128],
          recrow[0:1, 0:512], start=True, stop=True,
          rd=["recrow", "ones"], wr=[PB[2]], inc="sT")
    em.op("scalar", "copy", recb[:, 0:512], pb[2][:, 0:512],
          rd=[PB[2]], wr=["recb"], inc="sA")
    em.op("tensor", "matmul", pb[2][:, 0:512], ones[0:1, 0:128],
          recrow[0:1, 512:1024], start=True, stop=True,
          rd=["recrow", "ones"], wr=[PB[2]], inc="sT")
    em.op("scalar", "copy", recb[:, 512:1024], pb[2][:, 0:512],
          rd=[PB[2]], wr=["recb"], inc="sA")

    def soft_barrier(sems, snap):
        for eng in ENGINES:
            for s in sems:
                em.wait(eng, s, snap[s])

    soft_barrier(("sT", "sA", "sG", "dWo", "dWoB", "dWoC", "dWoD",
                  "dO0", "dO1", "dXr", "dX2s"), hsnap)
    sc2a.close()

    sc4 = contextlib.ExitStack()
    wstg = [sb("wstg%d" % p, [128, 2048], scope=sc4) for p in range(2)]
    sgw = [sb("sgw%d" % p, [128, NT * 256], BF16, scope=sc4) for p in range(2)]
    suw = [sb("suw%d" % p, [128, NT * 256], BF16, scope=sc4) for p in range(2)]
    sdw = [sb("sdw%d" % p, [128, 2 * 2048], BF16, scope=sc4) for p in range(2)]
    gbuf = sb("gbuf", [128, 2048], scope=sc4)
    ubuf = sb("ubuf", [128, 2048], scope=sc4)
    pTl = [sb("pTl%d" % p, [128, 2 * 1024], BF16, scope=sc4) for p in range(2)]

    witems = []

    def enq_gu(ch):
        p = ch % 2
        for (dram, dstbuf, dn) in ((dWG, sgw[p], "sgw%d" % p),
                                   (dWU, suw[p], "suw%d" % p)):
            for half in range(2):
                witems.append((
                    bass.AP(dram, ch * 256 + half * 8 * 128 * HF,
                            [[HF, 128], [128 * HF, 8], [1, 256]]),
                    dstbuf, half * 2048, dn))

    def enq_d(ch):
        p = ch % 2
        for hs in range(2):
            witems.append((
                ap2(dWD, (ch * 256 + hs * 128) * D, D, 128, 2048),
                sdw[p], hs * 2048, "sdw%d" % p))

    wst = {"loaded": 0, "signed": 0}

    def wpump(n, flush=False):
        while True:
            did = False
            if n > 0 or flush:
                if wst["loaded"] < len(witems) and                         wst["loaded"] - wst["signed"] < 2:
                    i = wst["loaded"]
                    srcap = witems[i][0]
                    sl = i % 2
                    em.op("sync", "dma_start", out=wstg[sl][:, :], in_=srcap,
                          wr=["wstg%d" % sl], inc="dWg%d" % sl, n=16)
                    wst["loaded"] += 1
                    did = True
                lead = 0 if wst["loaded"] == len(witems) else 1
                if wst["signed"] < wst["loaded"] - (0 if flush else lead):
                    i = wst["signed"]
                    _, dstbuf, doff, dn = witems[i]
                    sl = i % 2
                    em.op("scalar", "activation",
                          dstbuf[:, doff:doff + 2048], wstg[sl][:, :],
                          AF.Sign, rd=["wstg%d" % sl], wr=[dn], inc="sA")
                    wst["signed"] += 1
                    did = True
            if not did:
                return
            n -= 1
            if n <= 0 and not flush:
                return
            if flush and wst["signed"] >= len(witems):
                return

    enq_gu(0)
    enq_d(0)
    enq_d(1)
    wpump(0, flush=True)

    y2t = [(x2f_[0], "x2f0"), (lnb_[0], "lnb0"), (qb2_[0], "qb20"),
           (qb3_[0], "qb30"), (x2f_[1], "x2f1"), (lnb_[1], "lnb1"),
           (qb2_[1], "qb21"), (qb3_[1], "qb31")]
    pTh = [kib_[0], kib_[1]]
    PTHN = ["kib0", "kib1"]

    for ch in range(NCH):
        p = ch % 2
        gn, un, dn = "sgw%d" % p, "suw%d" % p, "sdw%d" % p
        if ch + 1 < NCH:
            enq_gu(ch + 1)
        if ch % 2 == 0 and ch >= 2:
            enq_d(ch)
            enq_d(ch + 1)
        for tokh in range(2):
            for hh in range(2):
                bg, bgn = pb[hh], PB[hh]
                bu, bun = pb[2 + hh], PB[2 + hh]
                em.pre("tensor", rd=["kintT", gn, un], wr=[bgn, bun])
                for dcx in range(16):
                    em._emit_op(
                        "tensor", "matmul",
                        (bg[:, 0:512],
                         sgw[p][:, dcx * 256 + hh * 128:dcx * 256 + hh * 128 + 128],
                         kintT[:, dcx * 1024 + tokh * 512:dcx * 1024 + tokh * 512 + 512]),
                        dict(start=(dcx == 0), stop=(dcx == 15)), None, 1)
                    ev = em._emit_op(
                        "tensor", "matmul",
                        (bu[:, 0:512],
                         suw[p][:, dcx * 256 + hh * 128:dcx * 256 + hh * 128 + 128],
                         kintT[:, dcx * 1024 + tokh * 512:dcx * 1024 + tokh * 512 + 512]),
                        dict(start=(dcx == 0), stop=(dcx == 15)),
                        "sT" if dcx == 15 else None, 1)
                em.post(ev, rd=["kintT", gn, un], wr=[bgn, bun])
                pos = slice(hh * 1024 + tokh * 512, hh * 1024 + tokh * 512 + 512)
                em.op("vector", "tensor_tensor", ubuf[:, pos], bu[:, 0:512],
                      recb[:, tokh * 512:tokh * 512 + 512], AL.mult,
                      rd=[bun, "recb"], wr=["ubuf"], inc="sV")
                em.op("vector", "tensor_tensor", gbuf[:, pos], bg[:, 0:512],
                      recb[:, tokh * 512:tokh * 512 + 512], AL.mult,
                      rd=[bgn, "recb"], wr=["gbuf"], inc="sV")
                em.op("scalar", "activation", gbuf[:, pos], gbuf[:, pos],
                      AF.Silu, wr=["gbuf"], inc="sA")
                em.op("vector", "tensor_tensor", gbuf[:, pos], gbuf[:, pos],
                      ubuf[:, pos], AL.mult, rd=["ubuf"], wr=["gbuf"],
                      inc="sV")
                wpump(2)
                hn, ln_ = PTHN[p], "pTl%d" % p
                em.op("vector", "tensor_copy", pTh[p][:, pos], gbuf[:, pos],
                      rd=["gbuf"], wr=[hn], inc="sV")
                em.op("vector", "tensor_tensor", pTl[p][:, pos], gbuf[:, pos],
                      pTh[p][:, pos], AL.subtract, rd=["gbuf", hn], wr=[ln_],
                      inc="sV")
        if ch % 2 == 1:
            pair = ch // 2
            for jt in range(QT):
                for dc in range(4):
                    bi = 4 + ((jt * 4 + dc) % 4)
                    bank, bkn = pb[bi], PB[bi]
                    first = True
                    for cp in range(2):
                        for hs in range(2):
                            for pt, ptn in ((pTh[cp], PTHN[cp]),
                                            (pTl[cp], "pTl%d" % cp)):
                                last = (cp == 1 and hs == 1 and
                                        ptn.startswith("pTl"))
                                em.pre("tensor", rd=[ptn, "sdw%d" % cp],
                                       wr=[bkn] if first else [])
                                ev = em._emit_op(
                                    "tensor", "matmul",
                                    (bank[:, 0:512],
                                     pt[:, hs * 1024 + jt * 128:hs * 1024 + jt * 128 + 128],
                                     sdw[cp][:, hs * 2048 + dc * 512:hs * 2048 + dc * 512 + 512]),
                                    dict(start=first, stop=last),
                                    "sT" if last else None, 1)
                                if last:
                                    em.post(ev, rd=[PTHN[0], "pTl0", PTHN[1],
                                                    "pTl1", "sdw0", "sdw1"],
                                            wr=[bkn])
                                first = False
                    if dc == 0:
                        wpump(1)
                    y2b, y2n = y2t[jt]
                    dst = y2b[:, dc * 512:dc * 512 + 512]
                    if pair == 0:
                        em.op("vector", "tensor_copy", dst, bank[:, 0:512],
                              rd=[bkn], wr=[y2n], inc="sV")
                    else:
                        em.op("vector", "tensor_tensor", dst, dst,
                              bank[:, 0:512], AL.add, rd=[bkn], wr=[y2n],
                              inc="sV")
    for jt in range(QT):
        em.op("sync", "dma_start", out=ubuf[:, :],
              in_=ap2(dX2, jt * 128 * D, D, 128, 2048),
              rd=["x2s%d_0" % jt, "x2s%d_1" % jt],
              wr=["ubuf"], inc="dX2r", n=16)
        em.op("vector", "tensor_tensor", ubuf[:, :],
              y2t[jt][0][:, 0:2048], ubuf[:, :], AL.add,
              rd=[y2t[jt][1]], wr=["ubuf"], inc="sV")
        em.op("gpsimd", "dma_start", out=ap2(dY, jt * 128 * D, D, 128, 2048),
              in_=ubuf[:, :], rd=["ubuf"], inc="dY", n=16)
    sc4.close()

    em.wait("gpsimd", "dY", C["dY"])
    em.wait("gpsimd", "dOS", C["dOS"])
    em.wait("gpsimd", "dX2s", C["dX2s"])

    def _replay(eng_name):
        def run(e):
            for i, fn in enumerate(em.q[eng_name]):
                try:
                    fn(e)
                except Exception:
                    import traceback
                    print("FAILED at", eng_name, "op", i)
                    traceback.print_exc()
                    raise
        return run

    with nc.Block() as block:
        block.sync(_replay("sync"))
        block.tensor(_replay("tensor"))
        block.vector(_replay("vector"))
        block.scalar(_replay("scalar"))
        block.gpsimd(_replay("gpsimd"))

    stack.close()
    return nc


_NC_CACHE = None


def _get_nc():
    global _NC_CACHE
    if _NC_CACHE is None:
        _NC_CACHE = build()
    return _NC_CACHE


def kernel(L0_attn_x, L0_attn_Wq, L0_attn_Wk, L0_attn_Wv, L0_attn_Wo,
           L0_ffn_W_g, L0_ffn_W_u, L0_ffn_W_d):
    x = np.ascontiguousarray(np.asarray(L0_attn_x, dtype=np.float32))

    def _ulperm(wT):
        P = np.empty_like(wT)
        for g in range(8):
            b = g * 256
            P[:, b:b + 64] = wT[:, b:b + 64]
            P[:, b + 64:b + 128] = wT[:, b + 128:b + 192]
            P[:, b + 128:b + 192] = wT[:, b + 64:b + 128]
            P[:, b + 192:b + 256] = wT[:, b + 192:b + 256]
        return np.ascontiguousarray(P)

    wqT = _ulperm(np.asarray(L0_attn_Wq, dtype=np.float32).T)
    wkT = _ulperm(np.asarray(L0_attn_Wk, dtype=np.float32).T)
    wvT = np.ascontiguousarray(np.asarray(L0_attn_Wv, dtype=np.float32).T)
    woT = np.ascontiguousarray(np.asarray(L0_attn_Wo, dtype=np.float32).T)
    wgT = np.ascontiguousarray(np.asarray(L0_ffn_W_g, dtype=np.float32).T)
    wuT = np.ascontiguousarray(np.asarray(L0_ffn_W_u, dtype=np.float32).T)
    wdT = np.ascontiguousarray(np.asarray(L0_ffn_W_d, dtype=np.float32).T)

    _Q, _h = 2097152, 1048576

    def _dyn(qe0, qe1):
        return np.array([[qe0, qe1, qe0 + _h, qe0 + _Q, qe0 + _Q + _h,
                          qe1 + _h, qe1 + _Q, qe1 + _Q + _h]], dtype=np.int32)
    dyn_half = {0: _dyn(0, 1536), 1: _dyn(512, 1024)}
    sel_half = {
        0: np.array([[1.0, 0.0]], dtype=np.float32),
        1: np.array([[0.0, 1.0]], dtype=np.float32),
    }
    qtiles_half = {
        0: [0, 1, 2, 3, 12, 13, 14, 15],
        1: [4, 5, 6, 7, 8, 9, 10, 11],
    }
    perm_half = {}
    for hlf, qt in qtiles_half.items():
        P = np.zeros((NT, QT), dtype=np.float32)
        for j, gt in enumerate(qt):
            P[gt, j] = 1.0
        perm_half[hlf] = P
    toksel = {h: np.concatenate([np.arange(t * 128, t * 128 + 128)
                                 for t in qtiles_half[h]]) for h in (0, 1)}

    in_maps = []
    for c in range(8):
        b = c // 2
        hlf = c % 2
        xb = x[b]
        in_maps.append({
            "xT": np.ascontiguousarray(xb.T),
            "xqT": np.ascontiguousarray(xb.T[:, toksel[hlf]]),
            "xr": np.ascontiguousarray(xb[toksel[hlf]]),
            "wqT": wqT, "wkT": wkT, "wvT": wvT, "woT": woT,
            "wgT": wgT, "wuT": wuT, "wdT": wdT,
            "dynv_d": dyn_half[hlf], "selv": sel_half[hlf],
            "perm": perm_half[hlf],
        })

    nc = _get_nc()
    res = run_bass_kernel_spmd(nc, in_maps, core_ids=list(range(8)))

    out = np.empty((4, NTOK, D), dtype=np.float32)
    for c in range(8):
        b = c // 2
        hlf = c % 2
        out[b][toksel[hlf]] = res.results[c]["y"]
    return out


if __name__ == "__main__":
    nco = build()
    print("built ok")
